# version 02
# baseline (speedup 1.0000x reference)
import numpy as np
import ml_dtypes
from contextlib import ExitStack

import concourse.bass as bass
import concourse.mybir as mybir
from concourse.bass_utils import run_bass_kernel_spmd

F32 = mybir.dt.float32
BF16 = mybir.dt.bfloat16
AF = mybir.ActivationFunctionType
ALU = mybir.AluOpType

NCORES = 8
B = 2
SEQ = 8192
DM = 4096
DSSM = 8192
NST = 16
RDT = 256
TOK = SEQ // NCORES
HALO = 3
TCOLS = TOK + HALO
EPS_LN = 1e-5
EPS_RMS = 1e-6
BF = ml_dtypes.bfloat16


class Prog:
    CENG = ("v", "s", "t", "g", "q")

    def __init__(self, nc):
        self.nc = nc
        self.queues = {e: [] for e in self.CENG}
        self.cnt = {e: 0 for e in self.CENG}
        self.dcnt = {e: 0 for e in self.CENG}
        self.waited = {e: {} for e in self.CENG}

    def _waits(self, e, deps):
        for tok in deps:
            if tok is None:
                continue
            key, val = tok
            if key == e:
                pass
            w = self.waited[e]
            if w.get(key, -1) < val:
                w[key] = val
                self.queues[e].append(("wait", key, val))

    def op(self, e, fn, deps=()):
        self._waits(e, deps)
        self.cnt[e] += 1
        self.queues[e].append(("op", fn))
        return (e, self.cnt[e])

    def dma(self, e, out, in_, deps=()):
        if self.dcnt[e] > 0:
            self._waits(e, [("d" + e, self.dcnt[e])])
        self._waits(e, deps)
        self.dcnt[e] += 16
        self.queues[e].append(("dma", out, in_))
        return ("d" + e, self.dcnt[e])

    def dma_t(self, e, out, in_, deps=()):
        if self.dcnt[e] > 0:
            self._waits(e, [("d" + e, self.dcnt[e])])
        self._waits(e, deps)
        self.dcnt[e] += 16
        self.queues[e].append(("dmat", out, in_))
        return ("d" + e, self.dcnt[e])

    def barrier_tok(self, e):
        return (e, self.cnt[e])

    def all_toks(self):
        toks = [(e, self.cnt[e]) for e in self.CENG if self.cnt[e] > 0]
        toks += [("d" + e, self.dcnt[e]) for e in self.CENG if self.dcnt[e] > 0]
        return toks

    def replay(self, block, sems):
        def body_for(e):
            def body(eng):
                for item in self.queues[e]:
                    kind = item[0]
                    if kind == "wait":
                        eng.wait_ge(sems[item[1]], item[2])
                    elif kind == "op":
                        instr = item[1](eng)
                        instr.then_inc(sems[e], 1)
                    elif kind == "dma":
                        eng.dma_start(out=item[1], in_=item[2]).then_inc(
                            sems["d" + e], 16)
                    elif kind == "dmat":
                        eng.dma_start_transpose(item[1], item[2]).then_inc(
                            sems["d" + e], 16)
            return body
        block.vector(body_for("v"))
        block.scalar(body_for("s"))
        block.tensor(body_for("t"))
        block.gpsimd(body_for("g"))
        block.sync(body_for("q"))


def psum_bank(stack, nc, name, dtype=F32):
    n = 512 if dtype == F32 else 1024
    return stack.enter_context(nc.psum_tensor(name, [128, n], dtype))


class Buf:
    __slots__ = ("w", "r")

    def __init__(self):
        self.w = None
        self.r = []


def emit(p, eng, fn, reads=(), writes=(), extra=()):
    deps = [t for t in extra if t is not None]
    for b in reads:
        if b.w is not None:
            deps.append(b.w)
    for b in writes:
        deps.extend(b.r)
        if b.w is not None:
            deps.append(b.w)
    tok = p.op(eng, fn, deps=deps)
    for b in reads:
        b.r.append(tok)
    for b in writes:
        b.w = tok
        b.r = []
    return tok


def emit_dma(p, q, out_ap, in_ap, reads=(), writes=(), extra=()):
    deps = [t for t in extra if t is not None]
    for b in reads:
        if b.w is not None:
            deps.append(b.w)
    for b in writes:
        deps.extend(b.r)
        if b.w is not None:
            deps.append(b.w)
    tok = p.dma(q, out_ap, in_ap, deps=deps)
    for b in reads:
        b.r.append(tok)
    for b in writes:
        b.w = tok
        b.r = []
    return tok


def make_sems(stack, nc):
    sems = {}
    for e in Prog.CENG:
        sems[e] = stack.enter_context(nc.semaphore("sem_" + e))
        sems["d" + e] = stack.enter_context(nc.semaphore("sem_d" + e))
    return sems


def run_prog(nc, prog, in_maps, core_ids=None, **kw):
    raise NotImplementedError



IDENT_BF = np.eye(128, dtype=BF)
IDENT_F32 = np.eye(128, dtype=np.float32)
ONES_COL = np.ones((128, 1), dtype=BF)



def emit_transpose_bf16(p, sb, src_rows, n_rows, n_cols, ident_bf, out_cb,
                        dep_extra=()):
    raise NotImplementedError



def _ceil(a, b):
    return -(-a // b)


def build_prep():
    nc = bass.Bass("TRN2", target_bir_lowering=False)
    ins = {
        "wbase_s": (512, 4096), "wl1_s": (512, 4096), "win_s": (2048, 4096),
        "wout_s": (512, 8192), "wdt_s": (1024, 256), "wx_full": (288, 8192),
        "a0_full": (128, 4096), "wg_full": (8, 4096), "lim_full": (8, 32),
    }
    iaps = {k: nc.dram_tensor(k, list(v), F32, kind="ExternalInput").ap()
            for k, v in ins.items()}
    iaps["b0_full"] = nc.dram_tensor("b0_full", [8, 4096, 16], F32,
                                     kind="ExternalInput").ap()
    iaps["identb"] = nc.dram_tensor("identb", [128, 128], BF16,
                                    kind="ExternalInput").ap()
    outs = {
        "tbase": (4, 32, 128, 128), "tl1": (4, 32, 128, 128),
        "twin": (16, 32, 128, 128), "twout": (4, 64, 128, 128),
        "twdt": (8, 2, 128, 128), "twx": (3, 64, 128, 128),
        "ta0": (1, 32, 128, 128), "tb0": (32, 1, 128, 128),
        "twg": (4096, 8), "tlim": (32, 8),
    }
    oaps = {k: nc.dram_tensor(k, list(v), BF16, kind="ExternalOutput").ap()
            for k, v in outs.items()}

    with ExitStack() as stack:
        E = stack.enter_context
        sems = make_sems(stack, nc)
        p = Prog(nc)
        ident = E(nc.sbuf_tensor("ident", [128, 128], BF16))
        tok_id = p.dma("q", ident[:, :], iaps["identb"])

        stage = E(nc.sbuf_tensor("stage", [128, 16 * 4096], BF16))
        otile = [E(nc.sbuf_tensor(f"otile{i}", [128, 4096], BF16))
                 for i in range(2)]
        zpad = E(nc.sbuf_tensor("zpad", [128, 96], BF16))
        tok_z = p.op("v", lambda v: v.memset(zpad[:, :], 0.0))
        ps_tr = [psum_bank(stack, nc, f"ps_tr{i}", BF16) for i in range(4)]

        state = {"ps_free": [None] * 4, "ob_free": [None, None], "ps_i": 0}

        def transpose_job(src, R, C, dst, dst_is_tiled=False, row0=0,
                          pad_to=None):
            n_rt = _ceil(R, 128)
            n_ct = _ceil(C, 128)
            assert n_rt * C <= 16 * 4096 and R <= 4096
            stage_free = [p.barrier_tok("t")]
            row_aps = []
            ld_toks = []
            for rt in range(n_rt):
                rp = min(128, R - rt * 128)
                ap = stage[:, rt * C:(rt + 1) * C]
                ld_toks.append(p.dma("g", ap[0:rp, :],
                                     src[rt * 128:rt * 128 + rp, :],
                                     deps=stage_free))
                row_aps.append((ap, rp))
            for ct in range(n_ct):
                cp = min(128, C - ct * 128)
                ob = otile[ct % 2]
                evac_toks = []
                for rt in range(n_rt):
                    ap, rp = row_aps[rt]
                    pi = state["ps_i"] % 4
                    ps = ps_tr[pi]
                    def mk_tt(ps=ps, ap=ap, rp=rp, cp=cp, ct=ct):
                        def f(t):
                            return t.transpose(ps[0:cp, 0:rp],
                                               ap[0:rp, ct * 128:ct * 128 + cp],
                                               ident[0:rp, 0:rp])
                        return f
                    tt = p.op("t", mk_tt(),
                              deps=[ld_toks[rt], tok_id, state["ps_free"][pi]])
                    def mk_ev(ps=ps, ob=ob, rt=rt, rp=rp, cp=cp):
                        def f(s):
                            return s.copy(ob[0:cp, rt * 128:rt * 128 + rp],
                                          ps[0:cp, 0:rp])
                        return f
                    ev = p.op("s", mk_ev(),
                              deps=[tt, state["ob_free"][ct % 2]])
                    state["ps_free"][pi] = ev
                    evac_toks.append(ev)
                    state["ps_i"] += 1
                R_st = R
                if pad_to is not None:
                    mz = p.op("v", (lambda ob=ob, cp=cp: lambda v: v.memset(
                        ob[0:cp, R:pad_to], 0.0))(),
                        deps=[state["ob_free"][ct % 2]])
                    evac_toks.append(mz)
                    R_st = pad_to
                if dst_is_tiled:
                    dap = dst[:, ct, row0:row0 + cp, :].rearrange(
                        "o p c -> p o c")
                    sap = ob[0:cp, 0:R_st].rearrange("p (o c) -> p o c", c=128)
                else:
                    dap = dst[ct * 128:ct * 128 + cp, 0:R_st]
                    sap = ob[0:cp, 0:R_st]
                st = p.dma("q", dap, sap, deps=evac_toks)
                state["ob_free"][ct % 2] = st
        transpose_job(iaps["wbase_s"], 512, 4096, oaps["tbase"], True)
        transpose_job(iaps["wl1_s"], 512, 4096, oaps["tl1"], True)
        transpose_job(iaps["win_s"], 2048, 4096, oaps["twin"], True)
        transpose_job(iaps["wout_s"], 512, 8192, oaps["twout"], True)
        transpose_job(iaps["wdt_s"], 1024, 256, oaps["twdt"], True)
        transpose_job(iaps["wx_full"], 288, 8192, oaps["twx"], True,
                      pad_to=384)
        transpose_job(iaps["a0_full"], 128, 4096, oaps["ta0"], True)
        b0 = iaps["b0_full"]
        for e in range(8):
            transpose_job(b0[e, :, :], 4096, 16, oaps["tb0"], True,
                          row0=16 * e)
        transpose_job(iaps["wg_full"], 8, 4096, oaps["twg"])
        transpose_job(iaps["lim_full"], 8, 32, oaps["tlim"])

        with nc.Block() as block:
            p.replay(block, sems)
    return nc


def host_prep(inputs):
    nc = build_prep()
    in_maps = []
    for k in range(NCORES):
        m = {
            "wbase_s": np.ascontiguousarray(inputs["L0_ffn_W_base"][512 * k:512 * (k + 1)]),
            "wl1_s": np.ascontiguousarray(inputs["L1_ffn_W"][512 * k:512 * (k + 1)]),
            "win_s": np.ascontiguousarray(inputs["L1_attn_W_in"][2048 * k:2048 * (k + 1)]),
            "wout_s": np.ascontiguousarray(inputs["L1_attn_W_out"][512 * k:512 * (k + 1)]),
            "wdt_s": np.ascontiguousarray(inputs["L1_attn_Wdt"][1024 * k:1024 * (k + 1)]),
            "wx_full": np.ascontiguousarray(inputs["L1_attn_Wx"]),
            "a0_full": np.ascontiguousarray(inputs["L0_ffn_A"].reshape(128, 4096)),
            "b0_full": np.ascontiguousarray(inputs["L0_ffn_B"]),
            "wg_full": np.ascontiguousarray(inputs["L0_ffn_W_gate"]),
            "lim_full": np.ascontiguousarray(inputs["L1_ffn_LiMEs"]),
            "identb": IDENT_BF,
        }
        in_maps.append(m)
    res = run_bass_kernel_spmd(nc, in_maps, core_ids=list(range(NCORES)))
    r = res.results
    W = {}
    for k in ["tbase", "tl1", "twin", "twout", "twdt"]:
        W[k] = np.concatenate([r[c][k] for c in range(NCORES)], axis=0)
    for k in ["twx", "ta0", "tb0", "twg", "tlim"]:
        W[k] = r[0][k]
    return W



PAIRS = [(0, 0), (0, 1), (1, 1)]


def build_scores():
    nc = bass.Bass("TRN2", target_bir_lowering=False)
    xs = nc.dram_tensor("xs", [B, TCOLS, DM], F32, kind="ExternalInput").ap()
    identf = nc.dram_tensor("identf", [128, 128], F32, kind="ExternalInput").ap()
    psc = nc.dram_tensor("psc", [128, 96], F32, kind="ExternalOutput").ap()

    with ExitStack() as stack:
        E = stack.enter_context
        sems = make_sems(stack, nc)
        p = Prog(nc)
        idf = E(nc.sbuf_tensor("idf", [128, 128], F32))
        xtm = [E(nc.sbuf_tensor(f"xtm{b}", [128, DM], F32)) for b in range(B)]
        xf = [E(nc.sbuf_tensor(f"xf{b}", [128, DM], F32)) for b in range(B)]
        scr = E(nc.sbuf_tensor("scr", [128, 512], F32))
        part = E(nc.sbuf_tensor("part", [128, 96], F32))
        acc = E(nc.sbuf_tensor("acc", [128, 96], F32))
        pool = [psum_bank(stack, nc, f"pool{i}") for i in range(4)]

        idfB = Buf()
        xtmB = [Buf() for _ in range(B)]
        xfB = [[Buf() for _ in range(32)] for _ in range(B)]
        partB = [Buf() for _ in range(96)]
        accB = [Buf() for _ in range(96)]
        poolB = [Buf() for _ in range(4)]
        emit_dma(p, "q", idf[:, :], identf, writes=[idfB])
        emit(p, "v", lambda v: v.memset(acc[:, :], 0.0),
             writes=accB)
        pi = 0
        for tt in range(8):
            for b in range(B):
                emit_dma(p, "q", xtm[b][:, :],
                         xs[b, HALO + tt * 128:HALO + (tt + 1) * 128, :],
                         writes=[xtmB[b]])
            for b in range(B):
                for lt in range(32):
                    bank = pi % 4
                    tr = emit(p, "t", (lambda b=b, lt=lt, bank=bank:
                                       lambda t: t.transpose(
                                           pool[bank][:, 0:128],
                                           xtm[b][:, lt * 128:(lt + 1) * 128],
                                           idf[:, :]))(),
                              reads=[xtmB[b], idfB], writes=[poolB[bank]])
                    emit(p, "s", (lambda b=b, lt=lt, bank=bank: lambda s:
                                  s.copy(xf[b][:, lt * 128:(lt + 1) * 128],
                                         pool[bank][:, 0:128]))(),
                         reads=[poolB[bank]], writes=[xfB[b][lt]])
                    pi += 1
            scrB = [Buf() for _ in range(4)]
            for pr, (a, b2) in enumerate(PAIRS):
                for lt in range(32):
                    col = lt * 3 + pr
                    si = col % 4
                    emit(p, "v", (lambda a=a, b2=b2, lt=lt, si=si:
                                  lambda v: v.tensor_tensor(
                                      scr[:, si * 128:(si + 1) * 128],
                                      xf[a][:, lt * 128:(lt + 1) * 128],
                                      xf[b2][:, lt * 128:(lt + 1) * 128],
                                      op=ALU.mult))(),
                         reads=[xfB[a][lt], xfB[b2][lt]], writes=[scrB[si]])
                    emit(p, "v", (lambda col=col, si=si: lambda v:
                                  v.tensor_reduce(
                                      part[:, col:col + 1],
                                      scr[:, si * 128:(si + 1) * 128],
                                      mybir.AxisListType.X, ALU.add))(),
                         reads=[scrB[si]], writes=[partB[col]])
                    emit(p, "v", (lambda col=col: lambda v: v.tensor_tensor(
                        acc[:, col:col + 1], acc[:, col:col + 1],
                        part[:, col:col + 1], op=ALU.add))(),
                         reads=[partB[col]], writes=[accB[col]])
        st = p.dma("q", psc, acc[:, :], deps=[b.w for b in accB])
        p._waits("q", [st])
        with nc.Block() as block:
            p.replay(block, sems)
    return nc



def xncol(lt, b, t0=0):
    return lt * (B * TCOLS) + b * TCOLS + t0


CHUNKS = [(0, 512), (512, 512), (1024, 3)]
BLOCKS = [(0, 3)] + [(3 + 128 * i, 128) for i in range(8)]


def build_l0block():
    nc = bass.Bass("TRN2", target_bir_lowering=False)
    ap_in = {}
    for name, shape, dt in [
        ("xs", [B, TCOLS, DM], F32), ("pscall", [NCORES, 128, 96], F32),
        ("tbase4", [32, 32, 128, 128], BF16), ("ta04", [1, 32, 128, 128], BF16),
        ("tb04", [32, 1, 128, 128], BF16), ("twg", [4096, 8], BF16),
        ("bbase_col", [128, 32], F32), ("identf", [128, 128], F32),
        ("onesrow_f", [65, 128], F32), ("onesrow_b", [65, 128], BF16),
        ("ones_b", [128, 1], BF16), ("rep8", [8, 128], BF16),
    ]:
        ap_in[name] = nc.dram_tensor(name, shape, dt, kind="ExternalInput").ap()
    xn2_out = nc.dram_tensor("xn2", [128, 32 * B * TCOLS], BF16,
                             kind="ExternalOutput").ap()

    ybuf = nc.dram_tensor("ybuf", [B, 32, 128, TCOLS], BF16).ap()

    with ExitStack() as stack:
        E = stack.enter_context
        sems = make_sems(stack, nc)
        p = Prog(nc)

        def T(name, shape, dt=F32):
            t = E(nc.sbuf_tensor(name, shape, dt))
            return t, Buf()

        idf, idfB = T("idf", [128, 128])
        onesrob, onesrobB = T("onesrob", [65, 128], BF16)
        onesrow, onesrowB = T("onesrow", [65, 128])
        onesb, onesbB = T("onesb", [128, 1], BF16)
        rep8s, rep8B = T("rep8s", [8, 128], BF16)
        bbase, bbaseB = T("bbase", [128, 32])
        twg_sb, twgB = T("twg_sb", [128, 32 * 8], BF16)
        ta0_sb, ta0B = T("ta0_sb", [128, 32 * 128], BF16)
        tb0_sb, tb0B = T("tb0_sb", [128, 32 * 128], BF16)
        xres, xresB = T("xres", [128, 32 * B * TCOLS], BF16)
        xtm0, xtm0B = T("xtm0", [128, DM])
        xtm1, xtm1B = T("xtm1", [128, DM])
        xtmT = [xtm0, xtm1]; xtmB = [xtm0B, xtm1B]
        ybt = [xtm0.ap().bitcast(BF16), xtm1.ap().bitcast(BF16)]
        t1s, t1sB = T("t1s", [128, 512])
        t2s, t2sB = T("t2s", [128, 512])
        pall, pallB = T("pall", [128, 96])
        stot, stotB = T("stot", [128, 96])
        cdif, cdifB = T("cdif", [128, 64])
        coef, coefB = T("coef", [128, 4 * 32])
        glog, glogB = T("glog", [8, TCOLS])
        gtm, gtmB = T("gtm", [128, 9 * 8])
        rsc, rscB = T("rsc", [128, 48])
        rcol, rcolB = T("rcol", [128, 8])
        route, routeB = T("route", [8, TCOLS], BF16)
        rrep, rrepB = T("rrep", [128, TCOLS], BF16)
        hw, hwB = T("hw", [128, TCOLS], BF16)
        y2sb, y2sbB = T("y2sb", [128, TCOLS], BF16)
        ysq2, ysq2B = T("ysq2", [128, 512], BF16)
        tbb = [T(f"tbb{i}", [128, 8 * 128], BF16) for i in range(2)]
        cst, cstB = T("cst", [128, 4])
        PS = [psum_bank(stack, nc, f"bank{i}") for i in range(8)]
        PSB = [Buf() for _ in range(8)]

        def op(eng, fn, r=(), w=(), x=()):
            return emit(p, eng, fn, r, w, x)

        def dma(q, out_ap, in_ap, r=(), w=(), x=()):
            return emit_dma(p, q, out_ap, in_ap, r, w, x)

        op("v", lambda v: v.memset(cst[:, 0:1], float(1.0 / np.sqrt(8192.0))),
           w=[cstB])
        op("v", lambda v: v.memset(cst[:, 1:2], float(-1.0 / np.sqrt(8192.0))),
           r=[cstB])
        op("v", lambda v: v.memset(cst[:, 2:3], EPS_LN), r=[cstB])
        cstB.w = p.barrier_tok("v")

        dma("q", idf[:, :], ap_in["identf"], w=[idfB])
        dma("q", onesrow[:, :], ap_in["onesrow_f"], w=[onesrowB])
        dma("q", onesrob[:, :], ap_in["onesrow_b"], w=[onesrobB])
        dma("q", onesb[:, :], ap_in["ones_b"], w=[onesbB])
        dma("q", rep8s[:, :], ap_in["rep8"], w=[rep8B])
        dma("q", bbase[:, :], ap_in["bbase_col"], w=[bbaseB])
        dma("q", ta0_sb.ap().rearrange("p (k c) -> p k c", c=128),
            ap_in["ta04"][0].rearrange("k p c -> p k c"), w=[ta0B])
        for o in range(32):
            tok = dma("q", tb0_sb[:, o * 128:(o + 1) * 128], ap_in["tb04"][o, 0])
        tb0B.w = tok
        for kt in range(32):
            tok = dma("q", twg_sb[:, kt * 8:(kt + 1) * 8],
                      ap_in["twg"][kt * 128:(kt + 1) * 128, :])
        twgB.w = tok

        dma("q", pall[:, :], ap_in["pscall"][0], w=[pallB])
        op("v", lambda v: v.tensor_copy(stot[:, :], pall[:, :]),
           r=[pallB], w=[stotB])
        for c in range(1, NCORES):
            dma("q", pall[:, :], ap_in["pscall"][c], w=[pallB])
            op("v", (lambda c=c: lambda v: v.tensor_tensor(
                stot[:, :], stot[:, :], pall[:, :],
                op=ALU.add))(), r=[pallB, stotB], w=[stotB])
        op("v", lambda v: v.tensor_tensor(
            cdif[:, 0:32], stot[:, 0:96:3], stot[:, 1:96:3],
            op=ALU.subtract), r=[stotB], w=[cdifB])
        op("v", lambda v: v.tensor_tensor(
            cdif[:, 32:64], stot[:, 2:96:3], stot[:, 1:96:3],
            op=ALU.subtract), r=[stotB], w=[cdifB])
        op("s", lambda s: s.activation(coef[:, 0:32], cdif[:, 0:32],
                                       AF.Sigmoid, scale=cst[:, 0:1]),
           r=[cdifB, cstB], w=[coefB])
        op("s", lambda s: s.activation(coef[:, 32:64], cdif[:, 0:32],
                                       AF.Sigmoid, scale=cst[:, 1:2]),
           r=[cdifB, cstB], w=[coefB])
        op("s", lambda s: s.activation(coef[:, 64:96], cdif[:, 32:64],
                                       AF.Sigmoid, scale=cst[:, 1:2]),
           r=[cdifB, cstB], w=[coefB])
        op("s", lambda s: s.activation(coef[:, 96:128], cdif[:, 32:64],
                                       AF.Sigmoid, scale=cst[:, 0:1]),
           r=[cdifB, cstB], w=[coefB])
        op("v", lambda v: v.tensor_scalar(coef[:, 0:32], coef[:, 0:32],
                                          1.0, None, op0=ALU.add),
           r=[coefB], w=[coefB])
        op("v", lambda v: v.tensor_scalar(coef[:, 96:128], coef[:, 96:128],
                                          1.0, None, op0=ALU.add),
           r=[coefB], w=[coefB])

        for blk, (t0, tp) in enumerate(BLOCKS):
            for b in range(B):
                dma("q", xtmT[b][0:tp, :], ap_in["xs"][b, t0:t0 + tp, :],
                    w=[xtmB[b]])
            mv_done = []
            for lt in range(32):
                for b in range(B):
                    op("t", (lambda b=b, lt=lt, tp=tp: lambda t: t.transpose(
                        PS[b][0:128, 0:tp],
                        xtmT[b][0:tp, lt * 128:(lt + 1) * 128],
                        idf[0:tp, 0:tp]))(),
                       r=[xtmB[b], idfB], w=[PSB[b]])
                op("v", (lambda lt=lt, tp=tp: lambda v: v.tensor_scalar(
                    t1s[:, 0:tp], PS[1][:, 0:tp], coef[:, 32 + lt:33 + lt],
                    None, op0=ALU.mult))(), r=[PSB[1], coefB], w=[t1sB])
                y0 = op("v", (lambda lt=lt, tp=tp: lambda v:
                              v.scalar_tensor_tensor(
                                  ybt[0][0:128, lt * 128:lt * 128 + tp],
                                  PS[0][:, 0:tp], coef[:, lt:lt + 1],
                                  t1s[:, 0:tp], ALU.mult, ALU.add))(),
                        r=[PSB[0], coefB, t1sB], w=[xtm0B])
                op("v", (lambda lt=lt, tp=tp: lambda v: v.tensor_scalar(
                    t2s[:, 0:tp], PS[0][:, 0:tp], coef[:, 64 + lt:65 + lt],
                    None, op0=ALU.mult))(), r=[PSB[0], coefB], w=[t2sB])
                y1 = op("v", (lambda lt=lt, tp=tp: lambda v:
                              v.scalar_tensor_tensor(
                                  ybt[1][0:128, lt * 128:lt * 128 + tp],
                                  PS[1][:, 0:tp], coef[:, 96 + lt:97 + lt],
                                  t2s[:, 0:tp], ALU.mult, ALU.add))(),
                        r=[PSB[1], coefB, t2sB], w=[xtm1B])
                for b in range(B):
                    op("s", (lambda b=b, lt=lt, tp=tp: lambda s: s.activation(
                        ysq2[:, b * 128:b * 128 + tp],
                        ybt[b][0:128, lt * 128:lt * 128 + tp],
                        AF.Square))(), r=[xtmB[b]], w=[ysq2B])
                    op("t", (lambda b=b, lt=lt, tp=tp: lambda t: t.matmul(
                        PS[2 + 2 * b][0:1, 0:tp], onesb[:, 0:1],
                        ybt[b][0:128, lt * 128:lt * 128 + tp],
                        start=(lt == 0), stop=(lt == 31)))(),
                       r=[xtmB[b], onesbB], w=[PSB[2 + 2 * b]])
                    tokmv = op("t", (lambda b=b, lt=lt, tp=tp: lambda t:
                                     t.matmul(
                                         PS[3 + 2 * b][0:1, 0:tp],
                                         onesb[:, 0:1],
                                         ysq2[:, b * 128:b * 128 + tp],
                                         start=(lt == 0), stop=(lt == 31)))(),
                               r=[ysq2B, onesbB], w=[PSB[3 + 2 * b]])
                    if lt == 31:
                        mv_done.append(tokmv)
            for b in range(B):
                op("v", (lambda b=b, tp=tp: lambda v: v.tensor_scalar(
                    xtm1[0:1, 2560:2560 + tp], PS[2 + 2 * b][0:1, 0:tp], 1.0 / DM,
                    None, op0=ALU.mult))(), r=[PSB[2 + 2 * b]], w=[xtmB[1]])
                op("t", (lambda tp=tp: lambda t: t.matmul(
                    PS[6][:, 0:tp], onesrow[0:1, :], xtm1[0:1, 2560:2560 + tp],
                    start=True, stop=True))(), r=[xtmB[1], onesrowB], w=[PSB[6]])
                op("v", (lambda b=b, tp=tp: lambda v: v.tensor_scalar(
                    xtm1[0:1, 3072:3072 + tp], PS[3 + 2 * b][0:1, 0:tp],
                    1.0 / DM, None, op0=ALU.mult))(),
                   r=[PSB[3 + 2 * b]], w=[xtmB[1]])
                op("v", (lambda tp=tp: lambda v: v.tensor_tensor(
                    xtm1[0:1, 3584:3584 + tp], xtm1[0:1, 2560:2560 + tp], xtm1[0:1, 2560:2560 + tp],
                    op=ALU.mult))(), r=[xtmB[1]], w=[xtmB[1]])
                op("v", (lambda tp=tp: lambda v: v.tensor_tensor(
                    xtm1[0:1, 3584:3584 + tp], xtm1[0:1, 3072:3072 + tp], xtm1[0:1, 3584:3584 + tp],
                    op=ALU.subtract))(), r=[xtmB[1]], w=[xtmB[1]])
                op("s", (lambda tp=tp: lambda s: s.activation(
                    xtm1[0:1, 3072:3072 + tp], xtm1[0:1, 3584:3584 + tp],
                    AF.Sqrt, bias=cst[0:1, 2:3]))(), r=[xtmB[1], cstB], w=[xtmB[1]])
                op("v", (lambda tp=tp: lambda v: v.reciprocal(
                    xtm1[0:1, 3584:3584 + tp], xtm1[0:1, 3072:3072 + tp]))(),
                   r=[xtmB[1]], w=[xtmB[1]])
                op("t", (lambda tp=tp: lambda t: t.matmul(
                    PS[7][:, 0:tp], onesrow[0:1, :], xtm1[0:1, 3584:3584 + tp],
                    start=True, stop=True))(), r=[xtmB[1], onesrowB], w=[PSB[7]])
                for lt in range(32):
                    op("v", (lambda b=b, lt=lt, tp=tp: lambda v:
                             v.tensor_tensor(
                                 t1s[:, 0:tp],
                                 ybt[b][0:128, lt * 128:lt * 128 + tp],
                                 PS[6][:, 0:tp], op=ALU.subtract))(),
                       r=[xtmB[b], PSB[6]], w=[t1sB])
                    op("v", (lambda b=b, lt=lt, tp=tp, t0=t0: lambda v:
                             v.tensor_tensor(
                                 xres[:, xncol(lt, b, t0):xncol(lt, b, t0) + tp],
                                 t1s[:, 0:tp], PS[7][:, 0:tp],
                                 op=ALU.mult))(),
                       r=[t1sB, PSB[7]], w=[xresB])

        for bsel in range(B):
            for kt in range(32):
                for ci, (c0, cw) in enumerate(CHUNKS):
                    op("t", (lambda kt=kt, ci=ci, c0=c0, cw=cw, bsel=bsel:
                             lambda t: t.matmul(
                                 PS[ci][0:8, 0:cw],
                                 twg_sb[:, kt * 8:(kt + 1) * 8],
                                 xres[:, xncol(kt, bsel, c0):xncol(kt, bsel, c0) + cw],
                                 start=(kt == 0), stop=(kt == 31)))(),
                       r=[twgB, xresB], w=[PSB[ci]])
            for ci, (c0, cw) in enumerate(CHUNKS):
                op("s", (lambda ci=ci, c0=c0, cw=cw: lambda s: s.copy(
                    glog[0:8, c0:c0 + cw], PS[ci][0:8, 0:cw]))(),
                   r=[PSB[ci]], w=[glogB])
            for blk, (t0, tp) in enumerate(BLOCKS):
                op("t", (lambda t0=t0, tp=tp: lambda t: t.transpose(
                    PS[6][0:tp, 0:8], glog[0:8, t0:t0 + tp],
                    idf[0:8, 0:8]))(), r=[glogB, idfB], w=[PSB[6]])
                op("s", (lambda blk=blk, tp=tp: lambda s: s.copy(
                    gtm[0:tp, 0:8], PS[6][0:tp, 0:8]))(),
                   r=[PSB[6]], w=[gtmB])
                op("v", (lambda tp=tp: lambda v: v.tensor_reduce(
                    rcol[0:tp, 0:1], gtm[0:tp, 0:8], mybir.AxisListType.X,
                    ALU.max, negate=True))(), r=[gtmB], w=[rcolB])
                op("s", (lambda tp=tp: lambda s: s.activation(
                    rsc[0:tp, 0:8], gtm[0:tp, 0:8], AF.Exp,
                    bias=rcol[0:tp, 0:1]))(), r=[gtmB, rcolB], w=[rscB])
                op("v", (lambda tp=tp: lambda v: v.tensor_reduce(
                    rcol[0:tp, 1:2], rsc[0:tp, 0:8], mybir.AxisListType.X,
                    ALU.add))(), r=[rscB], w=[rcolB])
                op("v", (lambda tp=tp: lambda v: v.reciprocal(
                    rcol[0:tp, 2:3], rcol[0:tp, 1:2]))(), r=[rcolB], w=[rcolB])
                op("v", (lambda tp=tp: lambda v: v.tensor_scalar(
                    rsc[0:tp, 8:16], rsc[0:tp, 0:8], rcol[0:tp, 2:3], None,
                    op0=ALU.mult))(), r=[rscB, rcolB], w=[rscB])
                op("v", (lambda tp=tp: lambda v: v.tensor_reduce(
                    rcol[0:tp, 3:4], rsc[0:tp, 8:16], mybir.AxisListType.X,
                    ALU.max))(), r=[rscB], w=[rcolB])
                op("v", (lambda tp=tp: lambda v: v.tensor_scalar(
                    rsc[0:tp, 16:24], rsc[0:tp, 8:16], rcol[0:tp, 3:4], None,
                    op0=ALU.is_ge))(), r=[rscB, rcolB], w=[rscB])
                op("v", (lambda tp=tp: lambda v: v.scalar_tensor_tensor(
                    rsc[0:tp, 24:32], rsc[0:tp, 16:24], -2.0,
                    rsc[0:tp, 8:16], ALU.mult, ALU.add))(), r=[rscB], w=[rscB])
                op("v", (lambda tp=tp: lambda v: v.tensor_reduce(
                    rcol[0:tp, 4:5], rsc[0:tp, 24:32], mybir.AxisListType.X,
                    ALU.max))(), r=[rscB], w=[rcolB])
                op("v", (lambda tp=tp: lambda v: v.tensor_tensor(
                    rcol[0:tp, 5:6], rcol[0:tp, 3:4], rcol[0:tp, 4:5],
                    op=ALU.add))(), r=[rcolB], w=[rcolB])
                op("v", (lambda tp=tp: lambda v: v.reciprocal(
                    rcol[0:tp, 6:7], rcol[0:tp, 5:6]))(), r=[rcolB], w=[rcolB])
                op("v", (lambda tp=tp: lambda v: v.tensor_scalar(
                    rsc[0:tp, 32:40], rsc[0:tp, 8:16], rcol[0:tp, 4:5], None,
                    op0=ALU.is_ge))(), r=[rscB, rcolB], w=[rscB])
                op("v", (lambda tp=tp: lambda v: v.scalar_tensor_tensor(
                    rsc[0:tp, 40:48], rsc[0:tp, 8:16], rcol[0:tp, 6:7],
                    rsc[0:tp, 32:40], ALU.mult, ALU.mult))(),
                   r=[rscB, rcolB], w=[rscB])
                op("t", (lambda tp=tp: lambda t: t.transpose(
                    PS[6][0:8, 0:tp], rsc[0:tp, 40:48], idf[0:tp, 0:tp]))(),
                   r=[rscB, idfB], w=[PSB[6]])
                op("s", (lambda t0=t0, tp=tp: lambda s: s.copy(
                    route[0:8, t0:t0 + tp], PS[6][0:8, 0:tp]))(),
                   r=[PSB[6]], w=[routeB])
            for ci, (c0, cw) in enumerate(CHUNKS):
                op("t", (lambda c0=c0, cw=cw: lambda t: t.matmul(
                    PS[6][:, 0:cw], rep8s[0:8, :], route[0:8, c0:c0 + cw],
                    start=True, stop=True))(), r=[routeB, rep8B], w=[PSB[6]])
                op("s", (lambda c0=c0, cw=cw: lambda s: s.copy(
                    rrep[:, c0:c0 + cw], PS[6][:, 0:cw]))(),
                   r=[PSB[6]], w=[rrepB])
            for ci, (c0, cw) in enumerate(CHUNKS):
                for kt in range(32):
                    op("t", (lambda kt=kt, ci=ci, c0=c0, cw=cw, bsel=bsel:
                             lambda t: t.matmul(
                                 PS[ci][:, 0:cw],
                                 ta0_sb[:, kt * 128:(kt + 1) * 128],
                                 xres[:, xncol(kt, bsel, c0):xncol(kt, bsel, c0) + cw],
                                 start=(kt == 0), stop=(kt == 31)))(),
                       r=[ta0B, xresB], w=[PSB[ci]])
                op("v", (lambda ci=ci, c0=c0, cw=cw: lambda v:
                         v.tensor_tensor(hw[:, c0:c0 + cw], PS[ci][:, 0:cw],
                                         rrep[:, c0:c0 + cw],
                                         op=ALU.mult))(),
                   r=[PSB[ci], rrepB], w=[hwB])
            for ot in range(32):
                def ld_quarter(q, ot=ot):
                    tbt, tbtB = tbb[q % 2]
                    dma("s", tbt.ap().rearrange("p (k c) -> p k c", c=128),
                        ap_in["tbase4"][ot, q * 8:(q + 1) * 8].rearrange(
                            "k p c -> p k c"), w=[tbtB])
                ld_quarter(0)
                ld_quarter(1)
                for kt in range(32):
                    if kt == 8:
                        ld_quarter(2)
                    if kt == 16:
                        ld_quarter(3)
                    tbt, tbtB = tbb[(kt // 8) % 2]
                    for ci, (c0, cw) in enumerate(CHUNKS):
                        op("t", (lambda kt=kt, ci=ci, c0=c0, cw=cw, bsel=bsel,
                                 tbt=tbt, tbtB=tbtB: lambda t: t.matmul(
                                     PS[ci][:, 0:cw],
                                     tbt[:, (kt % 8) * 128:(kt % 8 + 1) * 128],
                                     xres[:, xncol(kt, bsel, c0):xncol(kt, bsel, c0) + cw],
                                     start=(kt == 0), stop=False))(),
                           r=[tbtB, xresB], w=[PSB[ci]])
                for ci, (c0, cw) in enumerate(CHUNKS):
                    op("t", (lambda ci=ci, c0=c0, cw=cw, ot=ot: lambda t:
                             t.matmul(PS[ci][:, 0:cw],
                                      tb0_sb[:, ot * 128:(ot + 1) * 128],
                                      hw[:, c0:c0 + cw],
                                      start=False, stop=True))(),
                       r=[tb0B, hwB], w=[PSB[ci]])
                    op("s", (lambda ci=ci, cw=cw, ot=ot: lambda s:
                             s.activation(t1s[:, 0:cw], PS[ci][:, 0:cw],
                                          AF.Identity,
                                          bias=bbase[:, ot:ot + 1]))(),
                       r=[PSB[ci], bbaseB], w=[t1sB])
                    op("v", (lambda ci=ci, c0=c0, cw=cw, ot=ot, bsel=bsel:
                             lambda v: v.tensor_tensor(
                                 y2sb[:, c0:c0 + cw], t1s[:, 0:cw],
                                 xres[:, xncol(ot, bsel, c0):xncol(ot, bsel, c0) + cw],
                                 op=ALU.add))(),
                       r=[t1sB, xresB], w=[y2sbB])
                    op("s", (lambda c0=c0, cw=cw: lambda s: s.activation(
                        ysq2[:, 0:cw], y2sb[:, c0:c0 + cw], AF.Square))(),
                       r=[y2sbB], w=[ysq2B])
                    op("t", (lambda ci=ci, c0=c0, cw=cw, ot=ot: lambda t:
                             t.matmul(PS[3 + ci][0:1, 0:cw], onesb[:, 0:1],
                                      y2sb[:, c0:c0 + cw],
                                      start=(ot == 0), stop=False))(),
                       r=[y2sbB, onesbB], w=[PSB[3 + ci]])
                    op("t", (lambda ci=ci, c0=c0, cw=cw, ot=ot: lambda t:
                             t.matmul(PS[3 + ci][32:33, 0:cw], onesb[:, 0:1],
                                      ysq2[:, 0:cw],
                                      start=(ot == 0), stop=(ot == 31)))(),
                       r=[ysq2B, onesbB], w=[PSB[3 + ci]])
                dma("q", ybuf[bsel, ot], y2sb[:, :], r=[y2sbB])
            for ci, (c0, cw) in enumerate(CHUNKS):
                op("v", (lambda ci=ci, cw=cw: lambda v: v.tensor_scalar(
                    xtm1[0:1, 2560:2560 + cw], PS[3 + ci][0:1, 0:cw], 1.0 / DM, None,
                    op0=ALU.mult))(), r=[PSB[3 + ci]], w=[xtmB[1]])
                op("v", (lambda ci=ci, c0=c0, cw=cw, bsel=bsel: lambda v:
                         v.tensor_copy(
                             ybt[0][0:1, bsel * TCOLS + c0:bsel * TCOLS + c0 + cw],
                             xtm1[0:1, 2560:2560 + cw]))(), r=[xtmB[1]], w=[xtmB[0]])
                op("v", (lambda ci=ci, cw=cw: lambda v: v.tensor_scalar(
                    xtm1[0:1, 3072:3072 + cw], PS[3 + ci][32:33, 0:cw],
                    1.0 / DM, None, op0=ALU.mult))(),
                   r=[PSB[3 + ci]], w=[xtmB[1]])
                op("v", (lambda cw=cw: lambda v: v.tensor_tensor(
                    xtm1[0:1, 3584:3584 + cw], xtm1[0:1, 2560:2560 + cw], xtm1[0:1, 2560:2560 + cw],
                    op=ALU.mult))(), r=[xtmB[1]], w=[xtmB[1]])
                op("v", (lambda cw=cw: lambda v: v.tensor_tensor(
                    xtm1[0:1, 3584:3584 + cw], xtm1[0:1, 3072:3072 + cw],
                    xtm1[0:1, 3584:3584 + cw], op=ALU.subtract))(),
                   r=[xtmB[1]], w=[xtmB[1]])
                op("s", (lambda cw=cw: lambda s: s.activation(
                    xtm1[0:1, 3072:3072 + cw], xtm1[0:1, 3584:3584 + cw],
                    AF.Sqrt, bias=cst[0:1, 2:3]))(), r=[xtmB[1], cstB], w=[xtmB[1]])
                op("v", (lambda cw=cw: lambda v: v.reciprocal(
                    xtm1[0:1, 3584:3584 + cw], xtm1[0:1, 3072:3072 + cw]))(),
                   r=[xtmB[1]], w=[xtmB[1]])
                op("v", (lambda ci=ci, c0=c0, cw=cw, bsel=bsel: lambda v:
                         v.tensor_copy(
                             ybt[0][0:1, 2054 + bsel * TCOLS + c0:2054 + bsel * TCOLS + c0 + cw],
                             xtm1[0:1, 3584:3584 + cw]))(), r=[xtmB[1]], w=[xtmB[0]])

        for bsel in range(B):
            for ci, (c0, cw) in enumerate(CHUNKS):
                op("t", (lambda ci=ci, c0=c0, cw=cw, bsel=bsel: lambda t:
                         t.matmul(PS[ci][:, 0:cw], onesrob[0:1, :],
                                  ybt[0][0:1, bsel * TCOLS + c0:bsel * TCOLS + c0 + cw],
                                  start=True, stop=True))(),
                   r=[xtmB[0], onesrobB], w=[PSB[ci]])
                op("t", (lambda ci=ci, c0=c0, cw=cw, bsel=bsel: lambda t:
                         t.matmul(PS[3 + ci][:, 0:cw], onesrob[0:1, :],
                                  ybt[0][0:1, 2054 + bsel * TCOLS + c0:2054 + bsel * TCOLS + c0 + cw],
                                  start=True, stop=True))(),
                   r=[xtmB[0], onesrobB], w=[PSB[3 + ci]])
            for ot in range(32):
                dma("g", y2sb[:, :], ybuf[bsel, ot], w=[y2sbB],
                    x=[("dq", p.dcnt["q"])])
                for ci, (c0, cw) in enumerate(CHUNKS):
                    op("v", (lambda ci=ci, c0=c0, cw=cw: lambda v:
                             v.tensor_tensor(t1s[:, 0:cw],
                                             y2sb[:, c0:c0 + cw],
                                             PS[ci][:, 0:cw],
                                             op=ALU.subtract))(),
                       r=[y2sbB, PSB[ci]], w=[t1sB])
                    op("v", (lambda ci=ci, c0=c0, cw=cw, ot=ot, bsel=bsel:
                             lambda v: v.tensor_tensor(
                                 xres[:, xncol(ot, bsel, c0):xncol(ot, bsel, c0) + cw],
                                 t1s[:, 0:cw], PS[3 + ci][:, 0:cw],
                                 op=ALU.mult))(),
                       r=[t1sB, PSB[3 + ci]], w=[xresB])
        st = dma("q", xn2_out, xres[:, :], r=[xresB])
        p._waits("q", [st])
        with nc.Block() as block:
            p.replay(block, sems)
    return nc



CH3 = [(0, 512), (512, 512), (1024, 3)]
OWN2 = [(HALO, 512), (HALO + 512, 512)]


def build_l1attn():
    nc = bass.Bass("TRN2", target_bir_lowering=False)
    ap_in = {}
    for name, shape, dt in [
        ("xn2", [128, 32 * B * TCOLS], BF16),
        ("twin4", [128, 32, 128, 128], BF16),
        ("twx4", [3, 64, 128, 128], BF16),
        ("twdt4", [64, 2, 128, 128], BF16),
        ("identb", [128, 128], BF16),
        ("onesrow_f", [65, 128], F32),
        ("ones_b", [128, 1], BF16),
        ("convw_col", [128, 256], F32),
        ("convb_col", [128, 64], F32),
        ("dtb_col", [128, 64], F32),
        ("dpar_col", [128, 64], F32),
        ("acol", [128, 1024], F32),
        ("dtlnw_col", [128, 2], F32),
        ("bcw_col", [16, 2], F32),
        ("rep_sel", [16, 2048], BF16),
    ]:
        ap_in[name] = nc.dram_tensor(name, shape, dt, kind="ExternalInput").ap()
    o_ypre = nc.dram_tensor("ypre", [B, 64, 128, 1024], BF16,
                            kind="ExternalOutput").ap()
    o_r = nc.dram_tensor("r_out", [B, 64, 128, 1024], F32,
                         kind="ExternalOutput").ap()
    o_sg = nc.dram_tensor("sg_out", [B, 64, 128, 1024], BF16,
                          kind="ExternalOutput").ap()
    o_ps = nc.dram_tensor("pso", [128, 4096], F32, kind="ExternalOutput").ap()
    o_cn = nc.dram_tensor("cn_out", [B, 2, 16, 512], BF16,
                          kind="ExternalOutput").ap()
    u_sp = nc.dram_tensor("u_sp", [B, 64, 128, 1024], BF16).ap()

    with ExitStack() as stack:
        E = stack.enter_context
        sems = make_sems(stack, nc)
        p = Prog(nc)

        def T(name, shape, dt=F32):
            t = E(nc.sbuf_tensor(name, shape, dt))
            return t, Buf()

        idb, idbB = T("idb", [128, 128], BF16)
        onesrow, onesrowB = T("onesrow", [65, 128])
        onesb, onesbB = T("onesb", [128, 1], BF16)
        convw, convwB = T("convw", [128, 256])
        convb, convbB = T("convb", [128, 64])
        dtb, dtbB = T("dtb", [128, 64])
        dpar, dparB = T("dpar", [128, 64])
        acol, acolB = T("acol_sb", [128, 1024])
        dtlnw, dtlnwB = T("dtlnw", [128, 2])
        bcw, bcwB = T("bcw", [16, 2])
        repsel, repselB = T("repsel", [16, 2048], BF16)
        xres, xresB = T("xres", [128, 32 * B * TCOLS], BF16)

        arena, arenaB = T("arena", [128, 4096])
        arena_bf = arena.ap().bitcast(BF16)
        twinh = [(arena_bf[:, 0:2048], Buf()), (arena_bf[:, 2048:4096], Buf())]
        u_raw = arena[:, 2048:2048 + TCOLS]
        u_rawB = Buf()
        cacc, caccB = T("cacc", [128, 1024])
        u_act, u_actB = T("u_act", [128, 1024], BF16)
        sg_t, sg_tB = T("sg_t", [128, 1024], BF16)
        twdt_t = [T(f"twdt{i}", [128, 256], BF16) for i in range(2)]
        dtn = [T(f"dtn{i}", [128, 512], BF16) for i in range(2)]
        sq_t, sq_tB = T("sq_t", [128, 512], BF16)
        rrep_sb, rrep_sbB = T("rrep_sb", [128, 512], BF16)
        repb_sb, repb_sbB = T("repb_sb", [16, 512], BF16)
        repc_sb, repc_sbB = T("repc_sb", [16, 512], BF16)
        bn_sb, bn_sbB = T("bn_sb", [16, 512], BF16)
        cn_sb, cn_sbB = T("cn_sb", [16, 512], BF16)
        rowS, rowSB = T("rowS", [128, 512])
        delta, deltaB = T("delta", [128, 512])
        r_t, r_tB = T("r_t", [128, 512])
        p_t, p_tB = T("p_t", [128, 512], BF16)
        ones_t, ones_tB = T("ones_t", [128, 512])
        dA = [T(f"dA{i}", [128, 512]) for i in range(2)]
        xin = [T(f"xin{i}", [128, 512], BF16) for i in range(2)]
        hall = arena_bf[:, 0:16 * 512]
        hallB = Buf()
        hct = [T(f"hct{i}", [128, 512], BF16) for i in range(2)]
        ud_t, ud_tB = T("ud_t", [128, 512], BF16)
        ypre_t, ypre_tB = T("ypre_t", [128, 512], BF16)
        tsp = [(cacc[:, 0:512], Buf()), (cacc[:, 512:1024], Buf())]

        cst, cstB = T("cst", [128, 4])
        PS = [psum_bank(stack, nc, f"bank{i}") for i in range(8)]
        PSB = [Buf() for _ in range(8)]
        xres_bf = xres.ap()
        xres_f32 = xres.ap().bitcast(F32)
        twx_sb = xres_bf[:, 0:24576]
        twx_sbB = Buf()
        brep = xres_bf[:, 24576:32768]
        brepB = Buf()
        ustr = [(E(nc.sbuf_tensor(f"ustr{i}", [128, 8 * 512], BF16)), Buf())
                for i in range(2)]
        crep = xres_bf[:, 32768:40960]
        crepB = Buf()
        carry_h = xres_f32[:, 20480:22528]
        carry_hB = Buf()
        parg = xres_f32[:, 22528:23552]
        pargB = Buf()
        r_carry = xres_f32[:, 23552:23680]
        r_carryB = Buf()

        def op(eng, fn, r=(), w=(), x=()):
            return emit(p, eng, fn, r, w, x)

        def dma(q, out_ap, in_ap, r=(), w=(), x=()):
            return emit_dma(p, q, out_ap, in_ap, r, w, x)

        op("v", lambda v: v.memset(cst[:, 0:1], EPS_RMS), w=[cstB])
        op("v", lambda v: v.memset(cst[:, 1:2], 1.0 / 256.0), r=[cstB])
        op("v", lambda v: v.memset(cst[:, 2:3], 1.0 / 16.0), r=[cstB])
        op("v", lambda v: v.memset(cst[:, 3:4], 1.0), r=[cstB])
        op("v", lambda v: v.memset(ones_t[:, :], 1.0), w=[ones_tB])
        cstB.w = p.barrier_tok("v")
        for apn, sb, bf in [("identb", idb, idbB), ("onesrow_f", onesrow, onesrowB),
                            ("ones_b", onesb, onesbB), ("convw_col", convw, convwB),
                            ("convb_col", convb, convbB), ("dtb_col", dtb, dtbB),
                            ("dpar_col", dpar, dparB), ("acol", acol, acolB),
                            ("dtlnw_col", dtlnw, dtlnwB), ("bcw_col", bcw, bcwB),
                            ("rep_sel", repsel, repselB)]:
            dma("q", sb[:, :] if hasattr(sb, "__getitem__") else sb, ap_in[apn],
                w=[bf])
        dma("q", xres[:, :], ap_in["xn2"], w=[xresB])

        for ut in range(64):
            for h in range(2):
                tw, twB = twinh[h]
                dma("s", tw.rearrange("p (k c) -> p k c", c=128),
                    ap_in["twin4"][ut, h * 16:(h + 1) * 16].rearrange(
                        "k p c -> p k c"), w=[twB])
            for b in range(B):
                for ci, (c0, cw) in enumerate(CH3):
                    for kt in range(32):
                        tw, twB = twinh[kt // 16]
                        op("t", (lambda kt=kt, ci=ci, c0=c0, cw=cw, b=b, tw=tw:
                                 lambda t: t.matmul(
                                     PS[ci][:, 0:cw],
                                     tw[:, (kt % 16) * 128:(kt % 16 + 1) * 128],
                                     xres[:, xncol(kt, b, c0):xncol(kt, b, c0) + cw],
                                     start=(kt == 0), stop=(kt == 31)))(),
                           r=[twB, xresB], w=[PSB[ci]])
                    op("s", (lambda ci=ci, c0=c0, cw=cw: lambda s: s.copy(
                        u_raw[:, c0:c0 + cw], PS[ci][:, 0:cw]))(),
                       r=[PSB[ci]], w=[u_rawB])
                op("v", (lambda ut=ut: lambda v: v.tensor_scalar(
                    cacc[:, :], u_raw[:, 0:1024], convw[:, 4 * ut:4 * ut + 1],
                    None, op0=ALU.mult))(), r=[u_rawB, convwB], w=[caccB])
                for k in range(1, 4):
                    op("v", (lambda ut=ut, k=k: lambda v:
                             v.scalar_tensor_tensor(
                                 cacc[:, :], u_raw[:, k:1024 + k],
                                 convw[:, 4 * ut + k:4 * ut + k + 1],
                                 cacc[:, :], ALU.mult, ALU.add))(),
                       r=[u_rawB, convwB, caccB], w=[caccB])
                op("s", (lambda ut=ut: lambda s: s.activation(
                    u_act[:, :], cacc[:, :], AF.Silu,
                    bias=convb[:, ut:ut + 1]))(),
                   r=[caccB, convbB], w=[u_actB])
                dma("q", u_sp[b, ut], u_act[:, :], r=[u_actB])
        for gt in range(64):
            for h in range(2):
                tw, twB = twinh[h]
                dma("s", tw.rearrange("p (k c) -> p k c", c=128),
                    ap_in["twin4"][64 + gt, h * 16:(h + 1) * 16].rearrange(
                        "k p c -> p k c"), w=[twB])
            for b in range(B):
                for ci, (c0, cw) in enumerate(OWN2):
                    for kt in range(32):
                        tw, twB = twinh[kt // 16]
                        op("t", (lambda kt=kt, ci=ci, c0=c0, cw=cw, b=b, tw=tw:
                                 lambda t: t.matmul(
                                     PS[ci][:, 0:cw],
                                     tw[:, (kt % 16) * 128:(kt % 16 + 1) * 128],
                                     xres[:, xncol(kt, b, c0):xncol(kt, b, c0) + cw],
                                     start=(kt == 0), stop=(kt == 31)))(),
                           r=[twB, xresB], w=[PSB[ci]])
                    op("s", (lambda ci=ci, cw=cw: lambda s: s.activation(
                        sg_t[:, ci * 512:ci * 512 + cw], PS[ci][:, 0:cw],
                        AF.Silu))(), r=[PSB[ci]], w=[sg_tB])
                dma("q", o_sg[b, gt], sg_t[:, :], r=[sg_tB])

        xres_dead = p.all_toks()
        for bb in (hallB, tsp[0][1], tsp[1][1], twx_sbB, brepB, crepB,
                   carry_hB, pargB, r_carryB):
            bb.r = list(xres_dead)
        op("v", lambda v: v.memset(carry_h[:, :], 0.0), w=[carry_hB])
        op("v", lambda v: v.memset(r_carry[:, :], 0.0), w=[r_carryB])
        for g in range(3):
            dma("g", twx_sb[:, g * 64 * 128:(g + 1) * 64 * 128].rearrange(
                "p (k c) -> p k c", c=128),
                ap_in["twx4"][g].rearrange("k p c -> p k c"), w=[twx_sbB])
        for b in range(B):
            for fc in range(2):
                def uload(g8, b=b, fc=fc):
                    ub, ubB = ustr[g8 % 2]
                    dma("g", ub.ap().rearrange("p (u t) -> p u t", t=512),
                        u_sp[b, g8 * 8:(g8 + 1) * 8, :,
                             fc * 512:(fc + 1) * 512].rearrange(
                        "u p t -> p u t"), w=[ubB])
                uload(0)
                uload(1)
                for kt in range(64):
                    if kt % 8 == 0 and kt >= 8 and kt // 8 + 1 < 8:
                        uload(kt // 8 + 1)
                    ub, ubB = ustr[(kt // 8) % 2]
                    for gi, (ps_i, lo, hi) in enumerate(
                            [(0, 0, 128), (1, 0, 128), (2, 0, 16), (3, 16, 32)]):
                        gsl = [0, 1, 2, 2][gi]
                        op("t", (lambda kt=kt, ps_i=ps_i, lo=lo, hi=hi,
                                 gsl=gsl, ub=ub: lambda t: t.matmul(
                                     PS[ps_i][0:hi - lo, 0:512],
                                     twx_sb[:, gsl * 64 * 128 + kt * 128 + lo:
                                            gsl * 64 * 128 + kt * 128 + hi],
                                     ub[:, (kt % 8) * 512:(kt % 8 + 1) * 512],
                                     start=(kt == 0), stop=(kt == 63)))(),
                           r=[twx_sbB, ubB], w=[PSB[ps_i]])
                sq1 = op("s", lambda s: s.activation(sq_t[:, :], PS[0][:, 0:512],
                                                     AF.Square),
                         r=[PSB[0]], w=[sq_tB])
                op("t", lambda t: t.matmul(PS[4][0:1, 0:512], onesb[:, 0:1],
                                           sq_t[:, :], start=True, stop=False),
                   r=[sq_tB, onesbB], w=[PSB[4]])
                op("s", lambda s: s.activation(sq_t[:, :], PS[1][:, 0:512],
                                               AF.Square),
                   r=[PSB[1], sq_tB], w=[sq_tB])
                op("t", lambda t: t.matmul(PS[4][0:1, 0:512], onesb[:, 0:1],
                                           sq_t[:, :], start=False, stop=True),
                   r=[sq_tB, onesbB], w=[PSB[4]])
                op("s", lambda s: s.activation(rowS[0:1, 0:512], PS[4][0:1, 0:512],
                                               AF.Sqrt, scale=cst[0:1, 1:2],
                                               bias=cst[0:1, 0:1]),
                   r=[PSB[4], cstB], w=[rowSB])
                op("v", lambda v: v.reciprocal(rowS[0:1, 0:512], rowS[0:1, 0:512]),
                   r=[rowSB], w=[rowSB])
                op("t", lambda t: t.matmul(PS[5][:, 0:512], onesrow[0:1, :],
                                           rowS[0:1, 0:512], start=True,
                                           stop=True),
                   r=[rowSB, onesrowB], w=[PSB[5]])
                op("s", lambda s: s.copy(rrep_sb[:, :], PS[5][:, 0:512]),
                   r=[PSB[5]], w=[rrep_sbB])
                for i in range(2):
                    op("v", (lambda i=i: lambda v: v.scalar_tensor_tensor(
                        dtn[i][0][:, :], PS[i][:, 0:512], dtlnw[:, i:i + 1],
                        rrep_sb[:, :], ALU.mult, ALU.mult))(),
                       r=[PSB[i], dtlnwB, rrep_sbB], w=[dtn[i][1]])
                for gi, (ps_i, nsb, nsbB, repsb, repsbB) in enumerate(
                        [(2, bn_sb, bn_sbB, repb_sb, repb_sbB),
                         (3, cn_sb, cn_sbB, repc_sb, repc_sbB)]):
                    op("s", (lambda ps_i=ps_i: lambda s: s.activation(
                        sq_t[0:16, 0:512], PS[ps_i][0:16, 0:512], AF.Square))(),
                       r=[PSB[ps_i], sq_tB], w=[sq_tB])
                    op("t", lambda t: t.matmul(PS[4][0:1, 0:512],
                                               onesb[0:16, 0:1],
                                               sq_t[0:16, 0:512],
                                               start=True, stop=True),
                       r=[sq_tB, onesbB], w=[PSB[4]])
                    op("s", lambda s: s.activation(rowS[0:1, 0:512],
                                                   PS[4][0:1, 0:512],
                                                   AF.Sqrt, scale=cst[0:1, 2:3],
                                                   bias=cst[0:1, 0:1]),
                       r=[PSB[4], cstB], w=[rowSB])
                    op("v", lambda v: v.reciprocal(rowS[0:1, 0:512],
                                                   rowS[0:1, 0:512]),
                       r=[rowSB], w=[rowSB])
                    op("t", lambda t: t.matmul(PS[5][0:16, 0:512],
                                               onesrow[0:1, 0:16],
                                               rowS[0:1, 0:512],
                                               start=True, stop=True),
                       r=[rowSB, onesrowB], w=[PSB[5]])
                    op("s", (lambda repsb=repsb: lambda s: s.copy(
                        repsb[:, :], PS[5][0:16, 0:512]))(),
                       r=[PSB[5]], w=[repsbB])
                    op("v", (lambda ps_i=ps_i, gi=gi, nsb=nsb, repsb=repsb:
                             lambda v: v.scalar_tensor_tensor(
                                 nsb[:, :], PS[ps_i][0:16, 0:512],
                                 bcw[:, gi:gi + 1], repsb[:, :],
                                 ALU.mult, ALU.mult))(),
                       r=[PSB[ps_i], bcwB, repsbB], w=[nsbB])
                dma("q", o_cn[b, fc], cn_sb[:, :], r=[cn_sbB])
                for n in range(16):
                    for (nsb, nsbB, dst, dstB) in [(bn_sb, bn_sbB, brep, brepB),
                                                   (cn_sb, cn_sbB, crep, crepB)]:
                        op("t", (lambda n=n, nsb=nsb: lambda t: t.matmul(
                            PS[5][:, 0:512], repsel[:, n * 128:(n + 1) * 128],
                            nsb[:, :], start=True, stop=True))(),
                           r=[nsbB, repselB], w=[PSB[5]])
                        op("s", (lambda n=n, dst=dst: lambda s: s.copy(
                            dst[:, n * 512:(n + 1) * 512], PS[5][:, 0:512]))(),
                           r=[PSB[5]], w=[dstB])
                uload(0)
                uload(1)
                for ut in range(64):
                    if ut % 8 == 0 and ut >= 8 and ut // 8 + 1 < 8:
                        uload(ut // 8 + 1)
                    uu, uuB = ustr[(ut // 8) % 2]
                    twd, twdB = twdt_t[ut % 2]
                    dma("s", twd.ap().rearrange("p (k c) -> p k c", c=128),
                        ap_in["twdt4"][ut].rearrange("k p c -> p k c"),
                        w=[twdB])
                    for i in range(2):
                        op("t", (lambda i=i, twd=twd: lambda t: t.matmul(
                            PS[6][:, 0:512], twd[:, i * 128:(i + 1) * 128],
                            dtn[i][0][:, :], start=(i == 0), stop=(i == 1)))(),
                           r=[twdB, dtn[i][1]], w=[PSB[6]])
                    op("s", (lambda ut=ut: lambda s: s.activation(
                        tsp[0][0][:, :], PS[6][:, 0:512], AF.Abs,
                        bias=dtb[:, ut:ut + 1]))(),
                       r=[PSB[6], dtbB], w=[tsp[0][1]])
                    op("s", lambda s: s.activation(tsp[0][0][:, :],
                                                   tsp[0][0][:, :], AF.Exp,
                                                   scale=-1.0),
                       r=[tsp[0][1]], w=[tsp[0][1]])
                    op("s", lambda s: s.activation(tsp[0][0][:, :],
                                                   tsp[0][0][:, :], AF.Ln,
                                                   bias=cst[:, 3:4]),
                       r=[tsp[0][1], cstB], w=[tsp[0][1]])
                    op("s", (lambda ut=ut: lambda s: s.activation(
                        tsp[1][0][:, :], PS[6][:, 0:512], AF.Relu,
                        bias=dtb[:, ut:ut + 1]))(),
                       r=[PSB[6], dtbB], w=[tsp[1][1]])
                    op("v", lambda v: v.tensor_tensor(delta[:, :],
                                                      tsp[0][0][:, :],
                                                      tsp[1][0][:, :],
                                                      op=ALU.add),
                       r=[tsp[0][1], tsp[1][1]], w=[deltaB])
                    op("v", (lambda b=b, ut=ut: lambda v: v.tensor_tensor_scan(
                        r_t[:, :], ones_t[:, :], delta[:, :],
                        r_carry[:, b * 64 + ut:b * 64 + ut + 1],
                        op0=ALU.mult, op1=ALU.add))(),
                       r=[ones_tB, deltaB, r_carryB], w=[r_tB])
                    op("v", (lambda b=b, ut=ut: lambda v: v.tensor_copy(
                        r_carry[:, b * 64 + ut:b * 64 + ut + 1],
                        r_t[:, 511:512]))(), r=[r_tB], w=[r_carryB])
                    dma("q", o_r[b, ut, :, fc * 512:(fc + 1) * 512], r_t[:, :],
                        r=[r_tB])
                    op("v", (lambda ut=ut, uu=uu: lambda v: v.tensor_tensor(
                        p_t[:, :], delta[:, :],
                        uu[:, (ut % 8) * 512:(ut % 8 + 1) * 512],
                        op=ALU.mult))(),
                       r=[deltaB, uuB], w=[p_tB])
                    for n in range(16):
                        da, daB = dA[n % 2]
                        xi, xiB = xin[n % 2]
                        op("s", (lambda ut=ut, n=n, da=da: lambda s:
                                 s.activation(da[:, :], delta[:, :], AF.Exp,
                                              scale=acol[:, ut * 16 + n:
                                                         ut * 16 + n + 1]))(),
                           r=[deltaB, acolB], w=[daB])
                        op("g", (lambda n=n, xi=xi: lambda g: g.tensor_tensor(
                            xi[:, :], p_t[:, :],
                            brep[:, n * 512:(n + 1) * 512], op=ALU.mult))(),
                           r=[p_tB, brepB], w=[xiB])
                        op("v", (lambda b=b, ut=ut, n=n, da=da, xi=xi:
                                 lambda v: v.tensor_tensor_scan(
                                     hall[:, n * 512:(n + 1) * 512],
                                     da[:, :], xi[:, :],
                                     carry_h[:, (b * 64 + ut) * 16 + n:
                                             (b * 64 + ut) * 16 + n + 1],
                                     op0=ALU.mult, op1=ALU.add))(),
                           r=[daB, xiB, carry_hB], w=[hallB])
                        hc, hcB = hct[n % 2]
                        op("v", (lambda n=n, hc=hc: lambda v: v.tensor_tensor(
                            hc[:, :], hall[:, n * 512:(n + 1) * 512],
                            crep[:, n * 512:(n + 1) * 512], op=ALU.mult))(),
                           r=[hallB, crepB], w=[hcB])
                        op("t", (lambda n=n, hc=hc: lambda t: t.matmul(
                            PS[7][:, 0:512], idb[:, :], hc[:, :],
                            start=(n == 0), stop=(n == 15)))(),
                           r=[hcB, idbB], w=[PSB[7]])
                    op("v", (lambda b=b, ut=ut: lambda v: v.tensor_copy(
                        carry_h[:, (b * 64 + ut) * 16:(b * 64 + ut + 1) * 16],
                        hall[:, 511:16 * 512:512]))(),
                       r=[hallB], w=[carry_hB])
                    op("v", (lambda ut=ut, uu=uu: lambda v: v.tensor_scalar(
                        ud_t[:, :], uu[:, (ut % 8) * 512:(ut % 8 + 1) * 512],
                        dpar[:, ut:ut + 1], None, op0=ALU.mult))(),
                       r=[uuB, dparB], w=[ud_tB])
                    op("v", lambda v: v.tensor_tensor(ypre_t[:, :],
                                                      PS[7][:, 0:512],
                                                      ud_t[:, :], op=ALU.add),
                       r=[PSB[7], ud_tB], w=[ypre_tB])
                    dma("q", o_ypre[b, ut, :, fc * 512:(fc + 1) * 512],
                        ypre_t[:, :], r=[ypre_tB])
        for b in range(B):
            for ut in range(64):
                op("v", (lambda b=b, ut=ut: lambda v: v.tensor_scalar(
                    parg[:, ut * 16:(ut + 1) * 16],
                    acol[:, ut * 16:(ut + 1) * 16],
                    r_carry[:, b * 64 + ut:b * 64 + ut + 1], None,
                    op0=ALU.mult))(), r=[acolB, r_carryB], w=[pargB])
            op("s", lambda s: s.activation(parg[:, :], parg[:, :], AF.Exp),
               r=[pargB], w=[pargB])
            dma("q", o_ps[:, b * 1024:(b + 1) * 1024], parg[:, :], r=[pargB])
        st = dma("q", o_ps[:, 2048:4096], carry_h[:, :], r=[carry_hB])
        p._waits("q", [st])
        with nc.Block() as block:
            p.replay(block, sems)
    return nc


def host_l1attn(xn2_list, W, inputs):
    nc = build_l1attn()
    cw = np.asarray(inputs["L1_attn_conv_w"], np.float32)
    convw_col = cw.reshape(64, 128, 4).transpose(1, 0, 2).reshape(128, 256)
    col64 = lambda v: np.ascontiguousarray(
        np.asarray(v, np.float32).reshape(64, 128).T)
    acol = np.ascontiguousarray(np.asarray(
        inputs["L1_attn_A"], np.float32).reshape(64, 128, 16).transpose(
        1, 0, 2).reshape(128, 1024))
    dtlnw = np.ascontiguousarray(np.asarray(
        inputs["L1_attn_dt_ln_w"], np.float32).reshape(2, 128).T)
    bcw = np.ascontiguousarray(np.stack(
        [np.asarray(inputs["L1_attn_b_ln_w"], np.float32),
         np.asarray(inputs["L1_attn_c_ln_w"], np.float32)], axis=1))
    rep_sel = np.concatenate(
        [np.tile(np.eye(16, dtype=BF)[:, n:n + 1], (1, 128))
         for n in range(16)], axis=1)
    in_maps = []
    for k in range(NCORES):
        in_maps.append({
            "xn2": xn2_list[k], "twin4": W["twin"], "twx4": W["twx"],
            "twdt4": W["twdt"], "identb": IDENT_BF,
            "onesrow_f": ONES_ROW_F32, "ones_b": ONES_COL_BF,
            "convw_col": np.ascontiguousarray(convw_col),
            "convb_col": col64(inputs["L1_attn_conv_b"]),
            "dtb_col": col64(inputs["L1_attn_dt_bias"]),
            "dpar_col": col64(inputs["L1_attn_D_param"]),
            "acol": acol, "dtlnw_col": dtlnw, "bcw_col": bcw,
            "rep_sel": np.ascontiguousarray(rep_sel),
        })
    res = run_bass_kernel_spmd(nc, in_maps, core_ids=list(range(NCORES)))
    keys = ["ypre", "r_out", "sg_out", "pso", "cn_out"]
    return [{kk: res.results[k][kk] for kk in keys} for k in range(NCORES)]



def x3col(ot, b, t0=0):
    return ot * (B * TOK) + b * TOK + t0


def build_l1fix():
    nc = bass.Bass("TRN2", target_bir_lowering=False)
    ap_in = {}
    for name, shape, dt in [
        ("pred", [7, 128, 4096], F32),
        ("ypre", [B, 64, 128, 1024], BF16),
        ("r_in", [B, 64, 128, 1024], F32),
        ("sg_in", [B, 64, 128, 1024], BF16),
        ("cn_in", [B, 2, 16, 512], BF16),
        ("xn2", [128, 32 * B * TCOLS], BF16),
        ("twout4", [32, 64, 128, 128], BF16),
        ("acol", [128, 1024], F32),
        ("identb", [128, 128], BF16),
        ("onesrow_f", [65, 128], F32),
        ("onesrow_b", [65, 128], BF16),
        ("ones_b", [128, 1], BF16),
        ("rep_sel", [16, 2048], BF16),
    ]:
        ap_in[name] = nc.dram_tensor(name, shape, dt, kind="ExternalInput").ap()
    xn3_out = nc.dram_tensor("xn3", [128, 32 * B * TOK], BF16,
                             kind="ExternalOutput").ap()
    ybuf3 = nc.dram_tensor("ybuf3", [B, 2, 32, 128, 512], BF16).ap()

    with ExitStack() as stack:
        E = stack.enter_context
        sems = make_sems(stack, nc)
        p = Prog(nc)

        def T(name, shape, dt=F32):
            t = E(nc.sbuf_tensor(name, shape, dt))
            return t, Buf()

        idb, idbB = T("idb", [128, 128], BF16)
        onesrow, onesrowB = T("onesrow", [65, 128])
        onesrob, onesrobB = T("onesrob", [65, 128], BF16)
        onesb, onesbB = T("onesb", [128, 1], BF16)
        acol, acolB = T("acol_sb", [128, 1024])
        repsel, repselB = T("repsel", [16, 2048], BF16)
        hin, hinB = T("hin", [128, 2048])
        predb = [T(f"pred{i}", [128, 4096]) for i in range(2)]
        cn_sb, cn_sbB = T("cn_sb", [16, 512], BF16)
        crep, crepB = T("crep", [128, 16 * 512], BF16)
        y2fc, y2fcB = T("y2fc", [128, 64 * 512], BF16)
        r_t = [T(f"rt{i}", [128, 512]) for i in range(2)]
        et, etB = T("et", [128, 512])
        cct = [T(f"cct{i}", [128, 512], BF16) for i in range(2)]
        ypre_t, ypre_tB = T("ypre_t", [128, 512], BF16)
        sg_tt, sg_ttB = T("sg_tt", [128, 512], BF16)
        yc_t, yc_tB = T("yc_t", [128, 512])
        two_t = [T(f"two{i}", [128, 2048], BF16) for i in range(2)]
        xn2t, xn2tB = T("xn2t", [128, 512], BF16)
        y3t, y3tB = T("y3t", [128, 512], BF16)
        ysq3, ysq3B = T("ysq3", [128, 512], BF16)
        rowS, rowSB = T("rowS", [128, 512])
        mr3, mr3B = T("mr3", [128, 4096], BF16)
        cst, cstB = T("cst", [128, 4])
        t1s, t1sB = T("t1s", [128, 512])
        PS = [psum_bank(stack, nc, f"bank{i}") for i in range(8)]
        PSB = [Buf() for _ in range(8)]

        def op(eng, fn, r=(), w=(), x=()):
            return emit(p, eng, fn, r, w, x)

        def dma(q, out_ap, in_ap, r=(), w=(), x=()):
            return emit_dma(p, q, out_ap, in_ap, r, w, x)

        op("v", lambda v: v.memset(cst[:, 0:1], EPS_LN), w=[cstB])
        op("v", lambda v: v.memset(hin[:, :], 0.0), w=[hinB])
        cstB.w = p.barrier_tok("v")
        for apn, sb, bf in [("identb", idb, idbB), ("onesrow_f", onesrow, onesrowB),
                            ("onesrow_b", onesrob, onesrobB),
                            ("ones_b", onesb, onesbB), ("acol", acol, acolB),
                            ("rep_sel", repsel, repselB)]:
            dma("q", sb[:, :], ap_in[apn], w=[bf])
        for j in range(7):
            pb, pbB = predb[j % 2]
            dma("q", pb[:, :], ap_in["pred"][j], w=[pbB])
            op("v", (lambda pb=pb: lambda v: v.tensor_tensor(
                hin[:, :], hin[:, :], pb[:, 0:2048], op=ALU.mult))(),
               r=[pbB, hinB], w=[hinB])
            op("v", (lambda pb=pb: lambda v: v.tensor_tensor(
                hin[:, :], hin[:, :], pb[:, 2048:4096], op=ALU.add))(),
               r=[pbB, hinB], w=[hinB])
        for b in range(B):
            for fc in range(2):
                dma("q", cn_sb[:, :], ap_in["cn_in"][b, fc], w=[cn_sbB])
                for n in range(16):
                    op("t", (lambda n=n: lambda t: t.matmul(
                        PS[5][:, 0:512], repsel[:, n * 128:(n + 1) * 128],
                        cn_sb[:, :], start=True, stop=True))(),
                       r=[cn_sbB, repselB], w=[PSB[5]])
                    op("s", (lambda n=n: lambda s: s.copy(
                        crep[:, n * 512:(n + 1) * 512], PS[5][:, 0:512]))(),
                       r=[PSB[5]], w=[crepB])
                for ut in range(64):
                    rt, rtB = r_t[ut % 2]
                    dma("g", rt[:, :],
                        ap_in["r_in"][b, ut, :, fc * 512:(fc + 1) * 512],
                        w=[rtB])
                    for n in range(16):
                        op("s", (lambda ut=ut, n=n, rt=rt: lambda s:
                                 s.activation(et[:, :], rt[:, :], AF.Exp,
                                              scale=acol[:, ut * 16 + n:
                                                         ut * 16 + n + 1]))(),
                           r=[rtB, acolB], w=[etB])
                        cc, ccB = cct[n % 2]
                        op("v", (lambda b=b, ut=ut, n=n, cc=cc: lambda v:
                                 v.scalar_tensor_tensor(
                                     cc[:, :], et[:, :],
                                     hin[:, (b * 64 + ut) * 16 + n:
                                         (b * 64 + ut) * 16 + n + 1],
                                     crep[:, n * 512:(n + 1) * 512],
                                     ALU.mult, ALU.mult))(),
                           r=[etB, hinB, crepB], w=[ccB])
                        op("t", (lambda n=n, cc=cc: lambda t: t.matmul(
                            PS[7][:, 0:512], idb[:, :], cc[:, :],
                            start=(n == 0), stop=(n == 15)))(),
                           r=[ccB, idbB], w=[PSB[7]])
                    dma("s", ypre_t[:, :],
                        ap_in["ypre"][b, ut, :, fc * 512:(fc + 1) * 512],
                        w=[ypre_tB])
                    dma("q", sg_tt[:, :],
                        ap_in["sg_in"][b, ut, :, fc * 512:(fc + 1) * 512],
                        w=[sg_ttB])
                    op("v", lambda v: v.tensor_tensor(
                        yc_t[:, :], PS[7][:, 0:512], ypre_t[:, :], op=ALU.add),
                       r=[PSB[7], ypre_tB], w=[yc_tB])
                    op("v", (lambda ut=ut: lambda v: v.tensor_tensor(
                        y2fc[:, ut * 512:(ut + 1) * 512], yc_t[:, :],
                        sg_tt[:, :], op=ALU.mult))(),
                       r=[yc_tB, sg_ttB], w=[y2fcB])
                for ot in range(32):
                    def ldq(q4, ot=ot):
                        tw, twB = two_t[q4 % 2]
                        dma("s", tw.ap().rearrange("p (k c) -> p k c", c=128),
                            ap_in["twout4"][ot, q4 * 16:(q4 + 1) * 16].rearrange(
                                "k p c -> p k c"), w=[twB])
                    ldq(0)
                    ldq(1)
                    for kt in range(64):
                        if kt % 16 == 0 and kt >= 16 and kt // 16 + 1 < 4:
                            ldq(kt // 16 + 1)
                        tw, twB = two_t[(kt // 16) % 2]
                        op("t", (lambda kt=kt, tw=tw: lambda t: t.matmul(
                            PS[6][:, 0:512],
                            tw[:, (kt % 16) * 128:(kt % 16 + 1) * 128],
                            y2fc[:, kt * 512:(kt + 1) * 512],
                            start=(kt == 0), stop=(kt == 63)))(),
                           r=[twB, y2fcB], w=[PSB[6]])
                    dma("q", xn2t[:, :],
                        ap_in["xn2"][:, xncol(ot, b, HALO + fc * 512):
                                     xncol(ot, b, HALO + fc * 512) + 512],
                        w=[xn2tB])
                    op("v", lambda v: v.tensor_tensor(
                        y3t[:, :], PS[6][:, 0:512], xn2t[:, :], op=ALU.add),
                       r=[PSB[6], xn2tB], w=[y3tB])
                    op("s", lambda s: s.activation(ysq3[:, :], y3t[:, :],
                                                   AF.Square),
                       r=[y3tB], w=[ysq3B])
                    op("t", (lambda ot=ot: lambda t: t.matmul(
                        PS[4][0:1, 0:512], onesb[:, 0:1], y3t[:, :],
                        start=(ot == 0), stop=False))(),
                       r=[y3tB, onesbB], w=[PSB[4]])
                    op("t", (lambda ot=ot: lambda t: t.matmul(
                        PS[4][32:33, 0:512], onesb[:, 0:1], ysq3[:, :],
                        start=(ot == 0), stop=(ot == 31)))(),
                       r=[ysq3B, onesbB], w=[PSB[4]])
                    dma("q", ybuf3[b, fc, ot], y3t[:, :], r=[y3tB])
                cidx = b * 2 + fc
                op("v", (lambda: lambda v: v.tensor_scalar(
                    rowS[0:1, 0:512], PS[4][0:1, 0:512], 1.0 / DM, None,
                    op0=ALU.mult))(), r=[PSB[4]], w=[rowSB])
                op("v", (lambda cidx=cidx: lambda v: v.tensor_copy(
                    mr3[0:1, cidx * 512:(cidx + 1) * 512],
                    rowS[0:1, 0:512]))(), r=[rowSB], w=[mr3B])
                op("v", (lambda: lambda v: v.tensor_scalar(
                    t1s[0:1, 0:512], PS[4][32:33, 0:512], 1.0 / DM, None,
                    op0=ALU.mult))(), r=[PSB[4]], w=[t1sB])
                op("v", lambda v: v.tensor_tensor(
                    rowS[0:1, 0:512], rowS[0:1, 0:512], rowS[0:1, 0:512],
                    op=ALU.mult), r=[rowSB], w=[rowSB])
                op("v", lambda v: v.tensor_tensor(
                    rowS[0:1, 0:512], t1s[0:1, 0:512], rowS[0:1, 0:512],
                    op=ALU.subtract), r=[rowSB, t1sB], w=[rowSB])
                op("s", lambda s: s.activation(rowS[0:1, 0:512],
                                               rowS[0:1, 0:512], AF.Sqrt,
                                               bias=cst[0:1, 0:1]),
                   r=[rowSB, cstB], w=[rowSB])
                op("v", lambda v: v.reciprocal(rowS[0:1, 0:512],
                                               rowS[0:1, 0:512]),
                   r=[rowSB], w=[rowSB])
                op("v", (lambda cidx=cidx: lambda v: v.tensor_copy(
                    mr3[0:1, 2048 + cidx * 512:2048 + (cidx + 1) * 512],
                    rowS[0:1, 0:512]))(), r=[rowSB], w=[mr3B])
        for b in range(B):
            for fc in range(2):
                cidx = b * 2 + fc
                op("t", (lambda cidx=cidx: lambda t: t.matmul(
                    PS[0][:, 0:512], onesrob[0:1, :],
                    mr3[0:1, cidx * 512:(cidx + 1) * 512],
                    start=True, stop=True))(),
                   r=[mr3B, onesrobB], w=[PSB[0]])
                op("t", (lambda cidx=cidx: lambda t: t.matmul(
                    PS[1][:, 0:512], onesrob[0:1, :],
                    mr3[0:1, 2048 + cidx * 512:2048 + (cidx + 1) * 512],
                    start=True, stop=True))(),
                   r=[mr3B, onesrobB], w=[PSB[1]])
                for ot in range(32):
                    dma("g", y3t[:, :], ybuf3[b, fc, ot], w=[y3tB],
                        x=[("dq", p.dcnt["q"])])
                    op("v", lambda v: v.tensor_tensor(
                        t1s[:, :], y3t[:, :], PS[0][:, 0:512],
                        op=ALU.subtract), r=[y3tB, PSB[0]], w=[t1sB])
                    op("v", (lambda b=b, fc=fc, ot=ot: lambda v:
                             v.tensor_tensor(
                                 ysq3[:, :], t1s[:, :], PS[1][:, 0:512],
                                 op=ALU.mult))(),
                       r=[t1sB, PSB[1]], w=[ysq3B])
                    dma("q", xn3_out[:, x3col(ot, b, fc * 512):
                                     x3col(ot, b, fc * 512) + 512],
                        ysq3[:, :], r=[ysq3B])
        p._waits("q", [("dq", p.dcnt["q"])])
        with nc.Block() as block:
            p.replay(block, sems)
    return nc


def host_l1fix(l3outs, xn2_list, W, inputs):
    nc = build_l1fix()
    acol = np.ascontiguousarray(np.asarray(
        inputs["L1_attn_A"], np.float32).reshape(64, 128, 16).transpose(
        1, 0, 2).reshape(128, 1024))
    rep_sel = np.concatenate(
        [np.tile(np.eye(16, dtype=BF)[:, n:n + 1], (1, 128))
         for n in range(16)], axis=1)
    neutral = np.concatenate([np.ones((128, 2048), np.float32),
                              np.zeros((128, 2048), np.float32)], axis=1)
    in_maps = []
    for k in range(NCORES):
        preds = [neutral] * (7 - k) + [l3outs[j]["pso"] for j in range(k)]
        in_maps.append({
            "pred": np.stack(preds), "ypre": l3outs[k]["ypre"],
            "r_in": l3outs[k]["r_out"], "sg_in": l3outs[k]["sg_out"],
            "cn_in": l3outs[k]["cn_out"], "xn2": xn2_list[k],
            "twout4": W["twout"], "acol": acol, "identb": IDENT_BF,
            "onesrow_f": ONES_ROW_F32, "onesrow_b": ONES_ROW_BF,
            "ones_b": ONES_COL_BF, "rep_sel": np.ascontiguousarray(rep_sel),
        })
    res = run_bass_kernel_spmd(nc, in_maps, core_ids=list(range(NCORES)))
    return [res.results[k]["xn3"] for k in range(NCORES)]



def build_l1ffn_a():
    nc = bass.Bass("TRN2", target_bir_lowering=False)
    ap_in = {}
    for name, shape, dt in [
        ("xn3", [128, 32 * B * TOK], BF16),
        ("tl14", [32, 32, 128, 128], BF16),
        ("a_raw", [4096, 32], F32),
        ("bl1_col", [128, 32], F32),
        ("tlim", [32, 8], BF16),
        ("identf", [128, 128], F32),
    ]:
        ap_in[name] = nc.dram_tensor(name, shape, dt, kind="ExternalInput").ap()
    o_h = nc.dram_tensor("hout", [32, 128, B * TOK], BF16,
                         kind="ExternalOutput").ap()
    o_z = nc.dram_tensor("zout", [32, B * TOK], F32,
                         kind="ExternalOutput").ap()
    o_m = nc.dram_tensor("pmax", [1, 2], F32, kind="ExternalOutput").ap()

    with ExitStack() as stack:
        E = stack.enter_context
        sems = make_sems(stack, nc)
        p = Prog(nc)

        def T(name, shape, dt=F32):
            t = E(nc.sbuf_tensor(name, shape, dt))
            return t, Buf()

        idf, idfB = T("idf", [128, 128])
        azsb, azsbB = T("azsb", [128, 32 * 32], BF16)
        bl1, bl1B = T("bl1", [128, 32])
        tlim_sb, tlimB = T("tlim_sb", [32, 8], BF16)
        xres, xresB = T("xres", [128, 32 * B * TOK], BF16)
        zsb, zsbB = T("zsb", [32, B * TOK])
        zodd, zoddB = T("zodd", [32, B * TOK // 2], BF16)
        tlh = [T(f"tlh{i}", [128, 16 * 128], BF16) for i in range(2)]
        ht_t, ht_tB = T("ht_t", [128, 512], BF16)
        hm8, hm8B = T("hm8", [8, 4])
        dm8, dm8B = T("dm8", [8, 4])
        rowm, rowmB = T("rowm", [1, 16])
        PS = [psum_bank(stack, nc, f"bank{i}") for i in range(8)]
        PSB = [Buf() for _ in range(8)]

        def op(eng, fn, r=(), w=(), x=()):
            return emit(p, eng, fn, r, w, x)

        def dma(q, out_ap, in_ap, r=(), w=(), x=()):
            return emit_dma(p, q, out_ap, in_ap, r, w, x)

        dma("q", idf[:, :], ap_in["identf"], w=[idfB])
        dma("q", bl1[:, :], ap_in["bl1_col"], w=[bl1B])
        dma("q", tlim_sb[:, :], ap_in["tlim"], w=[tlimB])
        dma("q", xres[:, :], ap_in["xn3"], w=[xresB])
        for kt in range(32):
            dma("g", azsb[:, kt * 32:(kt + 1) * 32],
                ap_in["a_raw"][kt * 128:(kt + 1) * 128, :], w=[azsbB])
        op("v", lambda v: v.memset(hm8[:, :], 0.0), w=[hm8B])
        op("v", lambda v: v.memset(dm8[:, :], 0.0), w=[dm8B])
        for ci in range(4):
            b, fc = ci // 2, ci % 2
            for kt in range(32):
                op("t", (lambda kt=kt, ci=ci, b=b, fc=fc: lambda t: t.matmul(
                    PS[ci][0:32, 0:512], azsb[:, kt * 32:(kt + 1) * 32],
                    xres[:, x3col(kt, b, fc * 512):x3col(kt, b, fc * 512) + 512],
                    start=(kt == 0), stop=(kt == 31)))(),
                   r=[azsbB, xresB], w=[PSB[ci]])
            op("s", (lambda ci=ci, b=b, fc=fc: lambda s: s.copy(
                zsb[:, b * TOK + fc * 512:b * TOK + fc * 512 + 512],
                PS[ci][0:32, 0:512]))(), r=[PSB[ci]], w=[zsbB])
        dma("q", o_z, zsb[:, :], r=[zsbB])
        op("v", lambda v: v.tensor_copy(zodd[:, :], zsb[:, 1:B * TOK:2]),
           r=[zsbB], w=[zoddB])
        for ci in range(4):
            op("t", (lambda ci=ci: lambda t: t.matmul(
                PS[4][0:8, 0:256], tlim_sb[:, :],
                zodd[:, ci * 256:(ci + 1) * 256], start=True, stop=True))(),
               r=[zoddB, tlimB], w=[PSB[4]])
            op("v", (lambda ci=ci: lambda v: v.tensor_reduce(
                dm8[0:8, 1:2], PS[4][0:8, 0:256], mybir.AxisListType.X,
                ALU.max, apply_absolute_value=True))(),
               r=[PSB[4]], w=[dm8B])
            op("v", lambda v: v.tensor_tensor(dm8[0:8, 0:1], dm8[0:8, 1:2],
                                              dm8[0:8, 0:1], op=ALU.max),
               r=[dm8B], w=[dm8B])
        for ot in range(32):
            for h in range(2):
                tw, twB = tlh[h]
                dma("s", tw.ap().rearrange("p (k c) -> p k c", c=128),
                    ap_in["tl14"][ot, h * 16:(h + 1) * 16].rearrange(
                        "k p c -> p k c"), w=[twB])
            for ci in range(4):
                b, fc = ci // 2, ci % 2
                for kt in range(32):
                    tw, twB = tlh[kt // 16]
                    op("t", (lambda kt=kt, ci=ci, b=b, fc=fc, tw=tw:
                             lambda t: t.matmul(
                                 PS[ci][:, 0:512],
                                 tw[:, (kt % 16) * 128:(kt % 16 + 1) * 128],
                                 xres[:, x3col(kt, b, fc * 512):
                                      x3col(kt, b, fc * 512) + 512],
                                 start=(kt == 0), stop=(kt == 31)))(),
                       r=[twB, xresB], w=[PSB[ci]])
                op("s", (lambda ci=ci, ot=ot: lambda s: s.activation(
                    ht_t[:, :], PS[ci][:, 0:512], AF.Identity,
                    bias=bl1[:, ot:ot + 1]))(),
                   r=[PSB[ci], bl1B], w=[ht_tB])
                dma("q", o_h[ot, :, (2 * b + fc) * 512:(2 * b + fc + 1) * 512],
                    ht_t[:, :], r=[ht_tB])
                if ot == 0:
                    op("v", (lambda: lambda v: v.tensor_reduce(
                        hm8[0:8, 1:2], ht_t[0:8, 1:512:2],
                        mybir.AxisListType.X, ALU.max,
                        apply_absolute_value=True))(),
                       r=[ht_tB], w=[hm8B])
                    op("v", lambda v: v.tensor_tensor(
                        hm8[0:8, 0:1], hm8[0:8, 1:2], hm8[0:8, 0:1],
                        op=ALU.max), r=[hm8B], w=[hm8B])
        for (src8, srcB, col) in [(hm8, hm8B, 0), (dm8, dm8B, 1)]:
            op("t", (lambda src8=src8: lambda t: t.transpose(
                PS[5][0:4, 0:8], src8[0:8, 0:4], idf[0:8, 0:8]))(),
               r=[srcB, idfB], w=[PSB[5]])
            op("v", (lambda col=col: lambda v: v.tensor_reduce(
                rowm[0:1, col:col + 1], PS[5][0:1, 0:8],
                mybir.AxisListType.X, ALU.max))(),
               r=[PSB[5]], w=[rowmB])
        st = dma("q", o_m, rowm[0:1, 0:2], r=[rowmB])
        p._waits("q", [("dq", p.dcnt["q"])])
        with nc.Block() as block:
            p.replay(block, sems)
    return nc


def build_l1ffn_b():
    nc = bass.Bass("TRN2", target_bir_lowering=False)
    ap_in = {}
    for name, shape, dt in [
        ("xn3", [128, 32 * B * TOK], BF16),
        ("h_in", [32, 128, B * TOK], BF16),
        ("z_in", [32, B * TOK], F32),
        ("pmaxall", [NCORES, 1, 2], F32),
        ("tlim", [32, 8], BF16),
        ("limes_raw", [8, 32], F32),
        ("bmat_raw", [32, 4096], F32),
        ("identf", [128, 128], F32),
        ("onesrow_f", [65, 128], F32),
        ("onesrow_b", [65, 128], BF16),
        ("ones_b", [128, 1], BF16),
    ]:
        ap_in[name] = nc.dram_tensor(name, shape, dt, kind="ExternalInput").ap()
    o_f = nc.dram_tensor("outf", [B, TOK, DM], F32, kind="ExternalOutput").ap()
    ybuf4 = nc.dram_tensor("ybuf4", [B, 2, 32, 128, 512], BF16).ap()

    with ExitStack() as stack:
        E = stack.enter_context
        sems = make_sems(stack, nc)
        p = Prog(nc)

        def T(name, shape, dt=F32):
            t = E(nc.sbuf_tensor(name, shape, dt))
            return t, Buf()

        idf, idfB = T("idf", [128, 128])
        onesrow, onesrowB = T("onesrow", [65, 128])
        onesrob, onesrobB = T("onesrob", [65, 128], BF16)
        onesb, onesbB = T("onesb", [128, 1], BF16)
        tlim_sb, tlimB = T("tlim_sb", [32, 8], BF16)
        limn, limnB = T("limn", [8, 32], BF16)
        bmat_sb, bmatB = T("bmat_sb", [32, 4096], BF16)
        xres, xresB = T("xres", [128, 32 * B * TOK], BF16)
        zsb, zsbB = T("zsb", [32, B * TOK])
        zodd, zoddB = T("zodd", [32, B * TOK // 2], BF16)
        mrow, mrowB = T("mrow", [1, 16])
        a8, a8B = T("a8", [8, 4])
        hs8, hs8B = T("hs8", [8, 512], BF16)
        lg, lgB = T("lg", [8, 256])
        gtm, gtmB = T("gtm", [128, 8])
        rsc, rscB = T("rsc", [128, 48])
        rcol, rcolB = T("rcol", [128, 8])
        wfull, wfullB = T("wfull", [8, 512], BF16)
        zp, zpB = T("zp", [32, 512], BF16)
        ht_t, ht_tB = T("ht_t", [128, 512], BF16)
        xt_t, xt_tB = T("xt_t", [128, 512], BF16)
        y4t, y4tB = T("y4t", [128, 512], BF16)
        ysq4, ysq4B = T("ysq4", [128, 512], BF16)
        rowS, rowSB = T("rowS", [128, 512])
        t1s, t1sB = T("t1s", [128, 512])
        xo_t, xo_tB = T("xo_t", [128, 512])
        otm, otmB = T("otm", [128, 128])
        mr4, mr4B = T("mr4", [128, 4096], BF16)
        cst, cstB = T("cst", [128, 4])
        PS = [psum_bank(stack, nc, f"bank{i}") for i in range(8)]
        PSB = [Buf() for _ in range(8)]

        def op(eng, fn, r=(), w=(), x=()):
            return emit(p, eng, fn, r, w, x)

        def dma(q, out_ap, in_ap, r=(), w=(), x=()):
            return emit_dma(p, q, out_ap, in_ap, r, w, x)

        op("v", lambda v: v.memset(cst[:, 0:1], EPS_LN), w=[cstB])
        op("v", lambda v: v.memset(cst[:, 1:2], 2.0), r=[cstB])
        cstB.w = p.barrier_tok("v")
        for apn, sb, bf in [("identf", idf, idfB), ("onesrow_f", onesrow, onesrowB),
                            ("onesrow_b", onesrob, onesrobB),
                            ("ones_b", onesb, onesbB), ("tlim", tlim_sb, tlimB)]:
            dma("q", sb[:, :], ap_in[apn], w=[bf])
        dma("g", limn[:, :], ap_in["limes_raw"], w=[limnB])
        dma("g", bmat_sb[:, :], ap_in["bmat_raw"], w=[bmatB])
        dma("q", xres[:, :], ap_in["xn3"], w=[xresB])
        dma("q", zsb[:, :], ap_in["z_in"], w=[zsbB])
        dma("q", mrow[0:1, :], ap_in["pmaxall"].rearrange("c one q -> one (c q)"),
            w=[mrowB])
        op("v", lambda v: v.tensor_reduce(mrow[0:1, 2:3], mrow[0:1, 0:16:2],
                                          mybir.AxisListType.X, ALU.max),
           r=[mrowB], w=[mrowB])
        op("v", lambda v: v.tensor_reduce(mrow[0:1, 3:4], mrow[0:1, 1:16:2],
                                          mybir.AxisListType.X, ALU.max),
           r=[mrowB], w=[mrowB])
        op("v", lambda v: v.tensor_scalar(mrow[0:1, 2:4], mrow[0:1, 2:4],
                                          1e-6, None, op0=ALU.max),
           r=[mrowB], w=[mrowB])
        op("v", lambda v: v.reciprocal(mrow[0:1, 4:6], mrow[0:1, 2:4]),
           r=[mrowB], w=[mrowB])
        op("v", lambda v: v.tensor_scalar(mrow[0:1, 4:6], mrow[0:1, 4:6],
                                          0.5, None, op0=ALU.mult),
           r=[mrowB], w=[mrowB])
        for j in range(2):
            op("t", (lambda j=j: lambda t: t.matmul(
                PS[5][0:8, 0:1], onesrow[0:1, 0:8], mrow[0:1, 4 + j:5 + j],
                start=True, stop=True))(), r=[mrowB, onesrowB], w=[PSB[5]])
            op("s", (lambda j=j: lambda s: s.copy(a8[0:8, j:j + 1],
                                                  PS[5][0:8, 0:1]))(),
               r=[PSB[5]], w=[a8B])
        op("v", lambda v: v.tensor_copy(zodd[:, :], zsb[:, 1:B * TOK:2]),
           r=[zsbB], w=[zoddB])
        for b in range(B):
            for fc in range(2):
                ci = 2 * b + fc
                dma("s", hs8[0:8, :], ap_in["h_in"][0, 0:8,
                                                    ci * 512:(ci + 1) * 512],
                    w=[hs8B])
                op("t", (lambda ci=ci: lambda t: t.matmul(
                    PS[4][0:8, 0:256], tlim_sb[:, :],
                    zodd[:, ci * 256:(ci + 1) * 256], start=True,
                    stop=True))(), r=[zoddB, tlimB], w=[PSB[4]])
                op("v", (lambda ci=ci: lambda v: v.tensor_scalar(
                    lg[0:8, 0:256], hs8[0:8, 1:512:2], a8[0:8, 0:1], None,
                    op0=ALU.mult))(), r=[hs8B, a8B], w=[lgB])
                op("v", (lambda ci=ci: lambda v: v.scalar_tensor_tensor(
                    lg[0:8, 0:256], PS[4][0:8, 0:256], a8[0:8, 1:2],
                    lg[0:8, 0:256], ALU.mult, ALU.add))(),
                   r=[PSB[4], a8B, lgB], w=[lgB])
                for blk in range(2):
                    op("t", (lambda blk=blk: lambda t: t.transpose(
                        PS[5][0:128, 0:8], lg[0:8, blk * 128:(blk + 1) * 128],
                        idf[0:8, 0:8]))(), r=[lgB, idfB], w=[PSB[5]])
                    op("s", (lambda: lambda s: s.copy(
                        gtm[:, 0:8], PS[5][0:128, 0:8]))(),
                       r=[PSB[5]], w=[gtmB])
                    op("v", lambda v: v.tensor_reduce(
                        rcol[:, 0:1], gtm[:, 0:8], mybir.AxisListType.X,
                        ALU.max, negate=True), r=[gtmB], w=[rcolB])
                    op("s", lambda s: s.activation(
                        rsc[:, 0:8], gtm[:, 0:8], AF.Exp,
                        bias=rcol[:, 0:1]), r=[gtmB, rcolB], w=[rscB])
                    op("v", lambda v: v.tensor_reduce(
                        rcol[:, 1:2], rsc[:, 0:8], mybir.AxisListType.X,
                        ALU.add), r=[rscB], w=[rcolB])
                    op("v", lambda v: v.reciprocal(rcol[:, 2:3], rcol[:, 1:2]),
                       r=[rcolB], w=[rcolB])
                    op("v", lambda v: v.tensor_scalar(
                        rsc[:, 8:16], rsc[:, 0:8], rcol[:, 2:3], None,
                        op0=ALU.mult), r=[rscB, rcolB], w=[rscB])
                    op("v", lambda v: v.tensor_reduce(
                        rcol[:, 3:4], rsc[:, 8:16], mybir.AxisListType.X,
                        ALU.max), r=[rscB], w=[rcolB])
                    op("v", lambda v: v.tensor_scalar(
                        rsc[:, 16:24], rsc[:, 8:16], rcol[:, 3:4], None,
                        op0=ALU.is_ge), r=[rscB, rcolB], w=[rscB])
                    op("v", lambda v: v.scalar_tensor_tensor(
                        rsc[:, 24:32], rsc[:, 16:24], -2.0, rsc[:, 8:16],
                        ALU.mult, ALU.add), r=[rscB], w=[rscB])
                    op("v", lambda v: v.tensor_reduce(
                        rcol[:, 4:5], rsc[:, 24:32], mybir.AxisListType.X,
                        ALU.max), r=[rscB], w=[rcolB])
                    op("v", lambda v: v.tensor_scalar(
                        rsc[:, 32:40], rsc[:, 8:16], rcol[:, 4:5], None,
                        op0=ALU.subtract), r=[rscB, rcolB], w=[rscB])
                    op("s", lambda s: s.activation(
                        rsc[:, 32:40], rsc[:, 32:40], AF.Sigmoid,
                        scale=cst[:, 1:2]), r=[rscB, cstB], w=[rscB])
                    op("v", lambda v: v.tensor_tensor(
                        rsc[:, 40:48], rsc[:, 8:16], rsc[:, 32:40],
                        op=ALU.mult), r=[rscB], w=[rscB])
                    op("v", lambda v: v.tensor_reduce(
                        rcol[:, 5:6], rsc[:, 40:48], mybir.AxisListType.X,
                        ALU.add), r=[rscB], w=[rcolB])
                    op("v", lambda v: v.tensor_scalar(
                        rcol[:, 6:7], rcol[:, 5:6], 1e-9, None, op0=ALU.add),
                       r=[rcolB], w=[rcolB])
                    op("v", lambda v: v.reciprocal(rcol[:, 7:8], rcol[:, 6:7]),
                       r=[rcolB], w=[rcolB])
                    op("v", lambda v: v.tensor_scalar(
                        rsc[:, 40:48], rsc[:, 40:48], rcol[:, 7:8], None,
                        op0=ALU.mult), r=[rscB, rcolB], w=[rscB])
                    op("t", (lambda: lambda t: t.transpose(
                        PS[5][0:8, 0:128], rsc[:, 40:48], idf[:, :]))(),
                       r=[rscB, idfB], w=[PSB[5]])
                    for par in range(2):
                        op("s", (lambda blk=blk, par=par: lambda s: s.copy(
                            wfull[0:8, blk * 256 + par:blk * 256 + 256:2],
                            PS[5][0:8, 0:128]))(), r=[PSB[5]], w=[wfullB])
                op("t", (lambda ci=ci: lambda t: t.matmul(
                    PS[4][0:32, 0:512], limn[0:8, :], wfull[0:8, :],
                    start=True, stop=True))(), r=[wfullB, limnB], w=[PSB[4]])
                op("v", (lambda ci=ci: lambda v: v.scalar_tensor_tensor(
                    zp[:, :], zsb[:, ci * 512:(ci + 1) * 512], 0.5,
                    PS[4][0:32, 0:512], ALU.mult, ALU.mult))(),
                   r=[zsbB, PSB[4]], w=[zpB])
                for ot in range(32):
                    op("t", (lambda ot=ot: lambda t: t.matmul(
                        PS[6][:, 0:512], bmat_sb[:, ot * 128:(ot + 1) * 128],
                        zp[:, :], start=True, stop=True))(),
                       r=[zpB, bmatB], w=[PSB[6]])
                    dma("s", ht_t[:, :],
                        ap_in["h_in"][ot, :, ci * 512:(ci + 1) * 512],
                        w=[ht_tB])
                    dma("q", xt_t[:, :],
                        ap_in["xn3"][:, x3col(ot, b, fc * 512):
                                     x3col(ot, b, fc * 512) + 512],
                        w=[xt_tB])
                    op("v", lambda v: v.tensor_tensor(
                        t1s[:, :], PS[6][:, 0:512], ht_t[:, :], op=ALU.add),
                       r=[PSB[6], ht_tB], w=[t1sB])
                    op("v", lambda v: v.tensor_tensor(
                        y4t[:, :], t1s[:, :], xt_t[:, :], op=ALU.add),
                       r=[t1sB, xt_tB], w=[y4tB])
                    op("s", lambda s: s.activation(ysq4[:, :], y4t[:, :],
                                                   AF.Square),
                       r=[y4tB], w=[ysq4B])
                    op("t", (lambda ot=ot: lambda t: t.matmul(
                        PS[7][0:1, 0:512], onesb[:, 0:1], y4t[:, :],
                        start=(ot == 0), stop=False))(),
                       r=[y4tB, onesbB], w=[PSB[7]])
                    op("t", (lambda ot=ot: lambda t: t.matmul(
                        PS[7][32:33, 0:512], onesb[:, 0:1], ysq4[:, :],
                        start=(ot == 0), stop=(ot == 31)))(),
                       r=[ysq4B, onesbB], w=[PSB[7]])
                    dma("q", ybuf4[b, fc, ot], y4t[:, :], r=[y4tB])
                op("v", (lambda: lambda v: v.tensor_scalar(
                    rowS[0:1, 0:512], PS[7][0:1, 0:512], 1.0 / DM, None,
                    op0=ALU.mult))(), r=[PSB[7]], w=[rowSB])
                op("v", (lambda ci=ci: lambda v: v.tensor_copy(
                    mr4[0:1, ci * 512:(ci + 1) * 512], rowS[0:1, 0:512]))(),
                   r=[rowSB], w=[mr4B])
                op("v", (lambda: lambda v: v.tensor_scalar(
                    t1s[0:1, 0:512], PS[7][32:33, 0:512], 1.0 / DM, None,
                    op0=ALU.mult))(), r=[PSB[7]], w=[t1sB])
                op("v", lambda v: v.tensor_tensor(
                    rowS[0:1, 0:512], rowS[0:1, 0:512], rowS[0:1, 0:512],
                    op=ALU.mult), r=[rowSB], w=[rowSB])
                op("v", lambda v: v.tensor_tensor(
                    rowS[0:1, 0:512], t1s[0:1, 0:512], rowS[0:1, 0:512],
                    op=ALU.subtract), r=[rowSB, t1sB], w=[rowSB])
                op("s", lambda s: s.activation(rowS[0:1, 0:512],
                                               rowS[0:1, 0:512], AF.Sqrt,
                                               bias=cst[0:1, 0:1]),
                   r=[rowSB, cstB], w=[rowSB])
                op("v", lambda v: v.reciprocal(rowS[0:1, 0:512],
                                               rowS[0:1, 0:512]),
                   r=[rowSB], w=[rowSB])
                op("v", (lambda ci=ci: lambda v: v.tensor_copy(
                    mr4[0:1, 2048 + ci * 512:2048 + (ci + 1) * 512],
                    rowS[0:1, 0:512]))(), r=[rowSB], w=[mr4B])
        for b in range(B):
            for fc in range(2):
                ci = 2 * b + fc
                op("t", (lambda ci=ci: lambda t: t.matmul(
                    PS[0][:, 0:512], onesrob[0:1, :],
                    mr4[0:1, ci * 512:(ci + 1) * 512],
                    start=True, stop=True))(), r=[mr4B, onesrobB], w=[PSB[0]])
                op("t", (lambda ci=ci: lambda t: t.matmul(
                    PS[1][:, 0:512], onesrob[0:1, :],
                    mr4[0:1, 2048 + ci * 512:2048 + (ci + 1) * 512],
                    start=True, stop=True))(), r=[mr4B, onesrobB], w=[PSB[1]])
                for ot in range(32):
                    dma("g", y4t[:, :], ybuf4[b, fc, ot], w=[y4tB],
                        x=[("dq", p.dcnt["q"])])
                    op("v", lambda v: v.tensor_tensor(
                        t1s[:, :], y4t[:, :], PS[0][:, 0:512],
                        op=ALU.subtract), r=[y4tB, PSB[0]], w=[t1sB])
                    op("v", lambda v: v.tensor_tensor(
                        xo_t[:, :], t1s[:, :], PS[1][:, 0:512],
                        op=ALU.mult), r=[t1sB, PSB[1]], w=[xo_tB])
                    for blk in range(4):
                        op("t", (lambda blk=blk: lambda t: t.transpose(
                            PS[2 + (blk % 2)][:, 0:128],
                            xo_t[:, blk * 128:(blk + 1) * 128],
                            idf[:, :]))(), r=[xo_tB, idfB],
                           w=[PSB[2 + (blk % 2)]])
                        op("s", (lambda blk=blk: lambda s: s.copy(
                            otm[:, :], PS[2 + (blk % 2)][:, 0:128]))(),
                           r=[PSB[2 + (blk % 2)]], w=[otmB])
                        dma("q", o_f[b, fc * 512 + blk * 128:
                                     fc * 512 + (blk + 1) * 128,
                                     ot * 128:(ot + 1) * 128],
                            otm[:, :], r=[otmB])
        p._waits("q", [("dq", p.dcnt["q"])])
        with nc.Block() as block:
            p.replay(block, sems)
    return nc


def host_l1ffn(xn3_list, W, inputs):
    nc_a = build_l1ffn_a()
    bl1_col = np.ascontiguousarray(
        np.asarray(inputs["L1_ffn_b"], np.float32).reshape(32, 128).T)
    in_maps = [{"xn3": xn3_list[k], "tl14": W["tl1"],
                "a_raw": np.ascontiguousarray(np.asarray(inputs["L1_ffn_A"], np.float32)),
                "bl1_col": bl1_col, "tlim": W["tlim"], "identf": IDENT_F32}
               for k in range(NCORES)]
    res = run_bass_kernel_spmd(nc_a, in_maps, core_ids=list(range(NCORES)))
    aouts = [{kk: res.results[k][kk] for kk in ["hout", "zout", "pmax"]}
             for k in range(NCORES)]
    pmaxall = np.stack([a["pmax"] for a in aouts])
    nc_b = build_l1ffn_b()
    in_maps = [{"xn3": xn3_list[k], "h_in": aouts[k]["hout"],
                "z_in": aouts[k]["zout"], "pmaxall": pmaxall,
                "tlim": W["tlim"],
                "limes_raw": np.ascontiguousarray(np.asarray(inputs["L1_ffn_LiMEs"], np.float32)),
                "bmat_raw": np.ascontiguousarray(np.asarray(inputs["L1_ffn_Bmat"], np.float32)),
                "identf": IDENT_F32, "onesrow_f": ONES_ROW_F32,
                "onesrow_b": ONES_ROW_BF, "ones_b": ONES_COL_BF}
               for k in range(NCORES)]
    res = run_bass_kernel_spmd(nc_b, in_maps, core_ids=list(range(NCORES)))
    return [res.results[k]["outf"] for k in range(NCORES)]



def x_slices(x):
    out = []
    for k in range(NCORES):
        g0 = TOK * k - HALO
        if g0 < 0:
            sl = np.concatenate(
                [np.zeros((B, -g0, DM), np.float32), x[:, 0:TOK * (k + 1)]],
                axis=1)
        else:
            sl = x[:, g0:TOK * (k + 1)]
        out.append(np.ascontiguousarray(sl))
    return out


ONES_ROW_F32 = np.ones((65, 128), dtype=np.float32)
ONES_ROW_BF = np.ones((65, 128), dtype=BF)
ONES_COL_F32 = np.ones((128, 1), dtype=np.float32)
ONES_COL_BF = np.ones((128, 1), dtype=BF)
REP8 = np.repeat(np.eye(8, dtype=BF), 16, axis=1).reshape(8, 128)


def host_scores(xsl):
    nc = build_scores()
    in_maps = [{"xs": xsl[k], "identf": IDENT_F32} for k in range(NCORES)]
    res = run_bass_kernel_spmd(nc, in_maps, core_ids=list(range(NCORES)))
    return np.stack([res.results[k]["psc"] for k in range(NCORES)])


def host_l0block(xsl, pscall, W, inputs):
    nc = build_l0block()
    bbase_col = np.ascontiguousarray(
        np.asarray(inputs["L0_ffn_b_base"], np.float32).reshape(32, 128).T)
    in_maps = []
    for k in range(NCORES):
        in_maps.append({
            "xs": xsl[k], "pscall": pscall, "tbase4": W["tbase"],
            "ta04": W["ta0"], "tb04": W["tb0"], "twg": W["twg"],
            "bbase_col": bbase_col, "identf": IDENT_F32,
            "onesrow_f": ONES_ROW_F32, "onesrow_b": ONES_ROW_BF,
            "ones_b": ONES_COL_BF, "rep8": REP8,
        })
    res = run_bass_kernel_spmd(nc, in_maps, core_ids=list(range(NCORES)))
    return [res.results[k]["xn2"] for k in range(NCORES)]


def kernel(**inputs):
    x = np.ascontiguousarray(np.asarray(inputs["L0_attn_x"], np.float32))
    W = host_prep(inputs)
    xsl = x_slices(x)
    psc = host_scores(xsl)
    xn2_list = host_l0block(xsl, psc, W, inputs)
    l3 = host_l1attn(xn2_list, W, inputs)
    xn3 = host_l1fix(l3, xn2_list, W, inputs)
    outs = host_l1ffn(xn3, W, inputs)
    out = np.empty((B, SEQ, DM), np.float32)
    for k in range(NCORES):
        out[:, TOK * k:TOK * (k + 1)] = outs[k]
    return out


# version 03 (final)
# speedup vs baseline: 1.1056x; 1.1056x over previous
import numpy as np
import ml_dtypes
from contextlib import ExitStack

import concourse.bass as bass
import concourse.mybir as mybir
from concourse.bass_utils import run_bass_kernel_spmd

F32 = mybir.dt.float32
BF16 = mybir.dt.bfloat16
AF = mybir.ActivationFunctionType
ALU = mybir.AluOpType

NCORES = 8
B = 2
SEQ = 8192
DM = 4096
DSSM = 8192
NST = 16
RDT = 256
TOK = SEQ // NCORES
HALO = 3
TCOLS = TOK + HALO
EPS_LN = 1e-5
EPS_RMS = 1e-6
BF = ml_dtypes.bfloat16


class Prog:
    CENG = ("v", "s", "t", "g", "q")

    def __init__(self, nc):
        self.nc = nc
        self.queues = {e: [] for e in self.CENG}
        self.cnt = {e: 0 for e in self.CENG}
        self.dcnt = {e: 0 for e in self.CENG}
        self.waited = {e: {} for e in self.CENG}

    def _waits(self, e, deps):
        for tok in deps:
            if tok is None:
                continue
            key, val = tok
            if key == e:
                pass
            w = self.waited[e]
            if w.get(key, -1) < val:
                w[key] = val
                self.queues[e].append(("wait", key, val))

    def op(self, e, fn, deps=()):
        self._waits(e, deps)
        self.cnt[e] += 1
        self.queues[e].append(("op", fn))
        return (e, self.cnt[e])

    def dma(self, e, out, in_, deps=()):
        if self.dcnt[e] > 0:
            self._waits(e, [("d" + e, self.dcnt[e])])
        self._waits(e, deps)
        self.dcnt[e] += 16
        self.queues[e].append(("dma", out, in_))
        return ("d" + e, self.dcnt[e])

    def dma_t(self, e, out, in_, deps=()):
        if self.dcnt[e] > 0:
            self._waits(e, [("d" + e, self.dcnt[e])])
        self._waits(e, deps)
        self.dcnt[e] += 16
        self.queues[e].append(("dmat", out, in_))
        return ("d" + e, self.dcnt[e])

    def barrier_tok(self, e):
        return (e, self.cnt[e])

    def all_toks(self):
        toks = [(e, self.cnt[e]) for e in self.CENG if self.cnt[e] > 0]
        toks += [("d" + e, self.dcnt[e]) for e in self.CENG if self.dcnt[e] > 0]
        return toks

    def replay(self, block, sems):
        def body_for(e):
            def body(eng):
                for item in self.queues[e]:
                    kind = item[0]
                    if kind == "wait":
                        eng.wait_ge(sems[item[1]], item[2])
                    elif kind == "op":
                        instr = item[1](eng)
                        instr.then_inc(sems[e], 1)
                    elif kind == "dma":
                        eng.dma_start(out=item[1], in_=item[2]).then_inc(
                            sems["d" + e], 16)
                    elif kind == "dmat":
                        eng.dma_start_transpose(item[1], item[2]).then_inc(
                            sems["d" + e], 16)
            return body
        block.vector(body_for("v"))
        block.scalar(body_for("s"))
        block.tensor(body_for("t"))
        block.gpsimd(body_for("g"))
        block.sync(body_for("q"))


def psum_bank(stack, nc, name, dtype=F32):
    n = 512 if dtype == F32 else 1024
    return stack.enter_context(nc.psum_tensor(name, [128, n], dtype))


class Buf:
    __slots__ = ("w", "r")

    def __init__(self):
        self.w = None
        self.r = []


def emit(p, eng, fn, reads=(), writes=(), extra=()):
    deps = [t for t in extra if t is not None]
    for b in reads:
        if b.w is not None:
            deps.append(b.w)
    for b in writes:
        deps.extend(b.r)
        if b.w is not None:
            deps.append(b.w)
    tok = p.op(eng, fn, deps=deps)
    for b in reads:
        b.r.append(tok)
    for b in writes:
        b.w = tok
        b.r = []
    return tok


def emit_dma(p, q, out_ap, in_ap, reads=(), writes=(), extra=()):
    deps = [t for t in extra if t is not None]
    for b in reads:
        if b.w is not None:
            deps.append(b.w)
    for b in writes:
        deps.extend(b.r)
        if b.w is not None:
            deps.append(b.w)
    tok = p.dma(q, out_ap, in_ap, deps=deps)
    for b in reads:
        b.r.append(tok)
    for b in writes:
        b.w = tok
        b.r = []
    return tok


def make_sems(stack, nc):
    sems = {}
    for e in Prog.CENG:
        sems[e] = stack.enter_context(nc.semaphore("sem_" + e))
        sems["d" + e] = stack.enter_context(nc.semaphore("sem_d" + e))
    return sems


def run_prog(nc, prog, in_maps, core_ids=None, **kw):
    raise NotImplementedError



IDENT_BF = np.eye(128, dtype=BF)
IDENT_F32 = np.eye(128, dtype=np.float32)
ONES_COL = np.ones((128, 1), dtype=BF)



def emit_transpose_bf16(p, sb, src_rows, n_rows, n_cols, ident_bf, out_cb,
                        dep_extra=()):
    raise NotImplementedError



def _ceil(a, b):
    return -(-a // b)


def build_prep():
    nc = bass.Bass("TRN2", target_bir_lowering=False)
    ins = {
        "wbase_s": (512, 4096), "wl1_s": (512, 4096), "win_s": (2048, 4096),
        "wout_s": (512, 8192), "wdt_s": (1024, 256), "wx_full": (288, 8192),
        "a0_full": (128, 4096), "wg_full": (8, 4096), "lim_full": (8, 32),
    }
    iaps = {k: nc.dram_tensor(k, list(v), F32, kind="ExternalInput").ap()
            for k, v in ins.items()}
    iaps["b0_full"] = nc.dram_tensor("b0_full", [8, 4096, 16], F32,
                                     kind="ExternalInput").ap()
    iaps["identb"] = nc.dram_tensor("identb", [128, 128], BF16,
                                    kind="ExternalInput").ap()
    outs = {
        "tbase": (4, 32, 128, 128), "tl1": (4, 32, 128, 128),
        "twin": (16, 32, 128, 128), "twout": (4, 64, 128, 128),
        "twdt": (8, 2, 128, 128), "twx": (3, 64, 128, 128),
        "ta0": (1, 32, 128, 128), "tb0": (32, 1, 128, 128),
        "twg": (4096, 8), "tlim": (32, 8),
    }
    oaps = {k: nc.dram_tensor(k, list(v), BF16, kind="ExternalOutput").ap()
            for k, v in outs.items()}

    with ExitStack() as stack:
        E = stack.enter_context
        sems = make_sems(stack, nc)
        p = Prog(nc)
        ident = E(nc.sbuf_tensor("ident", [128, 128], BF16))
        tok_id = p.dma("q", ident[:, :], iaps["identb"])

        stage = E(nc.sbuf_tensor("stage", [128, 16 * 4096], BF16))
        otile = [E(nc.sbuf_tensor(f"otile{i}", [128, 4096], BF16))
                 for i in range(2)]
        zpad = E(nc.sbuf_tensor("zpad", [128, 96], BF16))
        tok_z = p.op("v", lambda v: v.memset(zpad[:, :], 0.0))
        ps_tr = [psum_bank(stack, nc, f"ps_tr{i}", BF16) for i in range(4)]

        state = {"ps_free": [None] * 4, "ob_free": [None, None], "ps_i": 0}

        def transpose_job(src, R, C, dst, dst_is_tiled=False, row0=0,
                          pad_to=None):
            n_rt = _ceil(R, 128)
            n_ct = _ceil(C, 128)
            assert n_rt * C <= 16 * 4096 and R <= 4096
            stage_free = [p.barrier_tok("t")]
            row_aps = []
            ld_toks = []
            for rt in range(n_rt):
                rp = min(128, R - rt * 128)
                ap = stage[:, rt * C:(rt + 1) * C]
                ld_toks.append(p.dma("g", ap[0:rp, :],
                                     src[rt * 128:rt * 128 + rp, :],
                                     deps=stage_free))
                row_aps.append((ap, rp))
            for ct in range(n_ct):
                cp = min(128, C - ct * 128)
                ob = otile[ct % 2]
                evac_toks = []
                for rt in range(n_rt):
                    ap, rp = row_aps[rt]
                    pi = state["ps_i"] % 4
                    ps = ps_tr[pi]
                    def mk_tt(ps=ps, ap=ap, rp=rp, cp=cp, ct=ct):
                        def f(t):
                            return t.transpose(ps[0:cp, 0:rp],
                                               ap[0:rp, ct * 128:ct * 128 + cp],
                                               ident[0:rp, 0:rp])
                        return f
                    tt = p.op("t", mk_tt(),
                              deps=[ld_toks[rt], tok_id, state["ps_free"][pi]])
                    def mk_ev(ps=ps, ob=ob, rt=rt, rp=rp, cp=cp):
                        def f(s):
                            return s.copy(ob[0:cp, rt * 128:rt * 128 + rp],
                                          ps[0:cp, 0:rp])
                        return f
                    ev = p.op("s", mk_ev(),
                              deps=[tt, state["ob_free"][ct % 2]])
                    state["ps_free"][pi] = ev
                    evac_toks.append(ev)
                    state["ps_i"] += 1
                R_st = R
                if pad_to is not None:
                    mz = p.op("v", (lambda ob=ob, cp=cp: lambda v: v.memset(
                        ob[0:cp, R:pad_to], 0.0))(),
                        deps=[state["ob_free"][ct % 2]])
                    evac_toks.append(mz)
                    R_st = pad_to
                if dst_is_tiled:
                    dap = dst[:, ct, row0:row0 + cp, :].rearrange(
                        "o p c -> p o c")
                    sap = ob[0:cp, 0:R_st].rearrange("p (o c) -> p o c", c=128)
                else:
                    dap = dst[ct * 128:ct * 128 + cp, 0:R_st]
                    sap = ob[0:cp, 0:R_st]
                st = p.dma("q", dap, sap, deps=evac_toks)
                state["ob_free"][ct % 2] = st
        transpose_job(iaps["wbase_s"], 512, 4096, oaps["tbase"], True)
        transpose_job(iaps["wl1_s"], 512, 4096, oaps["tl1"], True)
        transpose_job(iaps["win_s"], 2048, 4096, oaps["twin"], True)
        transpose_job(iaps["wout_s"], 512, 8192, oaps["twout"], True)
        transpose_job(iaps["wdt_s"], 1024, 256, oaps["twdt"], True)
        transpose_job(iaps["wx_full"], 288, 8192, oaps["twx"], True,
                      pad_to=384)
        transpose_job(iaps["a0_full"], 128, 4096, oaps["ta0"], True)
        b0 = iaps["b0_full"]
        for e in range(8):
            transpose_job(b0[e, :, :], 4096, 16, oaps["tb0"], True,
                          row0=16 * e)
        transpose_job(iaps["wg_full"], 8, 4096, oaps["twg"])
        transpose_job(iaps["lim_full"], 8, 32, oaps["tlim"])

        with nc.Block() as block:
            p.replay(block, sems)
    return nc


def host_prep(inputs):
    nc = build_prep()
    in_maps = []
    for k in range(NCORES):
        m = {
            "wbase_s": np.ascontiguousarray(inputs["L0_ffn_W_base"][512 * k:512 * (k + 1)]),
            "wl1_s": np.ascontiguousarray(inputs["L1_ffn_W"][512 * k:512 * (k + 1)]),
            "win_s": np.ascontiguousarray(inputs["L1_attn_W_in"][2048 * k:2048 * (k + 1)]),
            "wout_s": np.ascontiguousarray(inputs["L1_attn_W_out"][512 * k:512 * (k + 1)]),
            "wdt_s": np.ascontiguousarray(inputs["L1_attn_Wdt"][1024 * k:1024 * (k + 1)]),
            "wx_full": np.ascontiguousarray(inputs["L1_attn_Wx"]),
            "a0_full": np.ascontiguousarray(inputs["L0_ffn_A"].reshape(128, 4096)),
            "b0_full": np.ascontiguousarray(inputs["L0_ffn_B"]),
            "wg_full": np.ascontiguousarray(inputs["L0_ffn_W_gate"]),
            "lim_full": np.ascontiguousarray(inputs["L1_ffn_LiMEs"]),
            "identb": IDENT_BF,
        }
        in_maps.append(m)
    res = run_bass_kernel_spmd(nc, in_maps, core_ids=list(range(NCORES)))
    r = res.results
    W = {}
    for k in ["tbase", "tl1", "twin", "twout", "twdt"]:
        W[k] = np.concatenate([r[c][k] for c in range(NCORES)], axis=0)
    for k in ["twx", "ta0", "tb0", "twg", "tlim"]:
        W[k] = r[0][k]
    return W



PAIRS = [(0, 0), (0, 1), (1, 1)]


def build_scores():
    nc = bass.Bass("TRN2", target_bir_lowering=False)
    xs = nc.dram_tensor("xs", [B, TCOLS, DM], F32, kind="ExternalInput").ap()
    identf = nc.dram_tensor("identf", [128, 128], F32, kind="ExternalInput").ap()
    psc = nc.dram_tensor("psc", [128, 96], F32, kind="ExternalOutput").ap()

    with ExitStack() as stack:
        E = stack.enter_context
        sems = make_sems(stack, nc)
        p = Prog(nc)
        idf = E(nc.sbuf_tensor("idf", [128, 128], F32))
        xtm = [E(nc.sbuf_tensor(f"xtm{b}", [128, DM], F32)) for b in range(B)]
        xf = [E(nc.sbuf_tensor(f"xf{b}", [128, DM], F32)) for b in range(B)]
        scr = E(nc.sbuf_tensor("scr", [128, 512], F32))
        part = E(nc.sbuf_tensor("part", [128, 96], F32))
        acc = E(nc.sbuf_tensor("acc", [128, 96], F32))
        pool = [psum_bank(stack, nc, f"pool{i}") for i in range(4)]

        idfB = Buf()
        xtmB = [Buf() for _ in range(B)]
        xfB = [[Buf() for _ in range(32)] for _ in range(B)]
        partB = [Buf() for _ in range(96)]
        accB = [Buf() for _ in range(96)]
        poolB = [Buf() for _ in range(4)]
        emit_dma(p, "q", idf[:, :], identf, writes=[idfB])
        emit(p, "v", lambda v: v.memset(acc[:, :], 0.0),
             writes=accB)
        pi = 0
        for tt in range(8):
            for b in range(B):
                emit_dma(p, "q", xtm[b][:, :],
                         xs[b, HALO + tt * 128:HALO + (tt + 1) * 128, :],
                         writes=[xtmB[b]])
            for b in range(B):
                for lt in range(32):
                    bank = pi % 4
                    tr = emit(p, "t", (lambda b=b, lt=lt, bank=bank:
                                       lambda t: t.transpose(
                                           pool[bank][:, 0:128],
                                           xtm[b][:, lt * 128:(lt + 1) * 128],
                                           idf[:, :]))(),
                              reads=[xtmB[b], idfB], writes=[poolB[bank]])
                    emit(p, "s", (lambda b=b, lt=lt, bank=bank: lambda s:
                                  s.copy(xf[b][:, lt * 128:(lt + 1) * 128],
                                         pool[bank][:, 0:128]))(),
                         reads=[poolB[bank]], writes=[xfB[b][lt]])
                    pi += 1
            scrB = [Buf() for _ in range(4)]
            for pr, (a, b2) in enumerate(PAIRS):
                for lt in range(32):
                    col = lt * 3 + pr
                    si = col % 4
                    emit(p, "v", (lambda a=a, b2=b2, lt=lt, si=si:
                                  lambda v: v.tensor_tensor(
                                      scr[:, si * 128:(si + 1) * 128],
                                      xf[a][:, lt * 128:(lt + 1) * 128],
                                      xf[b2][:, lt * 128:(lt + 1) * 128],
                                      op=ALU.mult))(),
                         reads=[xfB[a][lt], xfB[b2][lt]], writes=[scrB[si]])
                    emit(p, "v", (lambda col=col, si=si: lambda v:
                                  v.tensor_reduce(
                                      part[:, col:col + 1],
                                      scr[:, si * 128:(si + 1) * 128],
                                      mybir.AxisListType.X, ALU.add))(),
                         reads=[scrB[si]], writes=[partB[col]])
                    emit(p, "v", (lambda col=col: lambda v: v.tensor_tensor(
                        acc[:, col:col + 1], acc[:, col:col + 1],
                        part[:, col:col + 1], op=ALU.add))(),
                         reads=[partB[col]], writes=[accB[col]])
        st = p.dma("q", psc, acc[:, :], deps=[b.w for b in accB])
        p._waits("q", [st])
        with nc.Block() as block:
            p.replay(block, sems)
    return nc



def xncol(lt, b, t0=0):
    return lt * (B * TCOLS) + b * TCOLS + t0


CHUNKS = [(0, 512), (512, 512), (1024, 3)]
BLOCKS = [(0, 3)] + [(3 + 128 * i, 128) for i in range(8)]


def build_l0block():
    nc = bass.Bass("TRN2", target_bir_lowering=False)
    ap_in = {}
    for name, shape, dt in [
        ("xs", [B, TCOLS, DM], F32), ("pscall", [NCORES, 128, 96], F32),
        ("tbase4", [32, 32, 128, 128], BF16), ("ta04", [1, 32, 128, 128], BF16),
        ("tb04", [32, 1, 128, 128], BF16), ("twg", [4096, 8], BF16),
        ("bbase_col", [128, 32], F32), ("identf", [128, 128], F32),
        ("onesrow_f", [65, 128], F32), ("onesrow_b", [65, 128], BF16),
        ("ones_b", [128, 1], BF16), ("rep8", [8, 128], BF16),
    ]:
        ap_in[name] = nc.dram_tensor(name, shape, dt, kind="ExternalInput").ap()
    xn2_out = nc.dram_tensor("xn2", [128, 32 * B * TCOLS], BF16,
                             kind="ExternalOutput").ap()

    ybuf = nc.dram_tensor("ybuf", [B, 32, 128, TCOLS], BF16).ap()

    with ExitStack() as stack:
        E = stack.enter_context
        sems = make_sems(stack, nc)
        p = Prog(nc)

        def T(name, shape, dt=F32):
            t = E(nc.sbuf_tensor(name, shape, dt))
            return t, Buf()

        idf, idfB = T("idf", [128, 128])
        onesrob, onesrobB = T("onesrob", [65, 128], BF16)
        onesrow, onesrowB = T("onesrow", [65, 128])
        onesb, onesbB = T("onesb", [128, 1], BF16)
        rep8s, rep8B = T("rep8s", [8, 128], BF16)
        bbase, bbaseB = T("bbase", [128, 32])
        twg_sb, twgB = T("twg_sb", [128, 32 * 8], BF16)
        ta0_sb, ta0B = T("ta0_sb", [128, 32 * 128], BF16)
        tb0_sb, tb0B = T("tb0_sb", [128, 32 * 128], BF16)
        xres, xresB = T("xres", [128, 32 * B * TCOLS], BF16)
        xtm0, xtm0B = T("xtm0", [128, DM])
        xtm1, xtm1B = T("xtm1", [128, DM])
        xtmT = [xtm0, xtm1]; xtmB = [xtm0B, xtm1B]
        ybt = [xtm0.ap().bitcast(BF16), xtm1.ap().bitcast(BF16)]
        t1s, t1sB = T("t1s", [128, 512])
        t2s, t2sB = T("t2s", [128, 512])
        pall, pallB = T("pall", [128, 96])
        stot, stotB = T("stot", [128, 96])
        cdif, cdifB = T("cdif", [128, 64])
        coef, coefB = T("coef", [128, 4 * 32])
        glog, glogB = T("glog", [8, TCOLS])
        gtm, gtmB = T("gtm", [128, 9 * 8])
        rsc, rscB = T("rsc", [128, 48])
        rcol, rcolB = T("rcol", [128, 8])
        route, routeB = T("route", [8, TCOLS], BF16)
        rrep, rrepB = T("rrep", [128, TCOLS], BF16)
        hw, hwB = T("hw", [128, TCOLS], BF16)
        y2sb, y2sbB = T("y2sb", [128, TCOLS], BF16)
        ysq2, ysq2B = T("ysq2", [128, 512], BF16)
        tbb = [T(f"tbb{i}", [128, 8 * 128], BF16) for i in range(2)]
        cst, cstB = T("cst", [128, 4])
        PS = [psum_bank(stack, nc, f"bank{i}") for i in range(8)]
        PSB = [Buf() for _ in range(8)]

        def op(eng, fn, r=(), w=(), x=()):
            return emit(p, eng, fn, r, w, x)

        def dma(q, out_ap, in_ap, r=(), w=(), x=()):
            return emit_dma(p, q, out_ap, in_ap, r, w, x)

        op("v", lambda v: v.memset(cst[:, 0:1], float(1.0 / np.sqrt(8192.0))),
           w=[cstB])
        op("v", lambda v: v.memset(cst[:, 1:2], float(-1.0 / np.sqrt(8192.0))),
           r=[cstB])
        op("v", lambda v: v.memset(cst[:, 2:3], EPS_LN), r=[cstB])
        cstB.w = p.barrier_tok("v")

        dma("q", idf[:, :], ap_in["identf"], w=[idfB])
        dma("q", onesrow[:, :], ap_in["onesrow_f"], w=[onesrowB])
        dma("q", onesrob[:, :], ap_in["onesrow_b"], w=[onesrobB])
        dma("q", onesb[:, :], ap_in["ones_b"], w=[onesbB])
        dma("q", rep8s[:, :], ap_in["rep8"], w=[rep8B])
        dma("q", bbase[:, :], ap_in["bbase_col"], w=[bbaseB])
        dma("q", ta0_sb.ap().rearrange("p (k c) -> p k c", c=128),
            ap_in["ta04"][0].rearrange("k p c -> p k c"), w=[ta0B])
        for o in range(32):
            tok = dma("q", tb0_sb[:, o * 128:(o + 1) * 128], ap_in["tb04"][o, 0])
        tb0B.w = tok
        for kt in range(32):
            tok = dma("q", twg_sb[:, kt * 8:(kt + 1) * 8],
                      ap_in["twg"][kt * 128:(kt + 1) * 128, :])
        twgB.w = tok

        dma("q", pall[:, :], ap_in["pscall"][0], w=[pallB])
        op("v", lambda v: v.tensor_copy(stot[:, :], pall[:, :]),
           r=[pallB], w=[stotB])
        for c in range(1, NCORES):
            dma("q", pall[:, :], ap_in["pscall"][c], w=[pallB])
            op("v", (lambda c=c: lambda v: v.tensor_tensor(
                stot[:, :], stot[:, :], pall[:, :],
                op=ALU.add))(), r=[pallB, stotB], w=[stotB])
        op("v", lambda v: v.tensor_tensor(
            cdif[:, 0:32], stot[:, 0:96:3], stot[:, 1:96:3],
            op=ALU.subtract), r=[stotB], w=[cdifB])
        op("v", lambda v: v.tensor_tensor(
            cdif[:, 32:64], stot[:, 2:96:3], stot[:, 1:96:3],
            op=ALU.subtract), r=[stotB], w=[cdifB])
        op("s", lambda s: s.activation(coef[:, 0:32], cdif[:, 0:32],
                                       AF.Sigmoid, scale=cst[:, 0:1]),
           r=[cdifB, cstB], w=[coefB])
        op("s", lambda s: s.activation(coef[:, 32:64], cdif[:, 0:32],
                                       AF.Sigmoid, scale=cst[:, 1:2]),
           r=[cdifB, cstB], w=[coefB])
        op("s", lambda s: s.activation(coef[:, 64:96], cdif[:, 32:64],
                                       AF.Sigmoid, scale=cst[:, 1:2]),
           r=[cdifB, cstB], w=[coefB])
        op("s", lambda s: s.activation(coef[:, 96:128], cdif[:, 32:64],
                                       AF.Sigmoid, scale=cst[:, 0:1]),
           r=[cdifB, cstB], w=[coefB])
        op("v", lambda v: v.tensor_scalar(coef[:, 0:32], coef[:, 0:32],
                                          1.0, None, op0=ALU.add),
           r=[coefB], w=[coefB])
        op("v", lambda v: v.tensor_scalar(coef[:, 96:128], coef[:, 96:128],
                                          1.0, None, op0=ALU.add),
           r=[coefB], w=[coefB])

        for blk, (t0, tp) in enumerate(BLOCKS):
            for b in range(B):
                dma("q", xtmT[b][0:tp, :], ap_in["xs"][b, t0:t0 + tp, :],
                    w=[xtmB[b]])
            mv_done = []
            for lt in range(32):
                for b in range(B):
                    op("t", (lambda b=b, lt=lt, tp=tp: lambda t: t.transpose(
                        PS[b][0:128, 0:tp],
                        xtmT[b][0:tp, lt * 128:(lt + 1) * 128],
                        idf[0:tp, 0:tp]))(),
                       r=[xtmB[b], idfB], w=[PSB[b]])
                op("v", (lambda lt=lt, tp=tp: lambda v: v.tensor_scalar(
                    t1s[:, 0:tp], PS[1][:, 0:tp], coef[:, 32 + lt:33 + lt],
                    None, op0=ALU.mult))(), r=[PSB[1], coefB], w=[t1sB])
                y0 = op("v", (lambda lt=lt, tp=tp: lambda v:
                              v.scalar_tensor_tensor(
                                  ybt[0][0:128, lt * 128:lt * 128 + tp],
                                  PS[0][:, 0:tp], coef[:, lt:lt + 1],
                                  t1s[:, 0:tp], ALU.mult, ALU.add))(),
                        r=[PSB[0], coefB, t1sB], w=[xtm0B])
                op("v", (lambda lt=lt, tp=tp: lambda v: v.tensor_scalar(
                    t2s[:, 0:tp], PS[0][:, 0:tp], coef[:, 64 + lt:65 + lt],
                    None, op0=ALU.mult))(), r=[PSB[0], coefB], w=[t2sB])
                y1 = op("v", (lambda lt=lt, tp=tp: lambda v:
                              v.scalar_tensor_tensor(
                                  ybt[1][0:128, lt * 128:lt * 128 + tp],
                                  PS[1][:, 0:tp], coef[:, 96 + lt:97 + lt],
                                  t2s[:, 0:tp], ALU.mult, ALU.add))(),
                        r=[PSB[1], coefB, t2sB], w=[xtm1B])
                for b in range(B):
                    op("s", (lambda b=b, lt=lt, tp=tp: lambda s: s.activation(
                        ysq2[:, b * 128:b * 128 + tp],
                        ybt[b][0:128, lt * 128:lt * 128 + tp],
                        AF.Square))(), r=[xtmB[b]], w=[ysq2B])
                    op("t", (lambda b=b, lt=lt, tp=tp: lambda t: t.matmul(
                        PS[2 + 2 * b][0:1, 0:tp], onesb[:, 0:1],
                        ybt[b][0:128, lt * 128:lt * 128 + tp],
                        start=(lt == 0), stop=(lt == 31)))(),
                       r=[xtmB[b], onesbB], w=[PSB[2 + 2 * b]])
                    tokmv = op("t", (lambda b=b, lt=lt, tp=tp: lambda t:
                                     t.matmul(
                                         PS[3 + 2 * b][0:1, 0:tp],
                                         onesb[:, 0:1],
                                         ysq2[:, b * 128:b * 128 + tp],
                                         start=(lt == 0), stop=(lt == 31)))(),
                               r=[ysq2B, onesbB], w=[PSB[3 + 2 * b]])
                    if lt == 31:
                        mv_done.append(tokmv)
            for b in range(B):
                op("v", (lambda b=b, tp=tp: lambda v: v.tensor_scalar(
                    xtm1[0:1, 2560:2560 + tp], PS[2 + 2 * b][0:1, 0:tp], 1.0 / DM,
                    None, op0=ALU.mult))(), r=[PSB[2 + 2 * b]], w=[xtmB[1]])
                op("t", (lambda tp=tp: lambda t: t.matmul(
                    PS[6][:, 0:tp], onesrow[0:1, :], xtm1[0:1, 2560:2560 + tp],
                    start=True, stop=True))(), r=[xtmB[1], onesrowB], w=[PSB[6]])
                op("v", (lambda b=b, tp=tp: lambda v: v.tensor_scalar(
                    xtm1[0:1, 3072:3072 + tp], PS[3 + 2 * b][0:1, 0:tp],
                    1.0 / DM, None, op0=ALU.mult))(),
                   r=[PSB[3 + 2 * b]], w=[xtmB[1]])
                op("v", (lambda tp=tp: lambda v: v.tensor_tensor(
                    xtm1[0:1, 3584:3584 + tp], xtm1[0:1, 2560:2560 + tp], xtm1[0:1, 2560:2560 + tp],
                    op=ALU.mult))(), r=[xtmB[1]], w=[xtmB[1]])
                op("v", (lambda tp=tp: lambda v: v.tensor_tensor(
                    xtm1[0:1, 3584:3584 + tp], xtm1[0:1, 3072:3072 + tp], xtm1[0:1, 3584:3584 + tp],
                    op=ALU.subtract))(), r=[xtmB[1]], w=[xtmB[1]])
                op("s", (lambda tp=tp: lambda s: s.activation(
                    xtm1[0:1, 3072:3072 + tp], xtm1[0:1, 3584:3584 + tp],
                    AF.Sqrt, bias=cst[0:1, 2:3]))(), r=[xtmB[1], cstB], w=[xtmB[1]])
                op("v", (lambda tp=tp: lambda v: v.reciprocal(
                    xtm1[0:1, 3584:3584 + tp], xtm1[0:1, 3072:3072 + tp]))(),
                   r=[xtmB[1]], w=[xtmB[1]])
                op("t", (lambda tp=tp: lambda t: t.matmul(
                    PS[7][:, 0:tp], onesrow[0:1, :], xtm1[0:1, 3584:3584 + tp],
                    start=True, stop=True))(), r=[xtmB[1], onesrowB], w=[PSB[7]])
                for lt in range(32):
                    op("v", (lambda b=b, lt=lt, tp=tp: lambda v:
                             v.tensor_tensor(
                                 t1s[:, 0:tp],
                                 ybt[b][0:128, lt * 128:lt * 128 + tp],
                                 PS[6][:, 0:tp], op=ALU.subtract))(),
                       r=[xtmB[b], PSB[6]], w=[t1sB])
                    op("v", (lambda b=b, lt=lt, tp=tp, t0=t0: lambda v:
                             v.tensor_tensor(
                                 xres[:, xncol(lt, b, t0):xncol(lt, b, t0) + tp],
                                 t1s[:, 0:tp], PS[7][:, 0:tp],
                                 op=ALU.mult))(),
                       r=[t1sB, PSB[7]], w=[xresB])

        for bsel in range(B):
            for kt in range(32):
                for ci, (c0, cw) in enumerate(CHUNKS):
                    op("t", (lambda kt=kt, ci=ci, c0=c0, cw=cw, bsel=bsel:
                             lambda t: t.matmul(
                                 PS[ci][0:8, 0:cw],
                                 twg_sb[:, kt * 8:(kt + 1) * 8],
                                 xres[:, xncol(kt, bsel, c0):xncol(kt, bsel, c0) + cw],
                                 start=(kt == 0), stop=(kt == 31)))(),
                       r=[twgB, xresB], w=[PSB[ci]])
            for ci, (c0, cw) in enumerate(CHUNKS):
                op("s", (lambda ci=ci, c0=c0, cw=cw: lambda s: s.copy(
                    glog[0:8, c0:c0 + cw], PS[ci][0:8, 0:cw]))(),
                   r=[PSB[ci]], w=[glogB])
            for blk, (t0, tp) in enumerate(BLOCKS):
                op("t", (lambda t0=t0, tp=tp: lambda t: t.transpose(
                    PS[6][0:tp, 0:8], glog[0:8, t0:t0 + tp],
                    idf[0:8, 0:8]))(), r=[glogB, idfB], w=[PSB[6]])
                op("s", (lambda blk=blk, tp=tp: lambda s: s.copy(
                    gtm[0:tp, 0:8], PS[6][0:tp, 0:8]))(),
                   r=[PSB[6]], w=[gtmB])
                op("v", (lambda tp=tp: lambda v: v.tensor_reduce(
                    rcol[0:tp, 0:1], gtm[0:tp, 0:8], mybir.AxisListType.X,
                    ALU.max, negate=True))(), r=[gtmB], w=[rcolB])
                op("s", (lambda tp=tp: lambda s: s.activation(
                    rsc[0:tp, 0:8], gtm[0:tp, 0:8], AF.Exp,
                    bias=rcol[0:tp, 0:1]))(), r=[gtmB, rcolB], w=[rscB])
                op("v", (lambda tp=tp: lambda v: v.tensor_reduce(
                    rcol[0:tp, 1:2], rsc[0:tp, 0:8], mybir.AxisListType.X,
                    ALU.add))(), r=[rscB], w=[rcolB])
                op("v", (lambda tp=tp: lambda v: v.reciprocal(
                    rcol[0:tp, 2:3], rcol[0:tp, 1:2]))(), r=[rcolB], w=[rcolB])
                op("v", (lambda tp=tp: lambda v: v.tensor_scalar(
                    rsc[0:tp, 8:16], rsc[0:tp, 0:8], rcol[0:tp, 2:3], None,
                    op0=ALU.mult))(), r=[rscB, rcolB], w=[rscB])
                op("v", (lambda tp=tp: lambda v: v.tensor_reduce(
                    rcol[0:tp, 3:4], rsc[0:tp, 8:16], mybir.AxisListType.X,
                    ALU.max))(), r=[rscB], w=[rcolB])
                op("v", (lambda tp=tp: lambda v: v.tensor_scalar(
                    rsc[0:tp, 16:24], rsc[0:tp, 8:16], rcol[0:tp, 3:4], None,
                    op0=ALU.is_ge))(), r=[rscB, rcolB], w=[rscB])
                op("v", (lambda tp=tp: lambda v: v.scalar_tensor_tensor(
                    rsc[0:tp, 24:32], rsc[0:tp, 16:24], -2.0,
                    rsc[0:tp, 8:16], ALU.mult, ALU.add))(), r=[rscB], w=[rscB])
                op("v", (lambda tp=tp: lambda v: v.tensor_reduce(
                    rcol[0:tp, 4:5], rsc[0:tp, 24:32], mybir.AxisListType.X,
                    ALU.max))(), r=[rscB], w=[rcolB])
                op("v", (lambda tp=tp: lambda v: v.tensor_tensor(
                    rcol[0:tp, 5:6], rcol[0:tp, 3:4], rcol[0:tp, 4:5],
                    op=ALU.add))(), r=[rcolB], w=[rcolB])
                op("v", (lambda tp=tp: lambda v: v.reciprocal(
                    rcol[0:tp, 6:7], rcol[0:tp, 5:6]))(), r=[rcolB], w=[rcolB])
                op("v", (lambda tp=tp: lambda v: v.tensor_scalar(
                    rsc[0:tp, 32:40], rsc[0:tp, 8:16], rcol[0:tp, 4:5], None,
                    op0=ALU.is_ge))(), r=[rscB, rcolB], w=[rscB])
                op("v", (lambda tp=tp: lambda v: v.scalar_tensor_tensor(
                    rsc[0:tp, 40:48], rsc[0:tp, 8:16], rcol[0:tp, 6:7],
                    rsc[0:tp, 32:40], ALU.mult, ALU.mult))(),
                   r=[rscB, rcolB], w=[rscB])
                op("t", (lambda tp=tp: lambda t: t.transpose(
                    PS[6][0:8, 0:tp], rsc[0:tp, 40:48], idf[0:tp, 0:tp]))(),
                   r=[rscB, idfB], w=[PSB[6]])
                op("s", (lambda t0=t0, tp=tp: lambda s: s.copy(
                    route[0:8, t0:t0 + tp], PS[6][0:8, 0:tp]))(),
                   r=[PSB[6]], w=[routeB])
            for ci, (c0, cw) in enumerate(CHUNKS):
                op("t", (lambda c0=c0, cw=cw: lambda t: t.matmul(
                    PS[6][:, 0:cw], rep8s[0:8, :], route[0:8, c0:c0 + cw],
                    start=True, stop=True))(), r=[routeB, rep8B], w=[PSB[6]])
                op("s", (lambda c0=c0, cw=cw: lambda s: s.copy(
                    rrep[:, c0:c0 + cw], PS[6][:, 0:cw]))(),
                   r=[PSB[6]], w=[rrepB])
            for ci, (c0, cw) in enumerate(CHUNKS):
                for kt in range(32):
                    op("t", (lambda kt=kt, ci=ci, c0=c0, cw=cw, bsel=bsel:
                             lambda t: t.matmul(
                                 PS[ci][:, 0:cw],
                                 ta0_sb[:, kt * 128:(kt + 1) * 128],
                                 xres[:, xncol(kt, bsel, c0):xncol(kt, bsel, c0) + cw],
                                 start=(kt == 0), stop=(kt == 31)))(),
                       r=[ta0B, xresB], w=[PSB[ci]])
                op("v", (lambda ci=ci, c0=c0, cw=cw: lambda v:
                         v.tensor_tensor(hw[:, c0:c0 + cw], PS[ci][:, 0:cw],
                                         rrep[:, c0:c0 + cw],
                                         op=ALU.mult))(),
                   r=[PSB[ci], rrepB], w=[hwB])
            for ot in range(32):
                def ld_quarter(q, ot=ot):
                    tbt, tbtB = tbb[q % 2]
                    dma("s", tbt.ap().rearrange("p (k c) -> p k c", c=128),
                        ap_in["tbase4"][ot, q * 8:(q + 1) * 8].rearrange(
                            "k p c -> p k c"), w=[tbtB])
                ld_quarter(0)
                ld_quarter(1)
                for kt in range(32):
                    if kt == 8:
                        ld_quarter(2)
                    if kt == 16:
                        ld_quarter(3)
                    tbt, tbtB = tbb[(kt // 8) % 2]
                    for ci, (c0, cw) in enumerate(CHUNKS):
                        op("t", (lambda kt=kt, ci=ci, c0=c0, cw=cw, bsel=bsel,
                                 tbt=tbt, tbtB=tbtB: lambda t: t.matmul(
                                     PS[ci][:, 0:cw],
                                     tbt[:, (kt % 8) * 128:(kt % 8 + 1) * 128],
                                     xres[:, xncol(kt, bsel, c0):xncol(kt, bsel, c0) + cw],
                                     start=(kt == 0), stop=False))(),
                           r=[tbtB, xresB], w=[PSB[ci]])
                for ci, (c0, cw) in enumerate(CHUNKS):
                    op("t", (lambda ci=ci, c0=c0, cw=cw, ot=ot: lambda t:
                             t.matmul(PS[ci][:, 0:cw],
                                      tb0_sb[:, ot * 128:(ot + 1) * 128],
                                      hw[:, c0:c0 + cw],
                                      start=False, stop=True))(),
                       r=[tb0B, hwB], w=[PSB[ci]])
                    op("s", (lambda ci=ci, cw=cw, ot=ot: lambda s:
                             s.activation(t1s[:, 0:cw], PS[ci][:, 0:cw],
                                          AF.Identity,
                                          bias=bbase[:, ot:ot + 1]))(),
                       r=[PSB[ci], bbaseB], w=[t1sB])
                    op("v", (lambda ci=ci, c0=c0, cw=cw, ot=ot, bsel=bsel:
                             lambda v: v.tensor_tensor(
                                 y2sb[:, c0:c0 + cw], t1s[:, 0:cw],
                                 xres[:, xncol(ot, bsel, c0):xncol(ot, bsel, c0) + cw],
                                 op=ALU.add))(),
                       r=[t1sB, xresB], w=[y2sbB])
                    op("s", (lambda c0=c0, cw=cw: lambda s: s.activation(
                        ysq2[:, 0:cw], y2sb[:, c0:c0 + cw], AF.Square))(),
                       r=[y2sbB], w=[ysq2B])
                    op("t", (lambda ci=ci, c0=c0, cw=cw, ot=ot: lambda t:
                             t.matmul(PS[3 + ci][0:1, 0:cw], onesb[:, 0:1],
                                      y2sb[:, c0:c0 + cw],
                                      start=(ot == 0), stop=False))(),
                       r=[y2sbB, onesbB], w=[PSB[3 + ci]])
                    op("t", (lambda ci=ci, c0=c0, cw=cw, ot=ot: lambda t:
                             t.matmul(PS[3 + ci][32:33, 0:cw], onesb[:, 0:1],
                                      ysq2[:, 0:cw],
                                      start=(ot == 0), stop=(ot == 31)))(),
                       r=[ysq2B, onesbB], w=[PSB[3 + ci]])
                dma("q", ybuf[bsel, ot], y2sb[:, :], r=[y2sbB])
            for ci, (c0, cw) in enumerate(CHUNKS):
                op("v", (lambda ci=ci, cw=cw: lambda v: v.tensor_scalar(
                    xtm1[0:1, 2560:2560 + cw], PS[3 + ci][0:1, 0:cw], 1.0 / DM, None,
                    op0=ALU.mult))(), r=[PSB[3 + ci]], w=[xtmB[1]])
                op("v", (lambda ci=ci, c0=c0, cw=cw, bsel=bsel: lambda v:
                         v.tensor_copy(
                             ybt[0][0:1, bsel * TCOLS + c0:bsel * TCOLS + c0 + cw],
                             xtm1[0:1, 2560:2560 + cw]))(), r=[xtmB[1]], w=[xtmB[0]])
                op("v", (lambda ci=ci, cw=cw: lambda v: v.tensor_scalar(
                    xtm1[0:1, 3072:3072 + cw], PS[3 + ci][32:33, 0:cw],
                    1.0 / DM, None, op0=ALU.mult))(),
                   r=[PSB[3 + ci]], w=[xtmB[1]])
                op("v", (lambda cw=cw: lambda v: v.tensor_tensor(
                    xtm1[0:1, 3584:3584 + cw], xtm1[0:1, 2560:2560 + cw], xtm1[0:1, 2560:2560 + cw],
                    op=ALU.mult))(), r=[xtmB[1]], w=[xtmB[1]])
                op("v", (lambda cw=cw: lambda v: v.tensor_tensor(
                    xtm1[0:1, 3584:3584 + cw], xtm1[0:1, 3072:3072 + cw],
                    xtm1[0:1, 3584:3584 + cw], op=ALU.subtract))(),
                   r=[xtmB[1]], w=[xtmB[1]])
                op("s", (lambda cw=cw: lambda s: s.activation(
                    xtm1[0:1, 3072:3072 + cw], xtm1[0:1, 3584:3584 + cw],
                    AF.Sqrt, bias=cst[0:1, 2:3]))(), r=[xtmB[1], cstB], w=[xtmB[1]])
                op("v", (lambda cw=cw: lambda v: v.reciprocal(
                    xtm1[0:1, 3584:3584 + cw], xtm1[0:1, 3072:3072 + cw]))(),
                   r=[xtmB[1]], w=[xtmB[1]])
                op("v", (lambda ci=ci, c0=c0, cw=cw, bsel=bsel: lambda v:
                         v.tensor_copy(
                             ybt[0][0:1, 2054 + bsel * TCOLS + c0:2054 + bsel * TCOLS + c0 + cw],
                             xtm1[0:1, 3584:3584 + cw]))(), r=[xtmB[1]], w=[xtmB[0]])

        for bsel in range(B):
            for ci, (c0, cw) in enumerate(CHUNKS):
                op("t", (lambda ci=ci, c0=c0, cw=cw, bsel=bsel: lambda t:
                         t.matmul(PS[ci][:, 0:cw], onesrob[0:1, :],
                                  ybt[0][0:1, bsel * TCOLS + c0:bsel * TCOLS + c0 + cw],
                                  start=True, stop=True))(),
                   r=[xtmB[0], onesrobB], w=[PSB[ci]])
                op("t", (lambda ci=ci, c0=c0, cw=cw, bsel=bsel: lambda t:
                         t.matmul(PS[3 + ci][:, 0:cw], onesrob[0:1, :],
                                  ybt[0][0:1, 2054 + bsel * TCOLS + c0:2054 + bsel * TCOLS + c0 + cw],
                                  start=True, stop=True))(),
                   r=[xtmB[0], onesrobB], w=[PSB[3 + ci]])
            for ot in range(32):
                dma("g", y2sb[:, :], ybuf[bsel, ot], w=[y2sbB],
                    x=[("dq", p.dcnt["q"])])
                for ci, (c0, cw) in enumerate(CHUNKS):
                    op("v", (lambda ci=ci, c0=c0, cw=cw: lambda v:
                             v.tensor_tensor(t1s[:, 0:cw],
                                             y2sb[:, c0:c0 + cw],
                                             PS[ci][:, 0:cw],
                                             op=ALU.subtract))(),
                       r=[y2sbB, PSB[ci]], w=[t1sB])
                    op("v", (lambda ci=ci, c0=c0, cw=cw, ot=ot, bsel=bsel:
                             lambda v: v.tensor_tensor(
                                 xres[:, xncol(ot, bsel, c0):xncol(ot, bsel, c0) + cw],
                                 t1s[:, 0:cw], PS[3 + ci][:, 0:cw],
                                 op=ALU.mult))(),
                       r=[t1sB, PSB[3 + ci]], w=[xresB])
        st = dma("q", xn2_out, xres[:, :], r=[xresB])
        p._waits("q", [st])
        with nc.Block() as block:
            p.replay(block, sems)
    return nc



CH3 = [(0, 512), (512, 512), (1024, 3)]
OWN2 = [(HALO, 512), (HALO + 512, 512)]


def build_l1attn():
    nc = bass.Bass("TRN2", target_bir_lowering=False)
    ap_in = {}
    for name, shape, dt in [
        ("xn2", [128, 32 * B * TCOLS], BF16),
        ("twin4", [128, 32, 128, 128], BF16),
        ("twx4", [3, 64, 128, 128], BF16),
        ("twdt4", [64, 2, 128, 128], BF16),
        ("identb", [128, 128], BF16),
        ("onesrow_f", [65, 128], F32),
        ("ones_b", [128, 1], BF16),
        ("convw_col", [128, 256], F32),
        ("convb_col", [128, 64], F32),
        ("dtb_col", [128, 64], F32),
        ("dpar_col", [128, 64], F32),
        ("acol", [128, 1024], F32),
        ("dtlnw_col", [128, 2], F32),
        ("bcw_col", [16, 2], F32),
        ("rep_sel", [16, 2048], BF16),
    ]:
        ap_in[name] = nc.dram_tensor(name, shape, dt, kind="ExternalInput").ap()
    o_ypre = nc.dram_tensor("ypre", [B, 64, 128, 1024], BF16,
                            kind="ExternalOutput").ap()
    o_r = nc.dram_tensor("r_out", [B, 64, 128, 1024], F32,
                         kind="ExternalOutput").ap()
    o_sg = nc.dram_tensor("sg_out", [B, 64, 128, 1024], BF16,
                          kind="ExternalOutput").ap()
    o_ps = nc.dram_tensor("pso", [128, 4096], F32, kind="ExternalOutput").ap()
    o_cn = nc.dram_tensor("cn_out", [B, 2, 16, 512], BF16,
                          kind="ExternalOutput").ap()
    u_sp = nc.dram_tensor("u_sp", [B, 64, 128, 1024], BF16).ap()

    with ExitStack() as stack:
        E = stack.enter_context
        sems = make_sems(stack, nc)
        p = Prog(nc)

        def T(name, shape, dt=F32):
            t = E(nc.sbuf_tensor(name, shape, dt))
            return t, Buf()

        idb, idbB = T("idb", [128, 128], BF16)
        onesrow, onesrowB = T("onesrow", [65, 128])
        onesb, onesbB = T("onesb", [128, 1], BF16)
        convw, convwB = T("convw", [128, 256])
        convb, convbB = T("convb", [128, 64])
        dtb, dtbB = T("dtb", [128, 64])
        dpar, dparB = T("dpar", [128, 64])
        acol, acolB = T("acol_sb", [128, 1024])
        dtlnw, dtlnwB = T("dtlnw", [128, 2])
        bcw, bcwB = T("bcw", [16, 2])
        repsel, repselB = T("repsel", [16, 2048], BF16)
        xres, xresB = T("xres", [128, 32 * B * TCOLS], BF16)

        arena, arenaB = T("arena", [128, 4096])
        arena_bf = arena.ap().bitcast(BF16)
        twinh = [(arena_bf[:, 0:2048], Buf()), (arena_bf[:, 2048:4096], Buf())]
        u_raw = arena[:, 2048:2048 + TCOLS]
        u_rawB = Buf()
        cacc, caccB = T("cacc", [128, 1024])
        u_act, u_actB = T("u_act", [128, 1024], BF16)
        sg_t, sg_tB = T("sg_t", [128, 1024], BF16)
        twdt_t = [T(f"twdt{i}", [128, 256], BF16) for i in range(2)]
        dtn = [T(f"dtn{i}", [128, 512], BF16) for i in range(2)]
        sq_t, sq_tB = T("sq_t", [128, 512], BF16)
        rrep_sb, rrep_sbB = T("rrep_sb", [128, 512], BF16)
        repb_sb, repb_sbB = T("repb_sb", [16, 512], BF16)
        repc_sb, repc_sbB = T("repc_sb", [16, 512], BF16)
        bn_sb, bn_sbB = T("bn_sb", [16, 512], BF16)
        cn_sb, cn_sbB = T("cn_sb", [16, 512], BF16)
        rowS, rowSB = T("rowS", [128, 512])
        delta, deltaB = T("delta", [128, 512])
        r_t, r_tB = T("r_t", [128, 512])
        p_t, p_tB = T("p_t", [128, 512], BF16)
        ones_t, ones_tB = T("ones_t", [128, 512])
        dA = [T(f"dA{i}", [128, 512]) for i in range(2)]
        xin = [T(f"xin{i}", [128, 512], BF16) for i in range(2)]
        hall = arena_bf[:, 0:16 * 512]
        hallB = Buf()
        hct = [T(f"hct{i}", [128, 512], BF16) for i in range(2)]
        ud_t, ud_tB = T("ud_t", [128, 512], BF16)
        ypre_t, ypre_tB = T("ypre_t", [128, 512], BF16)
        tsp = [(cacc[:, 0:512], Buf()), (cacc[:, 512:1024], Buf())]

        cst, cstB = T("cst", [128, 4])
        PS = [psum_bank(stack, nc, f"bank{i}") for i in range(8)]
        PSB = [Buf() for _ in range(8)]
        xres_bf = xres.ap()
        xres_f32 = xres.ap().bitcast(F32)
        twx_sb = xres_bf[:, 0:24576]
        twx_sbB = Buf()
        brep = xres_bf[:, 24576:32768]
        brepB = Buf()
        ustr = [(E(nc.sbuf_tensor(f"ustr{i}", [128, 8 * 512], BF16)), Buf())
                for i in range(2)]
        crep = xres_bf[:, 32768:40960]
        crepB = Buf()
        carry_h = xres_f32[:, 20480:22528]
        carry_hB = Buf()
        parg = xres_f32[:, 22528:23552]
        pargB = Buf()
        r_carry = xres_f32[:, 23552:23680]
        r_carryB = Buf()

        def op(eng, fn, r=(), w=(), x=()):
            return emit(p, eng, fn, r, w, x)

        def dma(q, out_ap, in_ap, r=(), w=(), x=()):
            return emit_dma(p, q, out_ap, in_ap, r, w, x)

        op("v", lambda v: v.memset(cst[:, 0:1], EPS_RMS), w=[cstB])
        op("v", lambda v: v.memset(cst[:, 1:2], 1.0 / 256.0), r=[cstB])
        op("v", lambda v: v.memset(cst[:, 2:3], 1.0 / 16.0), r=[cstB])
        op("v", lambda v: v.memset(cst[:, 3:4], 1.0), r=[cstB])
        op("v", lambda v: v.memset(ones_t[:, :], 1.0), w=[ones_tB])
        cstB.w = p.barrier_tok("v")
        for apn, sb, bf in [("identb", idb, idbB), ("onesrow_f", onesrow, onesrowB),
                            ("ones_b", onesb, onesbB), ("convw_col", convw, convwB),
                            ("convb_col", convb, convbB), ("dtb_col", dtb, dtbB),
                            ("dpar_col", dpar, dparB), ("acol", acol, acolB),
                            ("dtlnw_col", dtlnw, dtlnwB), ("bcw_col", bcw, bcwB),
                            ("rep_sel", repsel, repselB)]:
            dma("q", sb[:, :] if hasattr(sb, "__getitem__") else sb, ap_in[apn],
                w=[bf])
        dma("q", xres[:, :], ap_in["xn2"], w=[xresB])

        for ut in range(64):
            for h in range(2):
                tw, twB = twinh[h]
                dma("s", tw.rearrange("p (k c) -> p k c", c=128),
                    ap_in["twin4"][ut, h * 16:(h + 1) * 16].rearrange(
                        "k p c -> p k c"), w=[twB])
            for b in range(B):
                for ci, (c0, cw) in enumerate(CH3):
                    for kt in range(32):
                        tw, twB = twinh[kt // 16]
                        op("t", (lambda kt=kt, ci=ci, c0=c0, cw=cw, b=b, tw=tw:
                                 lambda t: t.matmul(
                                     PS[ci][:, 0:cw],
                                     tw[:, (kt % 16) * 128:(kt % 16 + 1) * 128],
                                     xres[:, xncol(kt, b, c0):xncol(kt, b, c0) + cw],
                                     start=(kt == 0), stop=(kt == 31)))(),
                           r=[twB, xresB], w=[PSB[ci]])
                    op("s", (lambda ci=ci, c0=c0, cw=cw: lambda s: s.copy(
                        u_raw[:, c0:c0 + cw], PS[ci][:, 0:cw]))(),
                       r=[PSB[ci]], w=[u_rawB])
                op("v", (lambda ut=ut: lambda v: v.tensor_scalar(
                    cacc[:, :], u_raw[:, 0:1024], convw[:, 4 * ut:4 * ut + 1],
                    None, op0=ALU.mult))(), r=[u_rawB, convwB], w=[caccB])
                for k in range(1, 4):
                    op("v", (lambda ut=ut, k=k: lambda v:
                             v.scalar_tensor_tensor(
                                 cacc[:, :], u_raw[:, k:1024 + k],
                                 convw[:, 4 * ut + k:4 * ut + k + 1],
                                 cacc[:, :], ALU.mult, ALU.add))(),
                       r=[u_rawB, convwB, caccB], w=[caccB])
                op("s", (lambda ut=ut: lambda s: s.activation(
                    u_act[:, :], cacc[:, :], AF.Silu,
                    bias=convb[:, ut:ut + 1]))(),
                   r=[caccB, convbB], w=[u_actB])
                dma("q", u_sp[b, ut], u_act[:, :], r=[u_actB])
        for gt in range(64):
            for h in range(2):
                tw, twB = twinh[h]
                dma("s", tw.rearrange("p (k c) -> p k c", c=128),
                    ap_in["twin4"][64 + gt, h * 16:(h + 1) * 16].rearrange(
                        "k p c -> p k c"), w=[twB])
            for b in range(B):
                for ci, (c0, cw) in enumerate(OWN2):
                    for kt in range(32):
                        tw, twB = twinh[kt // 16]
                        op("t", (lambda kt=kt, ci=ci, c0=c0, cw=cw, b=b, tw=tw:
                                 lambda t: t.matmul(
                                     PS[ci][:, 0:cw],
                                     tw[:, (kt % 16) * 128:(kt % 16 + 1) * 128],
                                     xres[:, xncol(kt, b, c0):xncol(kt, b, c0) + cw],
                                     start=(kt == 0), stop=(kt == 31)))(),
                           r=[twB, xresB], w=[PSB[ci]])
                    op("s", (lambda ci=ci, cw=cw: lambda s: s.activation(
                        sg_t[:, ci * 512:ci * 512 + cw], PS[ci][:, 0:cw],
                        AF.Silu))(), r=[PSB[ci]], w=[sg_tB])
                dma("q", o_sg[b, gt], sg_t[:, :], r=[sg_tB])

        xres_dead = p.all_toks()
        for bb in (hallB, tsp[0][1], tsp[1][1], twx_sbB, brepB, crepB,
                   carry_hB, pargB, r_carryB):
            bb.r = list(xres_dead)
        op("v", lambda v: v.memset(carry_h[:, :], 0.0), w=[carry_hB])
        op("v", lambda v: v.memset(r_carry[:, :], 0.0), w=[r_carryB])
        for g in range(3):
            dma("g", twx_sb[:, g * 64 * 128:(g + 1) * 64 * 128].rearrange(
                "p (k c) -> p k c", c=128),
                ap_in["twx4"][g].rearrange("k p c -> p k c"), w=[twx_sbB])
        for b in range(B):
            for fc in range(2):
                def uload(g8, b=b, fc=fc):
                    ub, ubB = ustr[g8 % 2]
                    dma("g", ub.ap().rearrange("p (u t) -> p u t", t=512),
                        u_sp[b, g8 * 8:(g8 + 1) * 8, :,
                             fc * 512:(fc + 1) * 512].rearrange(
                        "u p t -> p u t"), w=[ubB])
                uload(0)
                uload(1)
                for kt in range(64):
                    if kt % 8 == 0 and kt >= 8 and kt // 8 + 1 < 8:
                        uload(kt // 8 + 1)
                    ub, ubB = ustr[(kt // 8) % 2]
                    for gi, (ps_i, lo, hi) in enumerate(
                            [(0, 0, 128), (1, 0, 128), (2, 0, 16), (3, 16, 32)]):
                        gsl = [0, 1, 2, 2][gi]
                        op("t", (lambda kt=kt, ps_i=ps_i, lo=lo, hi=hi,
                                 gsl=gsl, ub=ub: lambda t: t.matmul(
                                     PS[ps_i][0:hi - lo, 0:512],
                                     twx_sb[:, gsl * 64 * 128 + kt * 128 + lo:
                                            gsl * 64 * 128 + kt * 128 + hi],
                                     ub[:, (kt % 8) * 512:(kt % 8 + 1) * 512],
                                     start=(kt == 0), stop=(kt == 63)))(),
                           r=[twx_sbB, ubB], w=[PSB[ps_i]])
                sq1 = op("s", lambda s: s.activation(sq_t[:, :], PS[0][:, 0:512],
                                                     AF.Square),
                         r=[PSB[0]], w=[sq_tB])
                op("t", lambda t: t.matmul(PS[4][0:1, 0:512], onesb[:, 0:1],
                                           sq_t[:, :], start=True, stop=False),
                   r=[sq_tB, onesbB], w=[PSB[4]])
                op("s", lambda s: s.activation(sq_t[:, :], PS[1][:, 0:512],
                                               AF.Square),
                   r=[PSB[1], sq_tB], w=[sq_tB])
                op("t", lambda t: t.matmul(PS[4][0:1, 0:512], onesb[:, 0:1],
                                           sq_t[:, :], start=False, stop=True),
                   r=[sq_tB, onesbB], w=[PSB[4]])
                op("s", lambda s: s.activation(rowS[0:1, 0:512], PS[4][0:1, 0:512],
                                               AF.Sqrt, scale=cst[0:1, 1:2],
                                               bias=cst[0:1, 0:1]),
                   r=[PSB[4], cstB], w=[rowSB])
                op("v", lambda v: v.reciprocal(rowS[0:1, 0:512], rowS[0:1, 0:512]),
                   r=[rowSB], w=[rowSB])
                op("t", lambda t: t.matmul(PS[5][:, 0:512], onesrow[0:1, :],
                                           rowS[0:1, 0:512], start=True,
                                           stop=True),
                   r=[rowSB, onesrowB], w=[PSB[5]])
                op("s", lambda s: s.copy(rrep_sb[:, :], PS[5][:, 0:512]),
                   r=[PSB[5]], w=[rrep_sbB])
                for i in range(2):
                    op("v", (lambda i=i: lambda v: v.scalar_tensor_tensor(
                        dtn[i][0][:, :], PS[i][:, 0:512], dtlnw[:, i:i + 1],
                        rrep_sb[:, :], ALU.mult, ALU.mult))(),
                       r=[PSB[i], dtlnwB, rrep_sbB], w=[dtn[i][1]])
                for gi, (ps_i, nsb, nsbB, repsb, repsbB) in enumerate(
                        [(2, bn_sb, bn_sbB, repb_sb, repb_sbB),
                         (3, cn_sb, cn_sbB, repc_sb, repc_sbB)]):
                    op("s", (lambda ps_i=ps_i: lambda s: s.activation(
                        sq_t[0:16, 0:512], PS[ps_i][0:16, 0:512], AF.Square))(),
                       r=[PSB[ps_i], sq_tB], w=[sq_tB])
                    op("t", lambda t: t.matmul(PS[4][0:1, 0:512],
                                               onesb[0:16, 0:1],
                                               sq_t[0:16, 0:512],
                                               start=True, stop=True),
                       r=[sq_tB, onesbB], w=[PSB[4]])
                    op("s", lambda s: s.activation(rowS[0:1, 0:512],
                                                   PS[4][0:1, 0:512],
                                                   AF.Sqrt, scale=cst[0:1, 2:3],
                                                   bias=cst[0:1, 0:1]),
                       r=[PSB[4], cstB], w=[rowSB])
                    op("v", lambda v: v.reciprocal(rowS[0:1, 0:512],
                                                   rowS[0:1, 0:512]),
                       r=[rowSB], w=[rowSB])
                    op("t", lambda t: t.matmul(PS[5][0:16, 0:512],
                                               onesrow[0:1, 0:16],
                                               rowS[0:1, 0:512],
                                               start=True, stop=True),
                       r=[rowSB, onesrowB], w=[PSB[5]])
                    op("s", (lambda repsb=repsb: lambda s: s.copy(
                        repsb[:, :], PS[5][0:16, 0:512]))(),
                       r=[PSB[5]], w=[repsbB])
                    op("v", (lambda ps_i=ps_i, gi=gi, nsb=nsb, repsb=repsb:
                             lambda v: v.scalar_tensor_tensor(
                                 nsb[:, :], PS[ps_i][0:16, 0:512],
                                 bcw[:, gi:gi + 1], repsb[:, :],
                                 ALU.mult, ALU.mult))(),
                       r=[PSB[ps_i], bcwB, repsbB], w=[nsbB])
                dma("q", o_cn[b, fc], cn_sb[:, :], r=[cn_sbB])
                for n in range(16):
                    for (nsb, nsbB, dst, dstB) in [(bn_sb, bn_sbB, brep, brepB),
                                                   (cn_sb, cn_sbB, crep, crepB)]:
                        op("t", (lambda n=n, nsb=nsb: lambda t: t.matmul(
                            PS[5][:, 0:512], repsel[:, n * 128:(n + 1) * 128],
                            nsb[:, :], start=True, stop=True))(),
                           r=[nsbB, repselB], w=[PSB[5]])
                        op("s", (lambda n=n, dst=dst: lambda s: s.copy(
                            dst[:, n * 512:(n + 1) * 512], PS[5][:, 0:512]))(),
                           r=[PSB[5]], w=[dstB])
                uload(0)
                uload(1)
                for ut in range(64):
                    if ut % 8 == 0 and ut >= 8 and ut // 8 + 1 < 8:
                        uload(ut // 8 + 1)
                    uu, uuB = ustr[(ut // 8) % 2]
                    twd, twdB = twdt_t[ut % 2]
                    dma("s", twd.ap().rearrange("p (k c) -> p k c", c=128),
                        ap_in["twdt4"][ut].rearrange("k p c -> p k c"),
                        w=[twdB])
                    for i in range(2):
                        op("t", (lambda i=i, twd=twd: lambda t: t.matmul(
                            PS[6][:, 0:512], twd[:, i * 128:(i + 1) * 128],
                            dtn[i][0][:, :], start=(i == 0), stop=(i == 1)))(),
                           r=[twdB, dtn[i][1]], w=[PSB[6]])
                    op("s", (lambda ut=ut: lambda s: s.activation(
                        tsp[0][0][:, :], PS[6][:, 0:512], AF.Abs,
                        bias=dtb[:, ut:ut + 1]))(),
                       r=[PSB[6], dtbB], w=[tsp[0][1]])
                    op("s", lambda s: s.activation(tsp[0][0][:, :],
                                                   tsp[0][0][:, :], AF.Exp,
                                                   scale=-1.0),
                       r=[tsp[0][1]], w=[tsp[0][1]])
                    op("s", lambda s: s.activation(tsp[0][0][:, :],
                                                   tsp[0][0][:, :], AF.Ln,
                                                   bias=cst[:, 3:4]),
                       r=[tsp[0][1], cstB], w=[tsp[0][1]])
                    op("s", (lambda ut=ut: lambda s: s.activation(
                        tsp[1][0][:, :], PS[6][:, 0:512], AF.Relu,
                        bias=dtb[:, ut:ut + 1]))(),
                       r=[PSB[6], dtbB], w=[tsp[1][1]])
                    op("v", lambda v: v.tensor_tensor(delta[:, :],
                                                      tsp[0][0][:, :],
                                                      tsp[1][0][:, :],
                                                      op=ALU.add),
                       r=[tsp[0][1], tsp[1][1]], w=[deltaB])
                    op("v", (lambda b=b, ut=ut: lambda v: v.tensor_tensor_scan(
                        r_t[:, :], ones_t[:, :], delta[:, :],
                        r_carry[:, b * 64 + ut:b * 64 + ut + 1],
                        op0=ALU.mult, op1=ALU.add))(),
                       r=[ones_tB, deltaB, r_carryB], w=[r_tB])
                    op("v", (lambda b=b, ut=ut: lambda v: v.tensor_copy(
                        r_carry[:, b * 64 + ut:b * 64 + ut + 1],
                        r_t[:, 511:512]))(), r=[r_tB], w=[r_carryB])
                    dma("q", o_r[b, ut, :, fc * 512:(fc + 1) * 512], r_t[:, :],
                        r=[r_tB])
                    op("v", (lambda ut=ut, uu=uu: lambda v: v.tensor_tensor(
                        p_t[:, :], delta[:, :],
                        uu[:, (ut % 8) * 512:(ut % 8 + 1) * 512],
                        op=ALU.mult))(),
                       r=[deltaB, uuB], w=[p_tB])
                    for n in range(16):
                        da, daB = dA[n % 2]
                        xi, xiB = xin[n % 2]
                        op("s", (lambda ut=ut, n=n, da=da: lambda s:
                                 s.activation(da[:, :], delta[:, :], AF.Exp,
                                              scale=acol[:, ut * 16 + n:
                                                         ut * 16 + n + 1]))(),
                           r=[deltaB, acolB], w=[daB])
                        op("v", (lambda n=n, xi=xi: lambda v: v.tensor_tensor(
                            xi[:, :], p_t[:, :],
                            brep[:, n * 512:(n + 1) * 512], op=ALU.mult))(),
                           r=[p_tB, brepB], w=[xiB])
                        op("v", (lambda b=b, ut=ut, n=n, da=da, xi=xi:
                                 lambda v: v.tensor_tensor_scan(
                                     hall[:, n * 512:(n + 1) * 512],
                                     da[:, :], xi[:, :],
                                     carry_h[:, (b * 64 + ut) * 16 + n:
                                             (b * 64 + ut) * 16 + n + 1],
                                     op0=ALU.mult, op1=ALU.add))(),
                           r=[daB, xiB, carry_hB], w=[hallB])
                        hc, hcB = hct[n % 2]
                        op("v", (lambda n=n, hc=hc: lambda v: v.tensor_tensor(
                            hc[:, :], hall[:, n * 512:(n + 1) * 512],
                            crep[:, n * 512:(n + 1) * 512], op=ALU.mult))(),
                           r=[hallB, crepB], w=[hcB])
                        op("t", (lambda n=n, hc=hc: lambda t: t.matmul(
                            PS[7][:, 0:512], idb[:, :], hc[:, :],
                            start=(n == 0), stop=(n == 15)))(),
                           r=[hcB, idbB], w=[PSB[7]])
                    op("v", (lambda b=b, ut=ut: lambda v: v.tensor_copy(
                        carry_h[:, (b * 64 + ut) * 16:(b * 64 + ut + 1) * 16],
                        hall[:, 511:16 * 512:512]))(),
                       r=[hallB], w=[carry_hB])
                    op("v", (lambda ut=ut, uu=uu: lambda v: v.tensor_scalar(
                        ud_t[:, :], uu[:, (ut % 8) * 512:(ut % 8 + 1) * 512],
                        dpar[:, ut:ut + 1], None, op0=ALU.mult))(),
                       r=[uuB, dparB], w=[ud_tB])
                    op("v", lambda v: v.tensor_tensor(ypre_t[:, :],
                                                      PS[7][:, 0:512],
                                                      ud_t[:, :], op=ALU.add),
                       r=[PSB[7], ud_tB], w=[ypre_tB])
                    dma("q", o_ypre[b, ut, :, fc * 512:(fc + 1) * 512],
                        ypre_t[:, :], r=[ypre_tB])
        for b in range(B):
            for ut in range(64):
                op("v", (lambda b=b, ut=ut: lambda v: v.tensor_scalar(
                    parg[:, ut * 16:(ut + 1) * 16],
                    acol[:, ut * 16:(ut + 1) * 16],
                    r_carry[:, b * 64 + ut:b * 64 + ut + 1], None,
                    op0=ALU.mult))(), r=[acolB, r_carryB], w=[pargB])
            op("s", lambda s: s.activation(parg[:, :], parg[:, :], AF.Exp),
               r=[pargB], w=[pargB])
            dma("q", o_ps[:, b * 1024:(b + 1) * 1024], parg[:, :], r=[pargB])
        st = dma("q", o_ps[:, 2048:4096], carry_h[:, :], r=[carry_hB])
        p._waits("q", [st])
        with nc.Block() as block:
            p.replay(block, sems)
    return nc


def host_l1attn(xn2_list, W, inputs):
    nc = build_l1attn()
    cw = np.asarray(inputs["L1_attn_conv_w"], np.float32)
    convw_col = cw.reshape(64, 128, 4).transpose(1, 0, 2).reshape(128, 256)
    col64 = lambda v: np.ascontiguousarray(
        np.asarray(v, np.float32).reshape(64, 128).T)
    acol = np.ascontiguousarray(np.asarray(
        inputs["L1_attn_A"], np.float32).reshape(64, 128, 16).transpose(
        1, 0, 2).reshape(128, 1024))
    dtlnw = np.ascontiguousarray(np.asarray(
        inputs["L1_attn_dt_ln_w"], np.float32).reshape(2, 128).T)
    bcw = np.ascontiguousarray(np.stack(
        [np.asarray(inputs["L1_attn_b_ln_w"], np.float32),
         np.asarray(inputs["L1_attn_c_ln_w"], np.float32)], axis=1))
    rep_sel = np.concatenate(
        [np.tile(np.eye(16, dtype=BF)[:, n:n + 1], (1, 128))
         for n in range(16)], axis=1)
    in_maps = []
    for k in range(NCORES):
        in_maps.append({
            "xn2": xn2_list[k], "twin4": W["twin"], "twx4": W["twx"],
            "twdt4": W["twdt"], "identb": IDENT_BF,
            "onesrow_f": ONES_ROW_F32, "ones_b": ONES_COL_BF,
            "convw_col": np.ascontiguousarray(convw_col),
            "convb_col": col64(inputs["L1_attn_conv_b"]),
            "dtb_col": col64(inputs["L1_attn_dt_bias"]),
            "dpar_col": col64(inputs["L1_attn_D_param"]),
            "acol": acol, "dtlnw_col": dtlnw, "bcw_col": bcw,
            "rep_sel": np.ascontiguousarray(rep_sel),
        })
    res = run_bass_kernel_spmd(nc, in_maps, core_ids=list(range(NCORES)))
    keys = ["ypre", "r_out", "sg_out", "pso", "cn_out"]
    return [{kk: res.results[k][kk] for kk in keys} for k in range(NCORES)]



def x3col(ot, b, t0=0):
    return ot * (B * TOK) + b * TOK + t0


def build_l1fix():
    nc = bass.Bass("TRN2", target_bir_lowering=False)
    ap_in = {}
    for name, shape, dt in [
        ("pred", [7, 128, 4096], F32),
        ("ypre", [B, 64, 128, 1024], BF16),
        ("r_in", [B, 64, 128, 1024], F32),
        ("sg_in", [B, 64, 128, 1024], BF16),
        ("cn_in", [B, 2, 16, 512], BF16),
        ("xn2", [128, 32 * B * TCOLS], BF16),
        ("twout4", [32, 64, 128, 128], BF16),
        ("acol", [128, 1024], F32),
        ("identb", [128, 128], BF16),
        ("onesrow_f", [65, 128], F32),
        ("onesrow_b", [65, 128], BF16),
        ("ones_b", [128, 1], BF16),
        ("rep_sel", [16, 2048], BF16),
    ]:
        ap_in[name] = nc.dram_tensor(name, shape, dt, kind="ExternalInput").ap()
    xn3_out = nc.dram_tensor("xn3", [128, 32 * B * TOK], BF16,
                             kind="ExternalOutput").ap()
    ybuf3 = nc.dram_tensor("ybuf3", [B, 2, 32, 128, 512], BF16).ap()

    with ExitStack() as stack:
        E = stack.enter_context
        sems = make_sems(stack, nc)
        p = Prog(nc)

        def T(name, shape, dt=F32):
            t = E(nc.sbuf_tensor(name, shape, dt))
            return t, Buf()

        idb, idbB = T("idb", [128, 128], BF16)
        onesrow, onesrowB = T("onesrow", [65, 128])
        onesrob, onesrobB = T("onesrob", [65, 128], BF16)
        onesb, onesbB = T("onesb", [128, 1], BF16)
        acol, acolB = T("acol_sb", [128, 1024])
        repsel, repselB = T("repsel", [16, 2048], BF16)
        hin, hinB = T("hin", [128, 2048])
        predb = [T(f"pred{i}", [128, 4096]) for i in range(2)]
        cn_sb, cn_sbB = T("cn_sb", [16, 512], BF16)
        crep, crepB = T("crep", [128, 16 * 512], BF16)
        y2fc, y2fcB = T("y2fc", [128, 64 * 512], BF16)
        r_t = [T(f"rt{i}", [128, 512]) for i in range(2)]
        ets = [T(f"et{i}", [128, 512]) for i in range(2)]
        cct = [T(f"cct{i}", [128, 512], BF16) for i in range(4)]
        ypre_t, ypre_tB = T("ypre_t", [128, 512], BF16)
        sg_tt, sg_ttB = T("sg_tt", [128, 512], BF16)
        yc_t, yc_tB = T("yc_t", [128, 512])
        two_t = [T(f"two{i}", [128, 2048], BF16) for i in range(2)]
        xn2t, xn2tB = T("xn2t", [128, 512], BF16)
        y3t, y3tB = T("y3t", [128, 512], BF16)
        ysq3, ysq3B = T("ysq3", [128, 512], BF16)
        rowS, rowSB = T("rowS", [128, 512])
        mr3, mr3B = T("mr3", [128, 4096], BF16)
        cst, cstB = T("cst", [128, 4])
        t1s, t1sB = T("t1s", [128, 512])
        PS = [psum_bank(stack, nc, f"bank{i}") for i in range(8)]
        PSB = [Buf() for _ in range(8)]

        def op(eng, fn, r=(), w=(), x=()):
            return emit(p, eng, fn, r, w, x)

        def dma(q, out_ap, in_ap, r=(), w=(), x=()):
            return emit_dma(p, q, out_ap, in_ap, r, w, x)

        op("v", lambda v: v.memset(cst[:, 0:1], EPS_LN), w=[cstB])
        op("v", lambda v: v.memset(hin[:, :], 0.0), w=[hinB])
        cstB.w = p.barrier_tok("v")
        for apn, sb, bf in [("identb", idb, idbB), ("onesrow_f", onesrow, onesrowB),
                            ("onesrow_b", onesrob, onesrobB),
                            ("ones_b", onesb, onesbB), ("acol", acol, acolB),
                            ("rep_sel", repsel, repselB)]:
            dma("q", sb[:, :], ap_in[apn], w=[bf])
        for j in range(7):
            pb, pbB = predb[j % 2]
            dma("q", pb[:, :], ap_in["pred"][j], w=[pbB])
            op("v", (lambda pb=pb: lambda v: v.tensor_tensor(
                hin[:, :], hin[:, :], pb[:, 0:2048], op=ALU.mult))(),
               r=[pbB, hinB], w=[hinB])
            op("v", (lambda pb=pb: lambda v: v.tensor_tensor(
                hin[:, :], hin[:, :], pb[:, 2048:4096], op=ALU.add))(),
               r=[pbB, hinB], w=[hinB])
        for b in range(B):
            for fc in range(2):
                dma("q", cn_sb[:, :], ap_in["cn_in"][b, fc], w=[cn_sbB])
                for n in range(16):
                    op("t", (lambda n=n: lambda t: t.matmul(
                        PS[5][:, 0:512], repsel[:, n * 128:(n + 1) * 128],
                        cn_sb[:, :], start=True, stop=True))(),
                       r=[cn_sbB, repselB], w=[PSB[5]])
                    op("s", (lambda n=n: lambda s: s.copy(
                        crep[:, n * 512:(n + 1) * 512], PS[5][:, 0:512]))(),
                       r=[PSB[5]], w=[crepB])
                for ut in range(64):
                    rt, rtB = r_t[ut % 2]
                    dma("g", rt[:, :],
                        ap_in["r_in"][b, ut, :, fc * 512:(fc + 1) * 512],
                        w=[rtB])
                    for n in range(16):
                        et, etB = ets[n % 2]
                        op("s", (lambda ut=ut, n=n, rt=rt, et=et: lambda s:
                                 s.activation(et[:, :], rt[:, :], AF.Exp,
                                              scale=acol[:, ut * 16 + n:
                                                         ut * 16 + n + 1]))(),
                           r=[rtB, acolB], w=[etB])
                        cc, ccB = cct[n % 4]
                        op("v", (lambda b=b, ut=ut, n=n, cc=cc, et=et,
                                 etB=etB: lambda v:
                                 v.scalar_tensor_tensor(
                                     cc[:, :], et[:, :],
                                     hin[:, (b * 64 + ut) * 16 + n:
                                         (b * 64 + ut) * 16 + n + 1],
                                     crep[:, n * 512:(n + 1) * 512],
                                     ALU.mult, ALU.mult))(),
                           r=[etB, hinB, crepB], w=[ccB])
                        op("t", (lambda n=n, cc=cc: lambda t: t.matmul(
                            PS[7][:, 0:512], idb[:, :], cc[:, :],
                            start=(n == 0), stop=(n == 15)))(),
                           r=[ccB, idbB], w=[PSB[7]])
                    dma("s", ypre_t[:, :],
                        ap_in["ypre"][b, ut, :, fc * 512:(fc + 1) * 512],
                        w=[ypre_tB])
                    dma("q", sg_tt[:, :],
                        ap_in["sg_in"][b, ut, :, fc * 512:(fc + 1) * 512],
                        w=[sg_ttB])
                    op("v", lambda v: v.tensor_tensor(
                        yc_t[:, :], PS[7][:, 0:512], ypre_t[:, :], op=ALU.add),
                       r=[PSB[7], ypre_tB], w=[yc_tB])
                    op("v", (lambda ut=ut: lambda v: v.tensor_tensor(
                        y2fc[:, ut * 512:(ut + 1) * 512], yc_t[:, :],
                        sg_tt[:, :], op=ALU.mult))(),
                       r=[yc_tB, sg_ttB], w=[y2fcB])
                for ot in range(32):
                    def ldq(q4, ot=ot):
                        tw, twB = two_t[q4 % 2]
                        dma("s", tw.ap().rearrange("p (k c) -> p k c", c=128),
                            ap_in["twout4"][ot, q4 * 16:(q4 + 1) * 16].rearrange(
                                "k p c -> p k c"), w=[twB])
                    ldq(0)
                    ldq(1)
                    for kt in range(64):
                        if kt % 16 == 0 and kt >= 16 and kt // 16 + 1 < 4:
                            ldq(kt // 16 + 1)
                        tw, twB = two_t[(kt // 16) % 2]
                        op("t", (lambda kt=kt, tw=tw: lambda t: t.matmul(
                            PS[6][:, 0:512],
                            tw[:, (kt % 16) * 128:(kt % 16 + 1) * 128],
                            y2fc[:, kt * 512:(kt + 1) * 512],
                            start=(kt == 0), stop=(kt == 63)))(),
                           r=[twB, y2fcB], w=[PSB[6]])
                    dma("q", xn2t[:, :],
                        ap_in["xn2"][:, xncol(ot, b, HALO + fc * 512):
                                     xncol(ot, b, HALO + fc * 512) + 512],
                        w=[xn2tB])
                    op("v", lambda v: v.tensor_tensor(
                        y3t[:, :], PS[6][:, 0:512], xn2t[:, :], op=ALU.add),
                       r=[PSB[6], xn2tB], w=[y3tB])
                    op("s", lambda s: s.activation(ysq3[:, :], y3t[:, :],
                                                   AF.Square),
                       r=[y3tB], w=[ysq3B])
                    op("t", (lambda ot=ot: lambda t: t.matmul(
                        PS[4][0:1, 0:512], onesb[:, 0:1], y3t[:, :],
                        start=(ot == 0), stop=False))(),
                       r=[y3tB, onesbB], w=[PSB[4]])
                    op("t", (lambda ot=ot: lambda t: t.matmul(
                        PS[4][32:33, 0:512], onesb[:, 0:1], ysq3[:, :],
                        start=(ot == 0), stop=(ot == 31)))(),
                       r=[ysq3B, onesbB], w=[PSB[4]])
                    dma("q", ybuf3[b, fc, ot], y3t[:, :], r=[y3tB])
                cidx = b * 2 + fc
                op("v", (lambda: lambda v: v.tensor_scalar(
                    rowS[0:1, 0:512], PS[4][0:1, 0:512], 1.0 / DM, None,
                    op0=ALU.mult))(), r=[PSB[4]], w=[rowSB])
                op("v", (lambda cidx=cidx: lambda v: v.tensor_copy(
                    mr3[0:1, cidx * 512:(cidx + 1) * 512],
                    rowS[0:1, 0:512]))(), r=[rowSB], w=[mr3B])
                op("v", (lambda: lambda v: v.tensor_scalar(
                    t1s[0:1, 0:512], PS[4][32:33, 0:512], 1.0 / DM, None,
                    op0=ALU.mult))(), r=[PSB[4]], w=[t1sB])
                op("v", lambda v: v.tensor_tensor(
                    rowS[0:1, 0:512], rowS[0:1, 0:512], rowS[0:1, 0:512],
                    op=ALU.mult), r=[rowSB], w=[rowSB])
                op("v", lambda v: v.tensor_tensor(
                    rowS[0:1, 0:512], t1s[0:1, 0:512], rowS[0:1, 0:512],
                    op=ALU.subtract), r=[rowSB, t1sB], w=[rowSB])
                op("s", lambda s: s.activation(rowS[0:1, 0:512],
                                               rowS[0:1, 0:512], AF.Sqrt,
                                               bias=cst[0:1, 0:1]),
                   r=[rowSB, cstB], w=[rowSB])
                op("v", lambda v: v.reciprocal(rowS[0:1, 0:512],
                                               rowS[0:1, 0:512]),
                   r=[rowSB], w=[rowSB])
                op("v", (lambda cidx=cidx: lambda v: v.tensor_copy(
                    mr3[0:1, 2048 + cidx * 512:2048 + (cidx + 1) * 512],
                    rowS[0:1, 0:512]))(), r=[rowSB], w=[mr3B])
        for b in range(B):
            for fc in range(2):
                cidx = b * 2 + fc
                op("t", (lambda cidx=cidx: lambda t: t.matmul(
                    PS[0][:, 0:512], onesrob[0:1, :],
                    mr3[0:1, cidx * 512:(cidx + 1) * 512],
                    start=True, stop=True))(),
                   r=[mr3B, onesrobB], w=[PSB[0]])
                op("t", (lambda cidx=cidx: lambda t: t.matmul(
                    PS[1][:, 0:512], onesrob[0:1, :],
                    mr3[0:1, 2048 + cidx * 512:2048 + (cidx + 1) * 512],
                    start=True, stop=True))(),
                   r=[mr3B, onesrobB], w=[PSB[1]])
                for ot in range(32):
                    dma("g", y3t[:, :], ybuf3[b, fc, ot], w=[y3tB],
                        x=[("dq", p.dcnt["q"])])
                    op("v", lambda v: v.tensor_tensor(
                        t1s[:, :], y3t[:, :], PS[0][:, 0:512],
                        op=ALU.subtract), r=[y3tB, PSB[0]], w=[t1sB])
                    op("v", (lambda b=b, fc=fc, ot=ot: lambda v:
                             v.tensor_tensor(
                                 ysq3[:, :], t1s[:, :], PS[1][:, 0:512],
                                 op=ALU.mult))(),
                       r=[t1sB, PSB[1]], w=[ysq3B])
                    dma("q", xn3_out[:, x3col(ot, b, fc * 512):
                                     x3col(ot, b, fc * 512) + 512],
                        ysq3[:, :], r=[ysq3B])
        p._waits("q", [("dq", p.dcnt["q"])])
        with nc.Block() as block:
            p.replay(block, sems)
    return nc


def host_l1fix(l3outs, xn2_list, W, inputs):
    nc = build_l1fix()
    acol = np.ascontiguousarray(np.asarray(
        inputs["L1_attn_A"], np.float32).reshape(64, 128, 16).transpose(
        1, 0, 2).reshape(128, 1024))
    rep_sel = np.concatenate(
        [np.tile(np.eye(16, dtype=BF)[:, n:n + 1], (1, 128))
         for n in range(16)], axis=1)
    neutral = np.concatenate([np.ones((128, 2048), np.float32),
                              np.zeros((128, 2048), np.float32)], axis=1)
    in_maps = []
    for k in range(NCORES):
        preds = [neutral] * (7 - k) + [l3outs[j]["pso"] for j in range(k)]
        in_maps.append({
            "pred": np.stack(preds), "ypre": l3outs[k]["ypre"],
            "r_in": l3outs[k]["r_out"], "sg_in": l3outs[k]["sg_out"],
            "cn_in": l3outs[k]["cn_out"], "xn2": xn2_list[k],
            "twout4": W["twout"], "acol": acol, "identb": IDENT_BF,
            "onesrow_f": ONES_ROW_F32, "onesrow_b": ONES_ROW_BF,
            "ones_b": ONES_COL_BF, "rep_sel": np.ascontiguousarray(rep_sel),
        })
    res = run_bass_kernel_spmd(nc, in_maps, core_ids=list(range(NCORES)))
    return [res.results[k]["xn3"] for k in range(NCORES)]



def build_l1ffn_a():
    nc = bass.Bass("TRN2", target_bir_lowering=False)
    ap_in = {}
    for name, shape, dt in [
        ("xn3", [128, 32 * B * TOK], BF16),
        ("tl14", [32, 32, 128, 128], BF16),
        ("a_raw", [4096, 32], F32),
        ("bl1_col", [128, 32], F32),
        ("tlim", [32, 8], BF16),
        ("identf", [128, 128], F32),
    ]:
        ap_in[name] = nc.dram_tensor(name, shape, dt, kind="ExternalInput").ap()
    o_h = nc.dram_tensor("hout", [32, 128, B * TOK], BF16,
                         kind="ExternalOutput").ap()
    o_z = nc.dram_tensor("zout", [32, B * TOK], F32,
                         kind="ExternalOutput").ap()
    o_m = nc.dram_tensor("pmax", [1, 2], F32, kind="ExternalOutput").ap()

    with ExitStack() as stack:
        E = stack.enter_context
        sems = make_sems(stack, nc)
        p = Prog(nc)

        def T(name, shape, dt=F32):
            t = E(nc.sbuf_tensor(name, shape, dt))
            return t, Buf()

        idf, idfB = T("idf", [128, 128])
        azsb, azsbB = T("azsb", [128, 32 * 32], BF16)
        bl1, bl1B = T("bl1", [128, 32])
        tlim_sb, tlimB = T("tlim_sb", [32, 8], BF16)
        xres, xresB = T("xres", [128, 32 * B * TOK], BF16)
        zsb, zsbB = T("zsb", [32, B * TOK])
        zodd, zoddB = T("zodd", [32, B * TOK // 2], BF16)
        tlh = [T(f"tlh{i}", [128, 16 * 128], BF16) for i in range(2)]
        ht_t, ht_tB = T("ht_t", [128, 512], BF16)
        hm8, hm8B = T("hm8", [8, 4])
        dm8, dm8B = T("dm8", [8, 4])
        rowm, rowmB = T("rowm", [1, 16])
        PS = [psum_bank(stack, nc, f"bank{i}") for i in range(8)]
        PSB = [Buf() for _ in range(8)]

        def op(eng, fn, r=(), w=(), x=()):
            return emit(p, eng, fn, r, w, x)

        def dma(q, out_ap, in_ap, r=(), w=(), x=()):
            return emit_dma(p, q, out_ap, in_ap, r, w, x)

        dma("q", idf[:, :], ap_in["identf"], w=[idfB])
        dma("q", bl1[:, :], ap_in["bl1_col"], w=[bl1B])
        dma("q", tlim_sb[:, :], ap_in["tlim"], w=[tlimB])
        dma("q", xres[:, :], ap_in["xn3"], w=[xresB])
        for kt in range(32):
            dma("g", azsb[:, kt * 32:(kt + 1) * 32],
                ap_in["a_raw"][kt * 128:(kt + 1) * 128, :], w=[azsbB])
        op("v", lambda v: v.memset(hm8[:, :], 0.0), w=[hm8B])
        op("v", lambda v: v.memset(dm8[:, :], 0.0), w=[dm8B])
        for ci in range(4):
            b, fc = ci // 2, ci % 2
            for kt in range(32):
                op("t", (lambda kt=kt, ci=ci, b=b, fc=fc: lambda t: t.matmul(
                    PS[ci][0:32, 0:512], azsb[:, kt * 32:(kt + 1) * 32],
                    xres[:, x3col(kt, b, fc * 512):x3col(kt, b, fc * 512) + 512],
                    start=(kt == 0), stop=(kt == 31)))(),
                   r=[azsbB, xresB], w=[PSB[ci]])
            op("s", (lambda ci=ci, b=b, fc=fc: lambda s: s.copy(
                zsb[:, b * TOK + fc * 512:b * TOK + fc * 512 + 512],
                PS[ci][0:32, 0:512]))(), r=[PSB[ci]], w=[zsbB])
        dma("q", o_z, zsb[:, :], r=[zsbB])
        op("v", lambda v: v.tensor_copy(zodd[:, :], zsb[:, 1:B * TOK:2]),
           r=[zsbB], w=[zoddB])
        for ci in range(4):
            op("t", (lambda ci=ci: lambda t: t.matmul(
                PS[4][0:8, 0:256], tlim_sb[:, :],
                zodd[:, ci * 256:(ci + 1) * 256], start=True, stop=True))(),
               r=[zoddB, tlimB], w=[PSB[4]])
            op("v", (lambda ci=ci: lambda v: v.tensor_reduce(
                dm8[0:8, 1:2], PS[4][0:8, 0:256], mybir.AxisListType.X,
                ALU.max, apply_absolute_value=True))(),
               r=[PSB[4]], w=[dm8B])
            op("v", lambda v: v.tensor_tensor(dm8[0:8, 0:1], dm8[0:8, 1:2],
                                              dm8[0:8, 0:1], op=ALU.max),
               r=[dm8B], w=[dm8B])
        for ot in range(32):
            for h in range(2):
                tw, twB = tlh[h]
                dma("s", tw.ap().rearrange("p (k c) -> p k c", c=128),
                    ap_in["tl14"][ot, h * 16:(h + 1) * 16].rearrange(
                        "k p c -> p k c"), w=[twB])
            for ci in range(4):
                b, fc = ci // 2, ci % 2
                for kt in range(32):
                    tw, twB = tlh[kt // 16]
                    op("t", (lambda kt=kt, ci=ci, b=b, fc=fc, tw=tw:
                             lambda t: t.matmul(
                                 PS[ci][:, 0:512],
                                 tw[:, (kt % 16) * 128:(kt % 16 + 1) * 128],
                                 xres[:, x3col(kt, b, fc * 512):
                                      x3col(kt, b, fc * 512) + 512],
                                 start=(kt == 0), stop=(kt == 31)))(),
                       r=[twB, xresB], w=[PSB[ci]])
                op("s", (lambda ci=ci, ot=ot: lambda s: s.activation(
                    ht_t[:, :], PS[ci][:, 0:512], AF.Identity,
                    bias=bl1[:, ot:ot + 1]))(),
                   r=[PSB[ci], bl1B], w=[ht_tB])
                dma("q", o_h[ot, :, (2 * b + fc) * 512:(2 * b + fc + 1) * 512],
                    ht_t[:, :], r=[ht_tB])
                if ot == 0:
                    op("v", (lambda: lambda v: v.tensor_reduce(
                        hm8[0:8, 1:2], ht_t[0:8, 1:512:2],
                        mybir.AxisListType.X, ALU.max,
                        apply_absolute_value=True))(),
                       r=[ht_tB], w=[hm8B])
                    op("v", lambda v: v.tensor_tensor(
                        hm8[0:8, 0:1], hm8[0:8, 1:2], hm8[0:8, 0:1],
                        op=ALU.max), r=[hm8B], w=[hm8B])
        for (src8, srcB, col) in [(hm8, hm8B, 0), (dm8, dm8B, 1)]:
            op("t", (lambda src8=src8: lambda t: t.transpose(
                PS[5][0:4, 0:8], src8[0:8, 0:4], idf[0:8, 0:8]))(),
               r=[srcB, idfB], w=[PSB[5]])
            op("v", (lambda col=col: lambda v: v.tensor_reduce(
                rowm[0:1, col:col + 1], PS[5][0:1, 0:8],
                mybir.AxisListType.X, ALU.max))(),
               r=[PSB[5]], w=[rowmB])
        st = dma("q", o_m, rowm[0:1, 0:2], r=[rowmB])
        p._waits("q", [("dq", p.dcnt["q"])])
        with nc.Block() as block:
            p.replay(block, sems)
    return nc


def build_l1ffn_b():
    nc = bass.Bass("TRN2", target_bir_lowering=False)
    ap_in = {}
    for name, shape, dt in [
        ("xn3", [128, 32 * B * TOK], BF16),
        ("h_in", [32, 128, B * TOK], BF16),
        ("z_in", [32, B * TOK], F32),
        ("pmaxall", [NCORES, 1, 2], F32),
        ("tlim", [32, 8], BF16),
        ("limes_raw", [8, 32], F32),
        ("bmat_raw", [32, 4096], F32),
        ("identf", [128, 128], F32),
        ("onesrow_f", [65, 128], F32),
        ("onesrow_b", [65, 128], BF16),
        ("ones_b", [128, 1], BF16),
    ]:
        ap_in[name] = nc.dram_tensor(name, shape, dt, kind="ExternalInput").ap()
    o_f = nc.dram_tensor("outf", [B, TOK, DM], F32, kind="ExternalOutput").ap()
    ybuf4 = nc.dram_tensor("ybuf4", [B, 2, 32, 128, 512], BF16).ap()

    with ExitStack() as stack:
        E = stack.enter_context
        sems = make_sems(stack, nc)
        p = Prog(nc)

        def T(name, shape, dt=F32):
            t = E(nc.sbuf_tensor(name, shape, dt))
            return t, Buf()

        idf, idfB = T("idf", [128, 128])
        onesrow, onesrowB = T("onesrow", [65, 128])
        onesrob, onesrobB = T("onesrob", [65, 128], BF16)
        onesb, onesbB = T("onesb", [128, 1], BF16)
        tlim_sb, tlimB = T("tlim_sb", [32, 8], BF16)
        limn, limnB = T("limn", [8, 32], BF16)
        bmat_sb, bmatB = T("bmat_sb", [32, 4096], BF16)
        xres, xresB = T("xres", [128, 32 * B * TOK], BF16)
        zsb, zsbB = T("zsb", [32, B * TOK])
        zodd, zoddB = T("zodd", [32, B * TOK // 2], BF16)
        mrow, mrowB = T("mrow", [1, 16])
        a8, a8B = T("a8", [8, 4])
        hs8, hs8B = T("hs8", [8, 512], BF16)
        lg, lgB = T("lg", [8, 256])
        gtm, gtmB = T("gtm", [128, 8])
        rsc, rscB = T("rsc", [128, 48])
        rcol, rcolB = T("rcol", [128, 8])
        wfull, wfullB = T("wfull", [8, 512], BF16)
        zp, zpB = T("zp", [32, 512], BF16)
        ht_t, ht_tB = T("ht_t", [128, 512], BF16)
        xt_t, xt_tB = T("xt_t", [128, 512], BF16)
        y4t, y4tB = T("y4t", [128, 512], BF16)
        ysq4, ysq4B = T("ysq4", [128, 512], BF16)
        rowS, rowSB = T("rowS", [128, 512])
        t1s, t1sB = T("t1s", [128, 512])
        xo_t, xo_tB = T("xo_t", [128, 512])
        otm, otmB = T("otm", [128, 128])
        mr4, mr4B = T("mr4", [128, 4096], BF16)
        cst, cstB = T("cst", [128, 4])
        PS = [psum_bank(stack, nc, f"bank{i}") for i in range(8)]
        PSB = [Buf() for _ in range(8)]

        def op(eng, fn, r=(), w=(), x=()):
            return emit(p, eng, fn, r, w, x)

        def dma(q, out_ap, in_ap, r=(), w=(), x=()):
            return emit_dma(p, q, out_ap, in_ap, r, w, x)

        op("v", lambda v: v.memset(cst[:, 0:1], EPS_LN), w=[cstB])
        op("v", lambda v: v.memset(cst[:, 1:2], 2.0), r=[cstB])
        cstB.w = p.barrier_tok("v")
        for apn, sb, bf in [("identf", idf, idfB), ("onesrow_f", onesrow, onesrowB),
                            ("onesrow_b", onesrob, onesrobB),
                            ("ones_b", onesb, onesbB), ("tlim", tlim_sb, tlimB)]:
            dma("q", sb[:, :], ap_in[apn], w=[bf])
        dma("g", limn[:, :], ap_in["limes_raw"], w=[limnB])
        dma("g", bmat_sb[:, :], ap_in["bmat_raw"], w=[bmatB])
        dma("q", xres[:, :], ap_in["xn3"], w=[xresB])
        dma("q", zsb[:, :], ap_in["z_in"], w=[zsbB])
        dma("q", mrow[0:1, :], ap_in["pmaxall"].rearrange("c one q -> one (c q)"),
            w=[mrowB])
        op("v", lambda v: v.tensor_reduce(mrow[0:1, 2:3], mrow[0:1, 0:16:2],
                                          mybir.AxisListType.X, ALU.max),
           r=[mrowB], w=[mrowB])
        op("v", lambda v: v.tensor_reduce(mrow[0:1, 3:4], mrow[0:1, 1:16:2],
                                          mybir.AxisListType.X, ALU.max),
           r=[mrowB], w=[mrowB])
        op("v", lambda v: v.tensor_scalar(mrow[0:1, 2:4], mrow[0:1, 2:4],
                                          1e-6, None, op0=ALU.max),
           r=[mrowB], w=[mrowB])
        op("v", lambda v: v.reciprocal(mrow[0:1, 4:6], mrow[0:1, 2:4]),
           r=[mrowB], w=[mrowB])
        op("v", lambda v: v.tensor_scalar(mrow[0:1, 4:6], mrow[0:1, 4:6],
                                          0.5, None, op0=ALU.mult),
           r=[mrowB], w=[mrowB])
        for j in range(2):
            op("t", (lambda j=j: lambda t: t.matmul(
                PS[5][0:8, 0:1], onesrow[0:1, 0:8], mrow[0:1, 4 + j:5 + j],
                start=True, stop=True))(), r=[mrowB, onesrowB], w=[PSB[5]])
            op("s", (lambda j=j: lambda s: s.copy(a8[0:8, j:j + 1],
                                                  PS[5][0:8, 0:1]))(),
               r=[PSB[5]], w=[a8B])
        op("v", lambda v: v.tensor_copy(zodd[:, :], zsb[:, 1:B * TOK:2]),
           r=[zsbB], w=[zoddB])
        for b in range(B):
            for fc in range(2):
                ci = 2 * b + fc
                dma("s", hs8[0:8, :], ap_in["h_in"][0, 0:8,
                                                    ci * 512:(ci + 1) * 512],
                    w=[hs8B])
                op("t", (lambda ci=ci: lambda t: t.matmul(
                    PS[4][0:8, 0:256], tlim_sb[:, :],
                    zodd[:, ci * 256:(ci + 1) * 256], start=True,
                    stop=True))(), r=[zoddB, tlimB], w=[PSB[4]])
                op("v", (lambda ci=ci: lambda v: v.tensor_scalar(
                    lg[0:8, 0:256], hs8[0:8, 1:512:2], a8[0:8, 0:1], None,
                    op0=ALU.mult))(), r=[hs8B, a8B], w=[lgB])
                op("v", (lambda ci=ci: lambda v: v.scalar_tensor_tensor(
                    lg[0:8, 0:256], PS[4][0:8, 0:256], a8[0:8, 1:2],
                    lg[0:8, 0:256], ALU.mult, ALU.add))(),
                   r=[PSB[4], a8B, lgB], w=[lgB])
                for blk in range(2):
                    op("t", (lambda blk=blk: lambda t: t.transpose(
                        PS[5][0:128, 0:8], lg[0:8, blk * 128:(blk + 1) * 128],
                        idf[0:8, 0:8]))(), r=[lgB, idfB], w=[PSB[5]])
                    op("s", (lambda: lambda s: s.copy(
                        gtm[:, 0:8], PS[5][0:128, 0:8]))(),
                       r=[PSB[5]], w=[gtmB])
                    op("v", lambda v: v.tensor_reduce(
                        rcol[:, 0:1], gtm[:, 0:8], mybir.AxisListType.X,
                        ALU.max, negate=True), r=[gtmB], w=[rcolB])
                    op("s", lambda s: s.activation(
                        rsc[:, 0:8], gtm[:, 0:8], AF.Exp,
                        bias=rcol[:, 0:1]), r=[gtmB, rcolB], w=[rscB])
                    op("v", lambda v: v.tensor_reduce(
                        rcol[:, 1:2], rsc[:, 0:8], mybir.AxisListType.X,
                        ALU.add), r=[rscB], w=[rcolB])
                    op("v", lambda v: v.reciprocal(rcol[:, 2:3], rcol[:, 1:2]),
                       r=[rcolB], w=[rcolB])
                    op("v", lambda v: v.tensor_scalar(
                        rsc[:, 8:16], rsc[:, 0:8], rcol[:, 2:3], None,
                        op0=ALU.mult), r=[rscB, rcolB], w=[rscB])
                    op("v", lambda v: v.tensor_reduce(
                        rcol[:, 3:4], rsc[:, 8:16], mybir.AxisListType.X,
                        ALU.max), r=[rscB], w=[rcolB])
                    op("v", lambda v: v.tensor_scalar(
                        rsc[:, 16:24], rsc[:, 8:16], rcol[:, 3:4], None,
                        op0=ALU.is_ge), r=[rscB, rcolB], w=[rscB])
                    op("v", lambda v: v.scalar_tensor_tensor(
                        rsc[:, 24:32], rsc[:, 16:24], -2.0, rsc[:, 8:16],
                        ALU.mult, ALU.add), r=[rscB], w=[rscB])
                    op("v", lambda v: v.tensor_reduce(
                        rcol[:, 4:5], rsc[:, 24:32], mybir.AxisListType.X,
                        ALU.max), r=[rscB], w=[rcolB])
                    op("v", lambda v: v.tensor_scalar(
                        rsc[:, 32:40], rsc[:, 8:16], rcol[:, 4:5], None,
                        op0=ALU.subtract), r=[rscB, rcolB], w=[rscB])
                    op("s", lambda s: s.activation(
                        rsc[:, 32:40], rsc[:, 32:40], AF.Sigmoid,
                        scale=cst[:, 1:2]), r=[rscB, cstB], w=[rscB])
                    op("v", lambda v: v.tensor_tensor(
                        rsc[:, 40:48], rsc[:, 8:16], rsc[:, 32:40],
                        op=ALU.mult), r=[rscB], w=[rscB])
                    op("v", lambda v: v.tensor_reduce(
                        rcol[:, 5:6], rsc[:, 40:48], mybir.AxisListType.X,
                        ALU.add), r=[rscB], w=[rcolB])
                    op("v", lambda v: v.tensor_scalar(
                        rcol[:, 6:7], rcol[:, 5:6], 1e-9, None, op0=ALU.add),
                       r=[rcolB], w=[rcolB])
                    op("v", lambda v: v.reciprocal(rcol[:, 7:8], rcol[:, 6:7]),
                       r=[rcolB], w=[rcolB])
                    op("v", lambda v: v.tensor_scalar(
                        rsc[:, 40:48], rsc[:, 40:48], rcol[:, 7:8], None,
                        op0=ALU.mult), r=[rscB, rcolB], w=[rscB])
                    op("t", (lambda: lambda t: t.transpose(
                        PS[5][0:8, 0:128], rsc[:, 40:48], idf[:, :]))(),
                       r=[rscB, idfB], w=[PSB[5]])
                    for par in range(2):
                        op("s", (lambda blk=blk, par=par: lambda s: s.copy(
                            wfull[0:8, blk * 256 + par:blk * 256 + 256:2],
                            PS[5][0:8, 0:128]))(), r=[PSB[5]], w=[wfullB])
                op("t", (lambda ci=ci: lambda t: t.matmul(
                    PS[4][0:32, 0:512], limn[0:8, :], wfull[0:8, :],
                    start=True, stop=True))(), r=[wfullB, limnB], w=[PSB[4]])
                op("v", (lambda ci=ci: lambda v: v.scalar_tensor_tensor(
                    zp[:, :], zsb[:, ci * 512:(ci + 1) * 512], 0.5,
                    PS[4][0:32, 0:512], ALU.mult, ALU.mult))(),
                   r=[zsbB, PSB[4]], w=[zpB])
                for ot in range(32):
                    op("t", (lambda ot=ot: lambda t: t.matmul(
                        PS[6][:, 0:512], bmat_sb[:, ot * 128:(ot + 1) * 128],
                        zp[:, :], start=True, stop=True))(),
                       r=[zpB, bmatB], w=[PSB[6]])
                    dma("s", ht_t[:, :],
                        ap_in["h_in"][ot, :, ci * 512:(ci + 1) * 512],
                        w=[ht_tB])
                    dma("q", xt_t[:, :],
                        ap_in["xn3"][:, x3col(ot, b, fc * 512):
                                     x3col(ot, b, fc * 512) + 512],
                        w=[xt_tB])
                    op("v", lambda v: v.tensor_tensor(
                        t1s[:, :], PS[6][:, 0:512], ht_t[:, :], op=ALU.add),
                       r=[PSB[6], ht_tB], w=[t1sB])
                    op("v", lambda v: v.tensor_tensor(
                        y4t[:, :], t1s[:, :], xt_t[:, :], op=ALU.add),
                       r=[t1sB, xt_tB], w=[y4tB])
                    op("s", lambda s: s.activation(ysq4[:, :], y4t[:, :],
                                                   AF.Square),
                       r=[y4tB], w=[ysq4B])
                    op("t", (lambda ot=ot: lambda t: t.matmul(
                        PS[7][0:1, 0:512], onesb[:, 0:1], y4t[:, :],
                        start=(ot == 0), stop=False))(),
                       r=[y4tB, onesbB], w=[PSB[7]])
                    op("t", (lambda ot=ot: lambda t: t.matmul(
                        PS[7][32:33, 0:512], onesb[:, 0:1], ysq4[:, :],
                        start=(ot == 0), stop=(ot == 31)))(),
                       r=[ysq4B, onesbB], w=[PSB[7]])
                    dma("q", ybuf4[b, fc, ot], y4t[:, :], r=[y4tB])
                op("v", (lambda: lambda v: v.tensor_scalar(
                    rowS[0:1, 0:512], PS[7][0:1, 0:512], 1.0 / DM, None,
                    op0=ALU.mult))(), r=[PSB[7]], w=[rowSB])
                op("v", (lambda ci=ci: lambda v: v.tensor_copy(
                    mr4[0:1, ci * 512:(ci + 1) * 512], rowS[0:1, 0:512]))(),
                   r=[rowSB], w=[mr4B])
                op("v", (lambda: lambda v: v.tensor_scalar(
                    t1s[0:1, 0:512], PS[7][32:33, 0:512], 1.0 / DM, None,
                    op0=ALU.mult))(), r=[PSB[7]], w=[t1sB])
                op("v", lambda v: v.tensor_tensor(
                    rowS[0:1, 0:512], rowS[0:1, 0:512], rowS[0:1, 0:512],
                    op=ALU.mult), r=[rowSB], w=[rowSB])
                op("v", lambda v: v.tensor_tensor(
                    rowS[0:1, 0:512], t1s[0:1, 0:512], rowS[0:1, 0:512],
                    op=ALU.subtract), r=[rowSB, t1sB], w=[rowSB])
                op("s", lambda s: s.activation(rowS[0:1, 0:512],
                                               rowS[0:1, 0:512], AF.Sqrt,
                                               bias=cst[0:1, 0:1]),
                   r=[rowSB, cstB], w=[rowSB])
                op("v", lambda v: v.reciprocal(rowS[0:1, 0:512],
                                               rowS[0:1, 0:512]),
                   r=[rowSB], w=[rowSB])
                op("v", (lambda ci=ci: lambda v: v.tensor_copy(
                    mr4[0:1, 2048 + ci * 512:2048 + (ci + 1) * 512],
                    rowS[0:1, 0:512]))(), r=[rowSB], w=[mr4B])
        for b in range(B):
            for fc in range(2):
                ci = 2 * b + fc
                op("t", (lambda ci=ci: lambda t: t.matmul(
                    PS[0][:, 0:512], onesrob[0:1, :],
                    mr4[0:1, ci * 512:(ci + 1) * 512],
                    start=True, stop=True))(), r=[mr4B, onesrobB], w=[PSB[0]])
                op("t", (lambda ci=ci: lambda t: t.matmul(
                    PS[1][:, 0:512], onesrob[0:1, :],
                    mr4[0:1, 2048 + ci * 512:2048 + (ci + 1) * 512],
                    start=True, stop=True))(), r=[mr4B, onesrobB], w=[PSB[1]])
                for ot in range(32):
                    dma("g", y4t[:, :], ybuf4[b, fc, ot], w=[y4tB],
                        x=[("dq", p.dcnt["q"])])
                    op("v", lambda v: v.tensor_tensor(
                        t1s[:, :], y4t[:, :], PS[0][:, 0:512],
                        op=ALU.subtract), r=[y4tB, PSB[0]], w=[t1sB])
                    op("v", lambda v: v.tensor_tensor(
                        xo_t[:, :], t1s[:, :], PS[1][:, 0:512],
                        op=ALU.mult), r=[t1sB, PSB[1]], w=[xo_tB])
                    for blk in range(4):
                        op("t", (lambda blk=blk: lambda t: t.transpose(
                            PS[2 + (blk % 2)][:, 0:128],
                            xo_t[:, blk * 128:(blk + 1) * 128],
                            idf[:, :]))(), r=[xo_tB, idfB],
                           w=[PSB[2 + (blk % 2)]])
                        op("s", (lambda blk=blk: lambda s: s.copy(
                            otm[:, :], PS[2 + (blk % 2)][:, 0:128]))(),
                           r=[PSB[2 + (blk % 2)]], w=[otmB])
                        dma("q", o_f[b, fc * 512 + blk * 128:
                                     fc * 512 + (blk + 1) * 128,
                                     ot * 128:(ot + 1) * 128],
                            otm[:, :], r=[otmB])
        p._waits("q", [("dq", p.dcnt["q"])])
        with nc.Block() as block:
            p.replay(block, sems)
    return nc


def host_l1ffn(xn3_list, W, inputs):
    nc_a = build_l1ffn_a()
    bl1_col = np.ascontiguousarray(
        np.asarray(inputs["L1_ffn_b"], np.float32).reshape(32, 128).T)
    in_maps = [{"xn3": xn3_list[k], "tl14": W["tl1"],
                "a_raw": np.ascontiguousarray(np.asarray(inputs["L1_ffn_A"], np.float32)),
                "bl1_col": bl1_col, "tlim": W["tlim"], "identf": IDENT_F32}
               for k in range(NCORES)]
    res = run_bass_kernel_spmd(nc_a, in_maps, core_ids=list(range(NCORES)))
    aouts = [{kk: res.results[k][kk] for kk in ["hout", "zout", "pmax"]}
             for k in range(NCORES)]
    pmaxall = np.stack([a["pmax"] for a in aouts])
    nc_b = build_l1ffn_b()
    in_maps = [{"xn3": xn3_list[k], "h_in": aouts[k]["hout"],
                "z_in": aouts[k]["zout"], "pmaxall": pmaxall,
                "tlim": W["tlim"],
                "limes_raw": np.ascontiguousarray(np.asarray(inputs["L1_ffn_LiMEs"], np.float32)),
                "bmat_raw": np.ascontiguousarray(np.asarray(inputs["L1_ffn_Bmat"], np.float32)),
                "identf": IDENT_F32, "onesrow_f": ONES_ROW_F32,
                "onesrow_b": ONES_ROW_BF, "ones_b": ONES_COL_BF}
               for k in range(NCORES)]
    res = run_bass_kernel_spmd(nc_b, in_maps, core_ids=list(range(NCORES)))
    return [res.results[k]["outf"] for k in range(NCORES)]



def x_slices(x):
    out = []
    for k in range(NCORES):
        g0 = TOK * k - HALO
        if g0 < 0:
            sl = np.concatenate(
                [np.zeros((B, -g0, DM), np.float32), x[:, 0:TOK * (k + 1)]],
                axis=1)
        else:
            sl = x[:, g0:TOK * (k + 1)]
        out.append(np.ascontiguousarray(sl))
    return out


ONES_ROW_F32 = np.ones((65, 128), dtype=np.float32)
ONES_ROW_BF = np.ones((65, 128), dtype=BF)
ONES_COL_F32 = np.ones((128, 1), dtype=np.float32)
ONES_COL_BF = np.ones((128, 1), dtype=BF)
REP8 = np.repeat(np.eye(8, dtype=BF), 16, axis=1).reshape(8, 128)


def host_scores(xsl):
    nc = build_scores()
    in_maps = [{"xs": xsl[k], "identf": IDENT_F32} for k in range(NCORES)]
    res = run_bass_kernel_spmd(nc, in_maps, core_ids=list(range(NCORES)))
    return np.stack([res.results[k]["psc"] for k in range(NCORES)])


def host_l0block(xsl, pscall, W, inputs):
    nc = build_l0block()
    bbase_col = np.ascontiguousarray(
        np.asarray(inputs["L0_ffn_b_base"], np.float32).reshape(32, 128).T)
    in_maps = []
    for k in range(NCORES):
        in_maps.append({
            "xs": xsl[k], "pscall": pscall, "tbase4": W["tbase"],
            "ta04": W["ta0"], "tb04": W["tb0"], "twg": W["twg"],
            "bbase_col": bbase_col, "identf": IDENT_F32,
            "onesrow_f": ONES_ROW_F32, "onesrow_b": ONES_ROW_BF,
            "ones_b": ONES_COL_BF, "rep8": REP8,
        })
    res = run_bass_kernel_spmd(nc, in_maps, core_ids=list(range(NCORES)))
    return [res.results[k]["xn2"] for k in range(NCORES)]


def kernel(**inputs):
    x = np.ascontiguousarray(np.asarray(inputs["L0_attn_x"], np.float32))
    W = host_prep(inputs)
    xsl = x_slices(x)
    psc = host_scores(xsl)
    xn2_list = host_l0block(xsl, psc, W, inputs)
    l3 = host_l1attn(xn2_list, W, inputs)
    xn3 = host_l1fix(l3, xn2_list, W, inputs)
    outs = host_l1ffn(xn3, W, inputs)
    out = np.empty((B, SEQ, DM), np.float32)
    for k in range(NCORES):
        out[:, TOK * k:TOK * (k + 1)] = outs[k]
    return out
